# version 14
# baseline (speedup 1.0000x reference)
import contextlib
import numpy as np
import ml_dtypes
import concourse.bass as bass
import concourse.mybir as mybir
from concourse.bass_utils import run_bass_kernel_spmd

F32 = mybir.dt.float32
BF16 = mybir.dt.bfloat16
I32 = mybir.dt.int32
AF = mybir.ActivationFunctionType
ALU = mybir.AluOpType
AX = mybir.AxisListType

T = 2048
NTIL = T // 128
Q = 128
NCH = T // Q
NH, HPD, DS = 32, 64, 128
DM, DIN, CONV_DIM, DIP = 1024, 2048, 2304, 4384
NDC = DM // 128
NCC = CONV_DIM // 128
INTER1, INTER2 = 2752, 3072
EPS_LN, EPS_RMS = 1e-5, 1e-6
CLAMP = 30.0
CW = 2052
FCONST_VALS = [1e-5, 1e-6, 1.0, 30.0, 0.0, 0.5, -1.0, 2.0]

ENGINES = ("sync", "tensor", "vector", "scalar", "gpsimd")

OFF = {}
def _mk_off():
    o = 0
    def put(name, nbytes):
        nonlocal o
        OFF[name] = o
        o += (nbytes + 31) // 32 * 32
    put("c_ssmw", 4096); put("c_postw", 2048); put("c_dexp", 4096)
    put("c_dtb", 128); put("c_A", 128); put("c_triu_f", 512)
    put("c_triu_b", 256); put("c_ident", 256); put("c_identf", 512)
    put("c_smask", 32); put("c_fconst", 32); put("c_convw", 320)
    put("c_convb", 96); put("r1col", 128); put("rstdln", 128)
    put("meanb", 128); put("nwcol", 32)
    put("dt2", 16384)
    put("pads", 128); put("lam_tok", 256); put("lamT", 1024)
    put("gam", 256); put("wrow", 256); put("dtw", 256); put("pt", 256)
    put("dtk", 512); put("lamend", 256); put("small2", 512); put("devscr", 32)
    for name, sz in [("R_C", 32768), ("R_A", 16384), ("R_B", 32768),
                     ("R_D", 18944), ("R_E", 73872)]:
        put(name, sz)
    OFF["END"] = o
    assert o <= 208600, o
_mk_off()


def ts(i, n=128):
    return slice(i * n, (i + 1) * n)


class B:
    def __init__(self, dev_phase=None):
        self.dev_phase = dev_phase
        self.nc = bass.Bass("TRN2", target_bir_lowering=False)
        self.es = contextlib.ExitStack()
        self.emits = {e: [] for e in ENGINES}
        self.sems = {}
        self.cnt = {}
        self.dram = {}
        for s in ("sd", "gd", "ad", "sv", "sa", "st", "sg", "so",
                  "dx0", "dx1", "dn0", "dn1", "dr", "dw", "db0", "db1",
                  "dz0", "dz1", "dh0", "dh1", "de0", "de1", "dl0", "dl1", "dp", "sq"):
            self.sem(s)
        self.arena = self.es.enter_context(
            self.nc.sbuf_tensor("ARENA", [128, OFF["END"]], mybir.dt.uint8))
        self.abase = self.nc.lookup_mloc(self.arena).addr
        self._carved = {}
        self.P0 = self.es.enter_context(self.nc.psum_tensor("P0", [128, 2048], F32))
        self.P1 = self.es.enter_context(self.nc.psum_tensor("P1", [128, 1024], F32))
        self.PT = self.es.enter_context(self.nc.psum_tensor("PT", [128, 512], BF16))
        self.PS = self.es.enter_context(self.nc.psum_tensor("PS", [128, 512], F32))

    def carve(self, name, off_bytes, shape, dtype):
        key = name
        if key not in self._carved:
            self._carved[key] = (self.nc.alloc_sbuf_tensor_at(
                name, shape, dtype, offset=self.abase + off_bytes), off_bytes)
        h, ob = self._carved[key]
        assert ob == off_bytes
        return h

    def sem(self, name):
        self.sems[name] = self.es.enter_context(self.nc.semaphore(name))
        self.cnt[name] = 0

    def op(self, eng, fn, inc=None, wait=None):
        tok = None
        if inc is not None:
            self.cnt[inc[0]] += inc[1]
            tok = self.cnt[inc[0]]
        waits = [(s, v) for (s, v) in (wait or []) if v]
        sems = self.sems

        def run(e, fn=fn, waits=waits, inc=inc):
            for sn, v in waits:
                e.wait_ge(sems[sn], v)
            inst = fn(e)
            if inc is not None and inst is not None:
                inst.then_inc(sems[inc[0]], inc[1])
        self.emits[eng].append(run)
        return tok

    def wait(self, eng, semname, val):
        if val:
            sems = self.sems
            self.emits[eng].append(lambda e, sn=semname, v=val: e.wait_ge(sems[sn], v))

    def barrier(self):
        snap = {s: self.cnt[s] for s in self.cnt if s != "so"}
        for e in ENGINES:
            for s, v in snap.items():
                self.wait(e, s, v)

    def din(self, name, shape, dtype=F32):
        self.dram[name] = self.nc.dram_tensor(name, shape, dtype, kind="ExternalInput").ap()
        return self.dram[name]

    def dout(self, name, shape, dtype=F32):
        self.dram[name] = self.nc.dram_tensor(name, shape, dtype, kind="ExternalOutput").ap()
        return self.dram[name]

    def dscratch(self, name, shape, dtype=F32):
        kind = "ExternalOutput" if self.dev_phase is not None else "Internal"
        self.dram[name] = self.nc.dram_tensor(name, shape, dtype, kind=kind).ap()
        return self.dram[name]

    def finalize(self):
        with self.nc.Block() as block:
            for e in ENGINES:
                fns = self.emits[e]
                getattr(block, e)(lambda eng, fns=fns: [f(eng) for f in fns])
        self.es.close()
        return self.nc

    def declare_io(self):
        d = self.din
        self.x_own = d("x_own", [T, DM])
        self.x_pre = d("x_pre", [T, DM])
        self.w_in = d("w_in", [DM, DIP])
        self.w_out = d("w_out", [DIN, DM])
        self.w_g1 = d("w_g1", [DM, INTER1])
        self.w_u1 = d("w_u1", [DM, INTER1])
        self.w_d1 = d("w_d1", [INTER1, DM])
        self.w_g2 = d("w_g2", [DM, INTER2], I32)
        self.w_u2 = d("w_u2", [DM, INTER2], I32)
        self.w_d2 = d("w_d2", [INTER2, DM], I32)
        self.conv_w = d("conv_w", [128, NCC * 4])
        self.conv_b = d("conv_b", [128, NCC])
        self.ssm_w_bc = d("ssm_w_bc", [128, DIN])
        self.post_w_bc = d("post_w_bc", [128, DM])
        self.d_exp_bc = d("d_exp_bc", [128, DIN])
        self.dtb_bc = d("dtb_bc", [32, 1])
        self.alog_bc = d("alog_bc", [32, 1])
        self.triu_f = d("triu_f", [128, 128])
        self.triu_b = d("triu_b", [128, 128], BF16)
        self.ident_b = d("ident_b", [128, 128], BF16)
        self.ident_f = d("ident_f", [128, 128])
        self.fconst = d("fconst", [128, 8])
        self.smask = d("smask", [128, 1])
        self.nwcol_h = d("nwcol", [128, NDC])
        self.out = self.dout("out", [T, DM])
        ds = self.dscratch
        self.xn_dram = ds("xn_dram", [T, DM])
        self.sz_dram = ds("sz_dram", [T, DIN], BF16)
        self.xf_dram = ds("xf_dram", [128, NDC * T], BF16)
        self.xfp_dram = ds("xfp_dram", [128, NDC * T], BF16)
        self.h1_dram = ds("h1_dram", [T, DM])
        self.h2t_dram = ds("h2t_dram", [128, 22 * T], BF16)
        self.h3t_dram = ds("h3t_dram", [128, 24 * T], BF16)
        self.hstate_dram = ds("hstate_dram", [128, DIN])
        self.row_dram = ds("row_dram", [1, 2 * T])
        self.lam_dram = ds("lam_dram", [1, NH * Q + 64])

    def load_consts(self):
        c = {}
        cv = self.carve
        c["ssmw"] = cv("c_ssmw", OFF["c_ssmw"], [128, DIN], BF16)
        c["postw"] = cv("c_postw", OFF["c_postw"], [128, DM], BF16)
        c["dexp"] = cv("c_dexp", OFF["c_dexp"], [128, DIN], BF16)
        c["dtb"] = cv("c_dtb", OFF["c_dtb"], [32, 1], F32)
        c["A"] = cv("c_A", OFF["c_A"], [32, 1], F32)
        c["triu_f"] = cv("c_triu_f", OFF["c_triu_f"], [128, 128], F32)
        c["triu_b"] = cv("c_triu_b", OFF["c_triu_b"], [128, 128], BF16)
        c["ident"] = cv("c_ident", OFF["c_ident"], [128, 128], BF16)
        c["identf"] = cv("c_identf", OFF["c_identf"], [128, 128], F32)
        c["smask"] = cv("c_smask", OFF["c_smask"], [128, 1], F32)
        c["fconst"] = cv("c_fconst", OFF["c_fconst"], [128, 8], F32)
        c["convw"] = cv("c_convw", OFF["c_convw"], [128, NCC * 4], F32)
        c["convb"] = cv("c_convb", OFF["c_convb"], [128, NCC], F32)
        self.c = c
        for j, v in enumerate(FCONST_VALS):
            self.nc.const_aps.register(F32, v, c["fconst"][:, j:j + 1])
        self.r1col = cv("r1col", OFF["r1col"], [128, 2 * NTIL], F32)
        self.rstdln = cv("rstdln", OFF["rstdln"], [128, 2 * NTIL], F32)
        self.meanb = cv("meanb", OFF["meanb"], [128, 2 * NTIL], F32)
        self.nwcol = cv("nwcol", OFF["nwcol"], [128, NDC], F32)
        self.dt2 = cv("dt2", OFF["dt2"], [32, 2 * T], F32)
        self.dtf = self.dt2[:, 0:T]
        self.dtAf = self.dt2[:, T:2 * T]
        self.dtscr_h = None
        self.pads_sb = cv("pads", OFF["pads"], [128, NCC * 3], BF16)
        self.lam_tok = [cv(f"lam_tok{p}", OFF["lam_tok"] + p * 128, [128, NH], F32)
                        for p in range(2)]
        self.lamT = [cv(f"lamT{p}", OFF["lamT"] + p * 512, [32, Q], F32)
                     for p in range(2)]
        self.gam = [cv(f"gam{p}", OFF["gam"] + p * 128, [128, NH], F32)
                    for p in range(2)]
        self.wrow = [cv(f"wrow{p}", OFF["wrow"] + p * 128, [128, NH], F32)
                     for p in range(2)]
        self.dtw = [cv(f"dtw{p}", OFF["dtw"] + p * 128, [128, NH], F32)
                    for p in range(2)]
        self.ptb = [cv(f"pt{p}", OFF["pt"] + p * 128, [128, NH], F32)
                    for p in range(2)]
        self.dtk = [cv(f"dtk{p}", OFF["dtk"] + p * 256, [128, 2 * NH], F32)
                    for p in range(2)]
        self.lamend = [cv(f"lamend{p}", OFF["lamend"] + p * 128, [128, NH], F32)
                       for p in range(2)]
        self.devscr = cv("devscr", OFF["devscr"], [128, 1], F32)
        pairs = [("ssmw", self.ssm_w_bc), ("postw", self.post_w_bc),
                 ("dexp", self.d_exp_bc), ("dtb", self.dtb_bc),
                 ("triu_f", self.triu_f), ("triu_b", self.triu_b),
                 ("ident", self.ident_b), ("identf", self.ident_f),
                 ("smask", self.smask), ("fconst", self.fconst),
                 ("A", self.alog_bc)]
        for key, src in pairs:
            kind = "gpsimd" if (c[key].dtype == BF16 and src.dtype == F32) else "sync"
            q = "gd" if kind == "gpsimd" else "sd"
            self.op(kind, lambda e, dd=c[key], s=src: e.dma_start(dd[:, :], s),
                    inc=(q, 16))
        self.op("sync", lambda e: e.dma_start(c["convw"][:, :], self.conv_w), inc=("sd", 16))
        self.op("sync", lambda e: e.dma_start(c["convb"][:, :], self.conv_b), inc=("sd", 16))
        self.op("sync", lambda e: e.dma_start(self.nwcol[:, :], self.nwcol_h), inc=("sd", 16))
        t_sd, t_gd = self.cnt["sd"], self.cnt["gd"]
        ta = self.op("scalar", lambda e: e.activation(c["A"][:, :], c["A"][:, :], AF.Exp),
                     wait=[("sd", t_sd), ("gd", t_gd)], inc=("sa", 1))
        self.op("vector", lambda e: e.tensor_scalar_mul(c["A"][:, :], c["A"][:, :], -1.0),
                wait=[("sa", ta)], inc=("sv", 1))
        self.barrier()

    def phase_A(self):
        cv = self.carve
        self.xf = cv("xf", OFF["R_B"], [128, NDC * T], BF16)
        NT2 = 2 * NTIL
        off = OFF["R_E"]
        xt = [cv(f"A_x{p}", off + p * 4096, [128, DM], F32) for p in range(2)]
        xn = [cv(f"A_n{p}", off + 8192 + p * 4096, [128, DM], F32) for p in range(2)]
        sq = [cv(f"A_q{p}", off + 16384 + p * 4096, [128, DM], F32) for p in range(2)]
        ssqb = cv("A_ssq", off + 24576, [128, NT2], F32)
        sumb = cv("A_sum", off + 24704, [128, NT2], F32)
        varb = cv("A_var", off + 24832, [128, NT2], F32)
        scrb = cv("A_scr", off + 24960, [128, NT2], F32)
        rrow = cv("A_rrow", off + 25088, [32, 128], F32)
        stg = [cv(f"A_st{p}", off + 25600 + p * 2048, [128, DM], BF16) for p in range(2)]

        t_sq = {}
        t_sum = {}
        for k in range(NT2):
            half, i = divmod(k, NTIL)
            src_h = self.x_own if half == 0 else self.x_pre
            p = k % 2
            dxp = "dx%d" % p
            ld = self.op("sync", lambda e, x=xt[p], s=src_h, i=i:
                         e.dma_start(x[:, :], s[ts(i), :]), inc=(dxp, 16),
                         wait=[("sa", t_sq.get(k - 2)), ("sv", t_sum.get(k - 2))])
            t_sq[k] = self.op("scalar", lambda e, x=xt[p], q=sq[p], k=k: e.activation(
                q[:, :], x[:, :], AF.Square, accum_out=ssqb[:, k:k + 1]),
                wait=[(dxp, ld)], inc=("sa", 1))
            t_sum[k] = self.op("vector", lambda e, x=xt[p], k=k: e.reduce_sum(
                sumb[:, k:k + 1], x[:, :], axis=AX.X), wait=[(dxp, ld)], inc=("sv", 1))
        h0 = self.op("vector", lambda e: e.tensor_scalar_mul(
            self.meanb[:, :], sumb[:, :], 1.0 / DM), inc=("sv", 1),
            wait=[("sv", t_sum[NT2 - 1]), ("sa", t_sq[NT2 - 1])])
        h1 = self.op("vector", lambda e: e.tensor_scalar_mul(
            varb[:, :], ssqb[:, :], 1.0 / DM), inc=("sv", 1))
        h2 = self.op("vector", lambda e: e.tensor_mul(
            scrb[:, :], self.meanb[:, :], self.meanb[:, :]),
            wait=[("sv", h0)], inc=("sv", 1))
        h3 = self.op("vector", lambda e: e.tensor_sub(
            varb[:, :], varb[:, :], scrb[:, :]), wait=[("sv", h2)], inc=("sv", 1))
        a0 = self.op("scalar", lambda e: e.activation(
            scrb[:, :], varb[:, :], AF.Ln, bias=EPS_LN), wait=[("sv", h3)], inc=("sa", 1))
        a1 = self.op("scalar", lambda e: e.activation(
            self.rstdln[:, :], scrb[:, :], AF.Exp, scale=-0.5),
            wait=[("sa", a0)], inc=("sa", 1))
        h4 = self.op("vector", lambda e: e.tensor_mul(
            scrb[:, :], self.rstdln[:, :], self.rstdln[:, :]),
            wait=[("sa", a1)], inc=("sv", 1))
        h5 = self.op("vector", lambda e: e.tensor_mul(
            scrb[:, :], scrb[:, :], varb[:, :]), wait=[("sv", h4)], inc=("sv", 1))
        a2 = self.op("scalar", lambda e: e.activation(
            scrb[:, :], scrb[:, :], AF.Ln, bias=EPS_RMS), wait=[("sv", h5)], inc=("sa", 1))
        a3 = self.op("scalar", lambda e: e.activation(
            self.r1col[:, :], scrb[:, :], AF.Exp, scale=-0.5),
            wait=[("sa", a2)], inc=("sa", 1))
        self.wait("vector", "sa", a3)

        t_v3 = {}
        t_tr = {}
        t_sp = {}
        ev_tok = {}
        for k in range(NT2):
            half, i = divmod(k, NTIL)
            src_h = self.x_own if half == 0 else self.x_pre
            p = k % 2
            dxp, dnp = "dx%d" % p, "dn%d" % p
            ld = self.op("sync", lambda e, x=xt[p], s=src_h, i=i:
                         e.dma_start(x[:, :], s[ts(i), :]), inc=(dxp, 16),
                         wait=[("sv", t_v3.get(k - 2))] +
                              ([("sa", a3), ("sv", h5)] if k < 2 else []))
            t_v3[k] = self.op("vector", lambda e, x=xt[p], n=xn[p], k=k:
                              e.tensor_scalar(n[:, :], x[:, :], self.meanb[:, k:k + 1],
                                              self.rstdln[:, k:k + 1],
                                              ALU.subtract, ALU.mult),
                              wait=[(dxp, ld), ("st", t_tr.get(k - 2)),
                                    (dnp, t_sp.get(k - 2))],
                              inc=("sv", 1))
            if half == 0:
                t_sp[k] = self.op("gpsimd", lambda e, n=xn[p], i=i: e.dma_start(
                    self.xn_dram[ts(i), :], n[:, :]),
                    wait=[("sv", t_v3[k])], inc=(dnp, 16))
            for dck in range(NDC):
                pp = self.P1[:, 0:128] if dck % 2 == 0 else self.P1[:, 512:640]
                tt = self.op("tensor", lambda e, n=xn[p], d=dck, pp=pp: e.transpose(
                    pp, n[:, ts(d)], self.c["identf"][:, :]),
                    wait=[("sv", t_v3[k]), ("sv", ev_tok.get(dck % 2))],
                    inc=("st", 1))
                if half == 0:
                    dst = self.xf[:, dck * T + i * 128: dck * T + (i + 1) * 128]
                else:
                    dst = stg[p][:, ts(dck)]
                ev_tok[dck % 2] = self.op(
                    "vector", lambda e, pp=pp, dst=dst: e.tensor_copy(dst, pp),
                    wait=[("st", tt)] + ([("dn%d" % p, t_sp.get(k - 2))]
                                         if half == 1 else []),
                    inc=("sv", 1))
            t_tr[k] = self.cnt["st"]
            if half == 1:
                ev_all = self.cnt["sv"]
                t_sp[k] = self.op("gpsimd", lambda e, s=stg[p], i=i: e.dma_start(
                    bass.AP(self.xfp_dram.tensor, i * 128,
                            [[NDC * T, 128], [T, NDC], [1, 128]]),
                    s[:, :]), wait=[("sv", ev_all)], inc=(dnp, 16))

        tt = self.op("tensor", lambda e: e.transpose(
            self.PS[0:32, 128:256], self.r1col[:, :], self.c["identf"][:, :]),
            wait=[("sa", a3)] + [("sv", ev_tok.get(q)) for q in range(2)],
            inc=("st", 1))
        ev = self.op("vector", lambda e: e.tensor_copy(rrow[:, :], self.PS[0:32, 128:256]),
                     wait=[("st", tt)], inc=("sv", 1))
        self.t_rowdram = self.op("sync", lambda e: e.dma_start(
            bass.AP(self.row_dram.tensor, 0, [[128, 32], [1, 128]]), rrow[:, :]),
            wait=[("sv", ev)], inc=("dr", 16))
        self.op("gpsimd", lambda e: e.dma_start(self.xf_dram, self.xf[:, :]),
                wait=[("sv", self.cnt["sv"])], inc=("gd", 16))
        self.barrier()

    def phase_B(self, stream, b_stop=None):
        pre = stream == 1
        cv = self.carve
        self.xbc = cv("xbc", OFF["R_E"], [128, NCC * CW], BF16)
        xbc = self.xbc
        self.r1bc = cv("r1bc", OFF["R_A"], [128, T], F32)
        self.dtscr = cv("dtscr", OFF["R_A"] + 8192, [32, T], F32)
        t_r1 = self.op("sync", lambda e, o=(T if pre else 0): e.dma_start(
            self.r1bc[:, :], bass.AP(self.row_dram.tensor, o, [[0, 128], [1, T]])),
            wait=[("dr", self.t_rowdram), ("sv", self.cnt["sv"]),
                  ("sa", self.cnt["sa"])], inc=("dr", 16))
        self.wbuf = cv("wbuf", OFF["R_D"], [128, NDC * 1184], BF16)
        xfh = cv("xfh", OFF["R_C"], [128, NDC * 1024], BF16)

        slots = [self.P0[:, 0:1024], self.P0[:, 1024:2048], self.P1[:, 0:1024]]
        t_ev = {}
        t_mm = {}
        ev_ec = {}
        uid = 0
        t_wv = getattr(self, "t_wdone", None)
        acc = [cv(f"B_acc{q}", OFF["R_C"] + 20480 + q * 4096, [128, T], BF16)
               for q in range(3)]
        t_conv = {}
        t_silu = {}
        self.op("vector", lambda e: e.memset(
            bass.AP(xbc, 2051, [[NCC * CW, 128], [CW, NCC], [1, 1]]), 0.0),
            inc=("sv", 1))

        def emit_pads(lo, hi):
            n = hi - lo
            if pre:
                pz = self.op("vector", lambda e, lo=lo, n=n: e.memset(
                    bass.AP(xbc, lo * CW, [[NCC * CW, 128], [CW, n], [1, 3]]), 0.0),
                    wait=[("sv", ev_ec[hi - 1])], inc=("sv", 1))
                tp = self.op("vector", lambda e, lo=lo, n=n: e.tensor_copy(
                    bass.AP(self.pads_sb, lo * 3, [[NCC * 3, 128], [3, n], [1, 3]]),
                    bass.AP(xbc, lo * CW + 2048, [[NCC * CW, 128], [CW, n], [1, 3]])),
                    wait=[("sv", ev_ec[hi - 1])], inc=("sv", 1))
                self.t_pads = tp
                return pz, tp
            pz = self.op("vector", lambda e, lo=lo, n=n: e.tensor_copy(
                bass.AP(xbc, lo * CW, [[NCC * CW, 128], [CW, n], [1, 3]]),
                bass.AP(self.pads_sb, lo * 3, [[NCC * 3, 128], [3, n], [1, 3]])),
                wait=[("sv", self.t_pads)], inc=("sv", 1))
            return pz, None

        def emit_conv(lo, hi):
            pz, tp = emit_pads(lo, hi)
            for g in range(lo, hi, 3):
                grp = list(range(g, min(g + 3, hi)))
                for step in range(4):
                    for j, ec in enumerate(grp):
                        a = acc[j]
                        if step == 0:
                            t_conv[ec] = self.op("vector", lambda e, ec=ec, a=a:
                                e.tensor_scalar(
                                    a[:, :], xbc[:, ec * CW:ec * CW + T],
                                    self.c["convw"][:, ec * 4:ec * 4 + 1], None,
                                    ALU.mult),
                                wait=[("sv", pz), ("sv", ev_ec[ec]),
                                      ("sa", t_silu.get(ec - 3))],
                                inc=("sv", 1))
                        else:
                            t_conv[ec] = self.op("vector", lambda e, ec=ec, a=a, k=step:
                                e.scalar_tensor_tensor(
                                    a[:, :], xbc[:, ec * CW + k:ec * CW + k + T],
                                    self.c["convw"][:, ec * 4 + k:ec * 4 + k + 1],
                                    a[:, :], ALU.mult, ALU.add),
                                wait=[("sv", t_conv[ec])], inc=("sv", 1))
                for j, ec in enumerate(grp):
                    t_silu[ec] = self.op("scalar", lambda e, ec=ec, a=acc[j]:
                        e.activation(
                            xbc[:, ec * CW + 3:ec * CW + 3 + T], a[:, :], AF.Silu,
                            bias=self.c["convb"][:, ec:ec + 1]),
                        wait=[("sv", t_conv[ec])] +
                             ([("sv", tp)] if pre else []),
                        inc=("sa", 1))

        for halfw, (cb, ncol, ecs, has_dt) in enumerate([
                (DIN, 1152, list(range(0, 9)), False),
                (DIN + 1152, 1184, list(range(9, 18)), True)]):
            ldw = self.op("gpsimd", lambda e, cb=cb, ncl=ncol: e.dma_start(
                bass.AP(self.wbuf, 0, [[NDC * 1184, 128], [1184, NDC], [1, ncl]]),
                bass.AP(self.w_in.tensor, cb, [[DIP, 128], [128 * DIP, NDC], [1, ncl]])),
                inc=("dw", 16), wait=[("st", self.cnt["st"])])
            wtoks = []
            for dc in range(NDC):
                wtoks.append(self.op("vector", lambda e, dc=dc, ncl=ncol:
                    e.tensor_scalar_mul(self.wbuf[:, dc * 1184:dc * 1184 + ncl],
                                        self.wbuf[:, dc * 1184:dc * 1184 + ncl],
                                        self.nwcol[:, dc:dc + 1]),
                    wait=[("dw", ldw)], inc=("sv", 1)))
            t_w = wtoks[-1]
            for th in range(2):
                if pre:
                    t_ldx = self.op("sync", lambda e, th=th: e.dma_start(
                        bass.AP(xfh, 0, [[NDC * 1024, 128], [1024, NDC], [1, 1024]]),
                        bass.AP(self.xfp_dram.tensor, th * 1024,
                                [[NDC * T, 128], [T, NDC], [1, 1024]])),
                        inc=("dx0", 16), wait=[("st", self.cnt["st"])])
                    rhs = lambda dc: xfh[:, dc * 1024:(dc + 1) * 1024]
                    wv_rhs = [("dx0", t_ldx)]
                else:
                    rhs = lambda dc: self.xf[:, dc * T + th * 1024:dc * T + th * 1024 + 1024]
                    wv_rhs = []
                for ec in ecs:
                    colrel = ec * 128 - (0 if halfw == 0 else 1152)
                    sl = slots[uid % 3]
                    for dc in range(NDC):
                        w0 = (wv_rhs + [("sv", t_w), ("sv", t_ev.get(uid - 3))]
                              if dc == 0 else [])
                        mm = self.op("tensor", lambda e, cr=colrel, dc=dc, sl=sl, r=rhs(dc):
                                     [e.matmul(sl[:, 0:512],
                                               self.wbuf[:, dc * 1184 + cr:
                                                          dc * 1184 + cr + 128],
                                               r[:, 0:512],
                                               start=(dc == 0), stop=(dc == NDC - 1)),
                                      e.matmul(sl[:, 512:1024],
                                               self.wbuf[:, dc * 1184 + cr:
                                                          dc * 1184 + cr + 128],
                                               r[:, 512:1024],
                                               start=(dc == 0), stop=(dc == NDC - 1))][-1],
                                     wait=w0,
                                     inc=("st", 1) if dc == NDC - 1 else None)
                    t_mm[uid] = mm
                    t_ev[uid] = self.op("vector", lambda e, ec=ec, th=th, sl=sl:
                        e.tensor_tensor(
                            xbc[:, ec * CW + 3 + th * 1024:ec * CW + 3 + th * 1024 + 1024],
                            sl, self.r1bc[:, th * 1024:th * 1024 + 1024], ALU.mult),
                        wait=[("st", mm), ("dr", t_r1)], inc=("sv", 1))
                    if th == 1:
                        ev_ec[ec] = t_ev[uid]
                    uid += 1
                if has_dt:
                    sl = slots[uid % 3]
                    for dc in range(NDC):
                        w0 = (wv_rhs + [("sv", t_w), ("sv", t_ev.get(uid - 3))]
                              if dc == 0 else [])
                        mm = self.op("tensor", lambda e, dc=dc, sl=sl, r=rhs(dc):
                            [e.matmul(sl[0:32, 0:512],
                                      self.wbuf[:, dc * 1184 + 1152:dc * 1184 + 1184],
                                      r[:, 0:512],
                                      start=(dc == 0), stop=(dc == NDC - 1)),
                             e.matmul(sl[0:32, 512:1024],
                                      self.wbuf[:, dc * 1184 + 1152:dc * 1184 + 1184],
                                      r[:, 512:1024],
                                      start=(dc == 0), stop=(dc == NDC - 1))][-1],
                            wait=w0, inc=("st", 1) if dc == NDC - 1 else None)
                    self.t_dtev = t_ev[uid] = self.op(
                        "vector", lambda e, th=th, sl=sl: e.tensor_copy(
                            self.dtf[:, th * 1024:th * 1024 + 1024], sl[0:32, :]),
                        wait=[("st", mm)], inc=("sv", 1))
                    uid += 1
            if halfw == 0 and b_stop != "mm":
                emit_conv(0, 9)
        self.t_wdone = self.cnt["st"]

        if not pre and b_stop != "mm":
            zst = [self.carve(f"zst{q}", OFF["R_C"] + 16384 + q * 2048, [128, 1024], BF16)
                   for q in range(2)]
            t_zev = {}
            t_zsp = {}
            zu = 0
            for zh in range(2):
                ldw = self.op("gpsimd", lambda e, zh=zh: e.dma_start(
                    bass.AP(self.wbuf, 0, [[NDC * 1184, 128], [1184, NDC], [1, 1024]]),
                    bass.AP(self.w_in.tensor, zh * 1024,
                            [[DIP, 128], [128 * DIP, NDC], [1, 1024]])),
                    inc=("dw", 16), wait=[("st", self.cnt["st"])])
                wtoks = []
                for dc in range(NDC):
                    wtoks.append(self.op("vector", lambda e, dc=dc:
                        e.tensor_scalar_mul(self.wbuf[:, dc * 1184:dc * 1184 + 1024],
                                            self.wbuf[:, dc * 1184:dc * 1184 + 1024],
                                            self.nwcol[:, dc:dc + 1]),
                        wait=[("dw", ldw)], inc=("sv", 1)))
                t_w = wtoks[-1]
                for i in range(NTIL):
                    sl = slots[zu % 3]
                    for dc in range(NDC):
                        w0 = (([("sv", t_w), ("sa", t_zev.get(zu - 3))] +
                               ([("sv", t_ev[uid - 1])] if zu < 3 else []))
                              if dc == 0 else [])
                        mm = self.op("tensor", lambda e, i=i, dc=dc, sl=sl:
                            [e.matmul(sl[:, 0:512],
                                      self.xf[:, dc * T + i * 128:dc * T + (i + 1) * 128],
                                      self.wbuf[:, dc * 1184:dc * 1184 + 512],
                                      start=(dc == 0), stop=(dc == NDC - 1)),
                             e.matmul(sl[:, 512:1024],
                                      self.xf[:, dc * T + i * 128:dc * T + (i + 1) * 128],
                                      self.wbuf[:, dc * 1184 + 512:dc * 1184 + 1024],
                                      start=(dc == 0), stop=(dc == NDC - 1))][-1],
                            wait=w0, inc=("st", 1) if dc == NDC - 1 else None)
                    t_zev[zu] = self.op("scalar", lambda e, i=i, sl=sl, zq=zu % 2:
                        e.activation(zst[zq][:, :], sl, AF.Silu,
                                     scale=self.r1col[:, i:i + 1]),
                        wait=[("st", mm), ("dn%d" % (zu % 2), t_zsp.get(zu - 2))],
                        inc=("sa", 1))
                    t_zsp[zu] = self.op("gpsimd", lambda e, i=i, zh=zh, zq=zu % 2:
                        e.dma_start(self.sz_dram[ts(i), zh * 1024:(zh + 1) * 1024],
                                    zst[zq][:, :]),
                        wait=[("sa", t_zev[zu])], inc=("dn%d" % (zu % 2), 16))
                    zu += 1
            self.t_zdone = self.cnt["sa"]

        if b_stop == "mm":
            return
        emit_conv(9, NCC)
        self.t_xbc = self.cnt["sa"]
        if b_stop == "conv":
            return

        h0 = self.op("vector", lambda e: e.tensor_mul(
            self.dtf[:, :], self.dtf[:, :], self.r1bc[0:32, :]),
            wait=[("sv", self.t_dtev), ("dr", t_r1)], inc=("sv", 1))
        h1 = self.op("vector", lambda e: e.tensor_tensor(
            self.dtf[:, :], self.dtf[:, :],
            bass.AP(self.c["dtb"], 0, [[1, 32], [0, T]]), ALU.add),
            wait=[("sv", h0)], inc=("sv", 1))
        a0 = self.op("scalar", lambda e: e.activation(self.dtscr[:, :], self.dtf[:, :],
                     AF.Abs), wait=[("sv", h1)], inc=("sa", 1))
        a1 = self.op("scalar", lambda e: e.activation(self.dtscr[:, :], self.dtscr[:, :],
                     AF.Exp, scale=-1.0), wait=[("sa", a0)], inc=("sa", 1))
        a2 = self.op("scalar", lambda e: e.activation(self.dtscr[:, :], self.dtscr[:, :],
                     AF.Ln, bias=1.0), wait=[("sa", a1)], inc=("sa", 1))
        h2 = self.op("vector", lambda e: e.tensor_scalar_max(
            self.dtf[:, :], self.dtf[:, :], 0.0), wait=[("sa", a0)], inc=("sv", 1))
        h3 = self.op("vector", lambda e: e.tensor_tensor(
            self.dtf[:, :], self.dtf[:, :], self.dtscr[:, :], ALU.add),
            wait=[("sa", a2), ("sv", h2)], inc=("sv", 1))
        h4 = self.op("vector", lambda e: e.tensor_tensor(
            self.dtAf[:, :], self.dtf[:, :],
            bass.AP(self.c["A"], 0, [[1, 32], [0, T]]), ALU.mult),
            wait=[("sv", h3)], inc=("sv", 1))
        self.t_dt = h4
        self.barrier()

    def phase_C(self, stream):
        import os as _os
        pre = stream == 1
        cv = self.carve
        xbc = self.xbc
        c = self.c
        H = cv("Hst", OFF["R_D"] + 8192, [128, DIN], F32)
        bc = [cv(f"bcast{q}", OFF["R_D"] + q * 4096, [128, 1024], F32) for q in range(2)]
        est = [cv(f"Est{q}", OFF["R_D"] + 16384 + q * 512, [128, 128], F32)
               for q in range(3)]
        emt = [cv(f"Emt{q}", OFF["R_D"] + 17920 + q * 256, [128, 128], BF16)
               for q in range(2)]
        y_acc = cv("y_acc", OFF["R_C"], [128, DIN], F32)
        ynT = cv("ynT", OFF["R_C"] + 8192, [128, DIN], BF16)
        xn_t = cv("xn_t", OFF["R_C"] + 12288, [128, DM], F32)
        h1_t = cv("h1_t", OFF["R_C"] + 16384, [128, DM], F32)
        xw = cv("xw", OFF["R_C"] + 20480, [128, DIN], BF16)
        cscr = cv("cscr", OFF["R_C"] + 24576, [128, DIN], F32)
        sz_t = cv("sz_t", OFF["R_A"], [128, DIN], BF16)
        x_tok = cv("x_tok", OFF["R_A"] + 4096, [128, DIN], BF16)
        H_bf = cv("H_bf", OFF["R_A"] + 8192, [128, DIN], BF16)
        yn = cv("yn", OFF["R_A"] + 12288, [128, DIN], BF16)
        B_tok = cv("B_tok", OFF["small2"], [128, 128], BF16)
        Gm = cv("Gm", OFF["small2"] + 256, [128, 128], BF16)
        ssq = cv("ssq", OFF["devscr"], [128, 1], F32)
        wout = cv("wout", OFF["R_B"], [128, 16 * DM], BF16)
        P0, P1, PT, PS = self.P0, self.P1, self.PT, self.PS

        if pre:
            t_h0 = self.op("vector", lambda e: e.memset(H[:, :], 0.0), inc=("sv", 1))
            t_carry = t_h0
        else:
            t_h = self.op("sync", lambda e: e.dma_start(H[:, :], self.dram["hstate_dram"]),
                          wait=[("dp", self.t_hspill)], inc=("dr", 16))
            self.wait("vector", "dr", t_h)
            t_carry = None
            self.t_wo = self.op("gpsimd", lambda e: e.dma_start(
                bass.AP(wout, 0, [[16 * DM, 128], [DM, 16], [1, DM]]),
                bass.AP(self.w_out.tensor, 0, [[DM, 128], [128 * DM, 16], [1, DM]])),
                inc=("dw", 16), wait=[("st", self.cnt["st"])])
            t_hbf = self.op("scalar", lambda e: e.activation(
                H_bf[:, :], H[:, :], AF.Copy), wait=[("dr", t_h)], inc=("sa", 1))

        slotA_f = PS[:, 0:128]
        slotA_b = PS[:, 128:192].bitcast(BF16)
        slotB_f = PT[:, 0:256].bitcast(F32)
        slotB_b = PT[:, 256:384]
        t_slot = {0: None, 1: None}

        def transpose(idx, src_ap, dst_ap, dt_, xtra_tr=(), xtra_ev=()):
            q = idx % 2
            pp = (slotA_f if dt_ == F32 else slotA_b) if q == 0 else \
                 (slotB_f if dt_ == F32 else slotB_b)
            ident = c["identf"] if dt_ == F32 else c["ident"]
            np_, nf = src_ap.shape[0], src_ap.shape[1]
            tt = self.op("tensor", lambda e, pp=pp, s=src_ap, i=ident, np_=np_, nf=nf:
                         e.transpose(pp[0:nf, 0:np_], s, i[0:np_, 0:np_]),
                         wait=[("sv", t_slot[q])] + list(xtra_tr), inc=("st", 1))
            t_slot[q] = self.op("vector",
                                lambda e, pp=pp, d=dst_ap, nf=nf, np_=np_:
                                e.tensor_copy(d, pp[0:nf, 0:np_]),
                                wait=[("st", tt)] + list(xtra_ev), inc=("sv", 1))
            return t_slot[q]

        exp64 = lambda buf, o=0: bass.AP(buf, o, [[buf.shape[1], 128], [1, NH], [0, HPD]])

        stA = {}
        t_endB = {-2: (0, 0, 0), -1: (0, 0, 0)}
        t_grp = {}
        t_gate = None
        t_zmm = None
        t_smm = None
        t_p0v = None
        t_xn_pv = None
        t_h1_sp = {}
        t_dchain = {-2: 0, -1: 0}
        t_lch = {-2: 0, -1: 0}
        t_tD = {}
        pend_tail = {}

        def emit_tail(cc):
            nonlocal t_xn_pv
            ynt_c, ldxn_c = pend_tail.pop(cc)
            omm_c = None
            for f in range(16):
                for q in range(2):
                    omm_c = self.op("tensor", lambda e, f=f, q=q: e.matmul(
                        P1[:, q * 512:(q + 1) * 512], ynT[:, f * 128:(f + 1) * 128],
                        wout[:, f * DM + q * 512:f * DM + (q + 1) * 512],
                        start=(f == 0), stop=(f == 15)),
                        wait=([("sv", ynt_c), ("dw", self.t_wo),
                               ("sv", t_xn_pv)]
                              if f == 0 and q == 0 else []),
                        inc=("st", 1) if f == 15 and q == 1 else None)
            ev_o = self.op("vector", lambda e: e.tensor_tensor(
                h1_t[:, :], P1[:, 0:1024], xn_t[:, :], ALU.add),
                wait=[("st", omm_c), ("dh%d" % (cc % 2), ldxn_c),
                      ("dn%d" % ((cc - 1) % 2), t_h1_sp.get(cc - 1))],
                inc=("sv", 1))
            t_xn_pv = ev_o
            t_h1_sp[cc] = self.op("gpsimd", lambda e, cc=cc: e.dma_start(
                self.h1_dram[ts(cc), :], h1_t[:, :]),
                wait=[("sv", ev_o)], inc=("dn%d" % (cc % 2), 16))

        def stageA(ch):
            par = ch % 2
            dbp = "db%d" % par
            svp, sap, stp = t_endB[ch - 2]
            W0 = ch * Q
            d = {}
            lt, lT = self.lam_tok[par], self.lamT[par]
            dk = self.dtk[par]
            gm_, wr_, dw_, pt_, le_ = (self.gam[par], self.wrow[par], self.dtw[par],
                                       self.ptb[par], self.lamend[par])
            tdt = transpose(0, self.dtf[:, W0:W0 + Q], dk[:, 0:NH], F32,
                            xtra_ev=[("sv", svp), ("sa", sap)])
            tda = transpose(1, self.dtAf[:, W0:W0 + Q], dk[:, NH:2 * NH], F32,
                            xtra_ev=[("sv", svp), ("sa", sap)])
            m1 = self.op("tensor", lambda e, dk=dk: e.matmul(
                slotA_f[:, 0:32], c["triu_f"][:, :], dk[:, NH:2 * NH],
                start=True, stop=True),
                wait=[("sv", tda), ("sv", svp), ("sv", t_slot[0])], inc=("st", 1))
            m2 = self.op("tensor", lambda e, dk=dk: e.matmul(
                slotB_f[0:32, 0:128], dk[:, NH:2 * NH], c["triu_f"][:, :],
                start=True, stop=True),
                wait=[("sv", t_slot[1])], inc=("st", 1))
            e1 = t_slot[0] = self.op("vector", lambda e, lt=lt: e.tensor_copy(
                lt[:, :], slotA_f[:, 0:32]),
                wait=[("st", m2), ("sa", sap)], inc=("sv", 1))
            e2 = t_slot[1] = self.op("vector", lambda e, lT=lT: e.tensor_copy(
                lT[:, 0:Q], slotB_f[0:32, 0:128]), inc=("sv", 1))
            m3 = self.op("tensor", lambda e, lT=lT: e.matmul(
                slotA_f[:, 0:32],
                bass.AP(lT, 127, [[128, 32], [0, 128]]),
                c["identf"][0:32, 0:32], start=True, stop=True),
                wait=[("sv", e2), ("sv", t_slot[0])], inc=("st", 1))
            e3 = t_slot[0] = self.op("vector", lambda e, le=le_: e.tensor_copy(
                le[:, :], slotA_f[:, 0:32]),
                wait=[("st", m3)], inc=("sv", 1))
            dtok = e3
            t_lch[ch] = e3
            vw = self.op("vector", lambda e, wr=wr_, le=le_, lt=lt: e.tensor_sub(
                wr[:, :], le[:, :], lt[:, :]),
                wait=[("sv", e3)], inc=("sv", 1))
            aw = self.op("scalar", lambda e, wr=wr_: e.activation(
                wr[:, :], wr[:, :], AF.Exp), wait=[("sv", vw)], inc=("sa", 1))
            ag = self.op("scalar", lambda e, gm=gm_, le=le_: e.activation(
                gm[:, :], le[:, :], AF.Exp),
                wait=[("sv", e3), ("sa", sap)], inc=("sa", 1))
            if not pre:
                ap_ = self.op("scalar", lambda e, pt=pt_, lt=lt: e.activation(
                    pt[:, 0:NH], lt[:, :], AF.Exp),
                    wait=[("sv", e1), ("sa", sap)], inc=("sa", 1))
                d["ap"] = ap_
            vdw = self.op("vector", lambda e, dw=dw_, dk=dk, wr=wr_: e.tensor_mul(
                dw[:, 0:NH], dk[:, 0:NH], wr[:, :]),
                wait=[("sa", aw), ("sv", tdt)], inc=("sv", 1))
            d.update(e1=e1, e2=e2, vw=vw, aw=aw, ag=ag, vdw=vdw, tdt=tdt, tda=tda,
                     dtok=dtok, par=par)
            return d

        def bcast_chain(chx):
            A = stA[chx]
            par = A["par"]
            dbp = "db%d" % par
            lT = self.lamT[par]
            rdhs = []
            dtok = t_dchain[chx - 2]
            for g in range(4):
                gq = chx * 4 + g
                wrh = self.op("sync", lambda e, lT=lT, g=g, par=par: e.dma_start(
                    bass.AP(self.lam_dram.tensor, par * 2048 + (g % 2) * 1024,
                            [[128, 8], [1, 128]]),
                    lT[g * 8:(g + 1) * 8, 0:Q]),
                    wait=[("sv", A["e2"]), (dbp, dtok)], inc=(dbp, 16))
                rdh = self.op("sync", lambda e, g=g, par=par: e.dma_start(
                    bc[g % 2][:, :],
                    bass.AP(self.lam_dram.tensor, par * 2048 + (g % 2) * 1024,
                            [[0, 128], [1, 1024]])),
                    wait=[(dbp, wrh), ("sq", max(0, gq - 1))],
                    inc=(dbp, 16))
                dtok = rdh
                rdhs.append(rdh)
            A["rdhs"] = rdhs
            t_dchain[chx] = dtok

        stA[0] = stageA(0)
        if not pre:
            bcast_chain(0)
        for ch in range(NCH):
            if ch + 1 < NCH:
                stA[ch + 1] = stageA(ch + 1)
                if not pre:
                    bcast_chain(ch + 1)
            A = stA[ch]
            par = ch % 2
            dbp = "db%d" % par
            dzp = "dz%d" % par
            dhp = "dh%d" % par
            dnp = "dn%d" % par
            W0 = ch * Q
            lt, dk = self.lam_tok[par], self.dtk[par]
            gm_, dw_, pt_ = self.gam[par], self.dtw[par], self.ptb[par]
            t_xtok = None
            for f in range(16):
                t_xtok = transpose(f, xbc[:, f * CW + 3 + W0:f * CW + 3 + W0 + Q],
                                   x_tok[:, f * 128:(f + 1) * 128], BF16,
                                   xtra_tr=[("sa", self.t_xbc)] if f == 0 else (),
                                   xtra_ev=[("st", t_smm),
                                            ("sg", t_tD.get(ch - 1))]
                                   if f == 0 else ())
            t_btok = transpose(1, xbc[:, 16 * CW + 3 + W0:16 * CW + 3 + W0 + Q],
                               B_tok[:, :], BF16)
            vxw = self.op("vector", lambda e, dw=dw_: e.tensor_tensor(
                xw[:, :], x_tok[:, :], exp64(dw), ALU.mult),
                wait=[("sv", A["vdw"]), ("sv", t_btok)], inc=("sv", 1))
            if not pre:
                mg = self.op("tensor", lambda e, W0=W0: e.matmul(
                    slotB_f[:, 0:128], xbc[:, 16 * CW + 3 + W0:16 * CW + 3 + W0 + Q],
                    xbc[:, 17 * CW + 3 + W0:17 * CW + 3 + W0 + Q],
                    start=True, stop=True),
                    wait=[("sv", t_slot[1]), ("sa", self.t_xbc)],
                    inc=("st", 1))
                t_gm = t_slot[1] = self.op("vector", lambda e: e.tensor_tensor(
                    Gm[:, :], slotB_f[:, 0:128], c["triu_b"][:, :], ALU.mult),
                    wait=[("st", mg)], inc=("sv", 1))
                zmm = None
                for q in range(4):
                    zmm = self.op("tensor", lambda e, q=q, W0=W0: e.matmul(
                        P0[:, q * 512:(q + 1) * 512],
                        xbc[:, 17 * CW + 3 + W0:17 * CW + 3 + W0 + Q],
                        H_bf[:, q * 512:(q + 1) * 512], start=True, stop=True),
                        wait=([("sa", t_hbf), ("sv", t_p0v)] if q == 0 else []),
                        inc=("st", 1) if q == 3 else None)
                t_zmm = zmm
                ez = self.op("vector", lambda e, pt=pt_: e.tensor_tensor(
                    y_acc[:, :], P0[:, :], exp64(pt), ALU.mult),
                    wait=[("st", zmm), ("sa", A["ap"]), ("sv", t_gate)], inc=("sv", 1))
                t_ee = {}
                t_mth = {}
                t_vm = {}
                for hh in range(NH + 2):
                    if hh < NH:
                        h = hh
                        g = h // 8
                        gq = ch * 4 + g
                        rdh = A["rdhs"][g]
                        q3 = h % 3
                        lastg = h % 8 == 7
                        vd = self.op("vector", lambda e, h=h, g=g, q3=q3, lt=lt:
                                     e.tensor_scalar(
                                         est[q3][:, :],
                                         bc[g % 2][:, (h % 8) * 128:(h % 8 + 1) * 128],
                                         lt[:, h:h + 1], CLAMP, ALU.subtract, ALU.min),
                                     wait=[(dbp, rdh), ("sv", t_vm.get(h - 3))],
                                     inc=("sq", 1) if lastg else ("sv", 1))
                        t_ee[h] = self.op("scalar", lambda e, q3=q3: e.activation(
                            est[q3][:, :], est[q3][:, :], AF.Exp),
                            wait=[("sq", gq + 1) if lastg else ("sv", vd)],
                            inc=("sa", 1))
                    if hh >= 2:
                        h = hh - 2
                        q = h % 2
                        q3 = h % 3
                        t_vm[h] = vm = self.op("vector", lambda e, h=h, q=q, q3=q3, dk=dk:
                                     e.scalar_tensor_tensor(
                                         emt[q][:, :], est[q3][:, :], dk[:, h:h + 1],
                                         Gm[:, :], ALU.mult, ALU.mult),
                                     wait=[("sa", t_ee[h]), ("st", t_mth.get(h - 2))] +
                                          ([("sv", t_gm)] if h == 0 else []),
                                     inc=("sv", 1))
                        t_mth[h] = self.op("tensor", lambda e, h=h, q=q: e.matmul(
                            P0[:, h * 64:(h + 1) * 64], emt[q][:, :],
                            x_tok[:, h * 64:(h + 1) * 64], start=True, stop=True),
                            wait=[("sv", vm)] + ([("sv", ez), ("sv", t_xtok)]
                                                 if h == 0 else []),
                            inc=("st", 1))
                t_mt_last = t_mth[NH - 1]
                if (ch - 1) in pend_tail:
                    emit_tail(ch - 1)
                ev_y = self.op("vector", lambda e: e.tensor_tensor(
                    y_acc[:, :], y_acc[:, :], P0[:, :], ALU.add),
                    wait=[("st", t_mt_last), ("sv", ez)], inc=("sv", 1))
                tD = self.op("gpsimd", lambda e: e.tensor_tensor(
                    cscr[:, :], x_tok[:, :], c["dexp"][:, :], ALU.mult),
                    wait=[("sv", t_xtok), ("sa", t_endB[ch - 1][1])], inc=("sg", 1))
                t_tD[ch] = tD
                ev_y2 = self.op("vector", lambda e: e.tensor_tensor(
                    y_acc[:, :], y_acc[:, :], cscr[:, :], ALU.add),
                    wait=[("sv", ev_y), ("sg", tD)], inc=("sv", 1))
                t_ya = ev_y2
                if self.dev_phase is not None:
                    if "dev_y" not in self.dram:
                        self.dout("dev_y", [T, DIN])
                    tdmp = self.op("scalar", lambda e, ch=ch: e.dma_start(
                        self.dram["dev_y"][ts(ch), :], y_acc[:, :]),
                        wait=[("sv", ev_y2), ("so", self.cnt["so"])], inc=("so", 16))
                    t_ya = ("so", tdmp)
            else:
                t_ya = None
            smm = None
            for q in range(4):
                smm = self.op("tensor", lambda e, q=q: e.matmul(
                    P0[:, q * 512:(q + 1) * 512], B_tok[:, :],
                    xw[:, q * 512:(q + 1) * 512], start=True, stop=True),
                    wait=([("sv", vxw)] +
                          ([("sv", ev_y)] if not pre else [("sv", t_p0v)])
                          ) if q == 0 else [],
                    inc=("st", 1) if q == 3 else None)
            t_smm = smm
            c1 = self.op("vector", lambda e, gm=gm_: e.tensor_tensor(
                cscr[:, :], H[:, :], exp64(gm), ALU.mult),
                wait=[("sa", A["ag"]), ("sv", t_carry)] +
                     ([(t_ya if isinstance(t_ya, tuple) else ("sv", t_ya))]
                      if not pre else []), inc=("sv", 1))
            t_carry = self.op("vector", lambda e: e.tensor_tensor(
                H[:, :], cscr[:, :], P0[:, :], ALU.add),
                wait=[("st", smm), ("sv", c1)], inc=("sv", 1))
            t_p0v = t_carry
            if not pre:
                ldsz = self.op("gpsimd", lambda e, ch=ch: e.dma_start(
                    sz_t[:, :], self.sz_dram[ts(ch), :]),
                    wait=[("sv", t_gate)], inc=(dzp, 16))
                ldxn = self.op("scalar", lambda e, ch=ch: e.dma_start(
                    xn_t[:, :], self.xn_dram[ts(ch), :]),
                    wait=[("sv", t_xn_pv)], inc=(dhp, 16))
                g1 = self.op("vector", lambda e: e.tensor_tensor(
                    y_acc[:, :], y_acc[:, :], sz_t[:, :], ALU.mult),
                    wait=[(t_ya if isinstance(t_ya, tuple) else ("sv", t_ya)),
                          (dzp, ldsz)], inc=("sv", 1))
                t_gate = g1
                a1 = self.op("scalar", lambda e: e.activation(
                    cscr[:, :], y_acc[:, :], AF.Square, accum_out=ssq[:, :]),
                    wait=[("sv", g1), ("sv", t_carry)], inc=("sa", 1))
                a2 = self.op("scalar", lambda e: e.activation(
                    ssq[:, :], ssq[:, :], AF.Ln, bias=EPS_RMS, scale=1.0 / DIN),
                    wait=[("sa", a1)], inc=("sa", 1))
                a3 = self.op("scalar", lambda e: e.activation(
                    ssq[:, :], ssq[:, :], AF.Exp, scale=-0.5),
                    wait=[("sa", a2)], inc=("sa", 1))
                vyn = self.op("vector", lambda e: e.scalar_tensor_tensor(
                    yn[:, :], y_acc[:, :], ssq[:, :1], c["ssmw"][:, :],
                    ALU.mult, ALU.mult), wait=[("sa", a3)], inc=("sv", 1))
                t_ynt = None
                for f in range(16):
                    t_ynt = transpose(f, yn[:, f * 128:(f + 1) * 128],
                                      ynT[:, f * 128:(f + 1) * 128], BF16,
                                      xtra_tr=[("sv", vyn)] if f == 0 else ())
                pend_tail[ch] = (t_ynt, ldxn)
                if ch + 1 < NCH:
                    t_hbf = self.op("scalar", lambda e: e.activation(
                        H_bf[:, :], H[:, :], AF.Copy),
                        wait=[("sv", t_carry), ("st", t_zmm)], inc=("sa", 1))
            t_endB[ch] = (self.cnt["sv"], self.cnt["sa"], self.cnt["st"])
        if not pre and (NCH - 1) in pend_tail:
            emit_tail(NCH - 1)
        if pre:
            tm = self.op("vector", lambda e: e.tensor_scalar_mul(
                H[:, :], H[:, :], c["smask"][:, 0:1]),
                wait=[("sv", t_carry)], inc=("sv", 1))
            self.t_hspill = self.op("gpsimd", lambda e: e.dma_start(
                self.dram["hstate_dram"], H[:, :]), wait=[("sv", tm)], inc=("dp", 16))
        self.barrier()

    def phase_E(self):
        cv = self.carve
        c = self.c
        P0, P1, PT, PS = self.P0, self.P1, self.PT, self.PS
        wg1 = cv("wg1", OFF["R_C"], [128, NDC * INTER1], BF16)
        wu1 = cv("wu1", OFF["R_D"], [128, NDC * INTER1], BF16)
        wd1 = cv("wd1", OFF["R_D"] + 44032, [128, 22 * DM], BF16)
        h1t = [cv(f"E_h1{p}", OFF["R_B"] + p * 4096, [128, DM], F32)
               for p in range(2)]
        hn = [cv(f"E_hn{p}", OFF["R_B"] + 8192 + p * 2048, [128, DM], BF16)
              for p in range(2)]
        hnTt = [cv(f"E_hnT{p}", OFF["R_B"] + 12288 + p * 2048, [128, NDC * 128],
                   BF16) for p in range(2)]
        h2 = cv("E_h2", OFF["R_B"] + 16384, [128, INTER1], BF16)
        h2t = cv("E_h2t", OFF["R_B"] + 21888, [128, 22 * 128], BF16)
        sqscr = cv("E_sqs", OFF["dt2"], [128, DM], F32)
        ssqE = cv("E_ssq", OFF["devscr"], [128, 2], F32)
        EPARTS = [(0, 1024), (1024, 1024), (2048, 704)]
        wsem = ["dw", "dl0", "dl1", "dp", "dw", "dl0", "dl1", "dp"]
        t_wg, t_wu = {}, {}
        si = 0
        first = [("st", self.cnt["st"])]
        lastw = {}
        for pi, (cb, cn) in enumerate(EPARTS):
            for wdst, wsrc, td in ((wg1, self.w_g1, t_wg), (wu1, self.w_u1, t_wu)):
                s = wsem[si]
                w = first + ([(s, lastw[s])] if s in lastw else [])
                tok = self.op("gpsimd",
                    lambda e, wdst=wdst, wsrc=wsrc, cb=cb, cn=cn: e.dma_start(
                        bass.AP(wdst, cb, [[NDC * INTER1, 128], [INTER1, NDC], [1, cn]]),
                        bass.AP(wsrc.tensor, cb,
                                [[INTER1, 128], [128 * INTER1, NDC], [1, cn]])),
                    inc=(s, 16), wait=w)
                td[pi] = (s, tok)
                lastw[s] = tok
                first = []
                si = (si + 1) % len(wsem)
        t_wd = ("dl1", self.op("gpsimd", lambda e: e.dma_start(
            bass.AP(wd1, 0, [[22 * DM, 128], [DM, 21], [1, DM]]),
            bass.AP(self.w_d1.tensor, 0, [[DM, 128], [128 * DM, 21], [1, DM]])),
            inc=("dl1", 16), wait=[("dl1", lastw["dl1"])]))
        t_wdb = ("dp", self.op("gpsimd", lambda e: e.dma_start(
            bass.AP(wd1, 21 * DM, [[22 * DM, 64], [1, DM]]),
            bass.AP(self.w_d1.tensor, 21 * 128 * DM, [[DM, 64], [1, DM]])),
            inc=("dp", 16), wait=[("dp", lastw["dp"])]))
        slotA_b = P1[:, 640:704].bitcast(BF16)
        slotB_b = PT[:, 256:384]
        t_slot = {0: None, 1: None}

        def transpose(idx, src_ap, dst_ap, xtra_tr=(), xtra_ev=()):
            q = idx % 2
            pp = slotA_b if q == 0 else slotB_b
            np_ = src_ap.shape[0]
            nf = src_ap.shape[1]
            tt = self.op("tensor", lambda e, pp=pp, s=src_ap, np_=np_, nf=nf:
                         e.transpose(pp[0:nf, 0:np_], s, c["ident"][0:np_, 0:np_]),
                         wait=[("sv", t_slot[q])] + list(xtra_tr), inc=("st", 1))
            t_slot[q] = self.op("vector",
                                lambda e, pp=pp, d=dst_ap, nf=nf, np_=np_:
                                e.tensor_copy(d, pp[0:nf, 0:np_]),
                                wait=[("st", tt)] + list(xtra_ev), inc=("sv", 1))
            return t_slot[q]

        t_ldh = {}
        t_rms = {}
        t_vh = {}
        t_hnt = {}
        t_ttl = {}
        t_guT = {}
        t_h2tt = {}
        t_v1b = {}
        t_sp = {}

        def front(i):
            p = i % 2
            dhp = "dh%d" % p
            t_ldh[i] = self.op("sync", lambda e, i=i, p=p: e.dma_start(
                h1t[p][:, :], self.h1_dram[ts(i), :]),
                wait=[("dn%d" % p, t_sp.get(i - 2))], inc=(dhp, 16))
            a1 = self.op("scalar", lambda e, p=p: e.activation(
                sqscr[:, :], h1t[p][:, :], AF.Square, accum_out=ssqE[:, p:p + 1]),
                wait=[(dhp, t_ldh[i]), ("sa", t_rms.get(i - 1))], inc=("sa", 1))
            a2 = self.op("scalar", lambda e, p=p: e.activation(
                ssqE[:, p:p + 1], ssqE[:, p:p + 1], AF.Ln, bias=EPS_RMS,
                scale=1.0 / DM), wait=[("sa", a1)], inc=("sa", 1))
            t_rms[i] = a3 = self.op("scalar", lambda e, p=p: e.activation(
                ssqE[:, p:p + 1], ssqE[:, p:p + 1], AF.Exp, scale=-0.5),
                wait=[("sa", a2)], inc=("sa", 1))
            t_vh[i] = self.op("vector", lambda e, p=p: e.scalar_tensor_tensor(
                hn[p][:, :], h1t[p][:, :], ssqE[:, p:p + 1], c["postw"][:, :],
                ALU.mult, ALU.mult),
                wait=[("sa", a3), ("st", t_ttl.get(i - 2))], inc=("sv", 1))
            for dc in range(NDC):
                t_hnt[i] = transpose(
                    dc, hn[p][:, ts(dc)], hnTt[p][:, ts(dc)],
                    xtra_tr=[("sv", t_vh[i])] if dc == 0 else (),
                    xtra_ev=[("st", t_guT.get(i - 2))] if dc == 0 else ())
            t_ttl[i] = self.cnt["st"]

        parts = [(0, 1024), (1024, 1024), (2048, 704)]
        slots = [P0[:, 0:1024], P0[:, 1024:2048]]
        t_silu_u = {}
        t_spu = {}
        t_dn = {}
        uu = 0
        front(0)
        for i in range(NTIL):
            p = i % 2
            dhp = "dh%d" % p
            for pi, (cb, cn) in enumerate(parts):
                for j, w in ((0, wg1), (1, wu1)):
                    mm = None
                    for dc in range(NDC):
                        mm = self.op("tensor", lambda e, dc=dc, cb=cb, cn=cn,
                                     j=j, w=w, p=p: ([
                            e.matmul(slots[j][:, 0:512],
                                     hnTt[p][:, dc * 128:(dc + 1) * 128],
                                     w[:, dc * INTER1 + cb:dc * INTER1 + cb + min(cn, 512)],
                                     start=(dc == 0), stop=(dc == NDC - 1))] +
                            ([e.matmul(slots[j][:, 512:512 + cn - 512],
                                       hnTt[p][:, dc * 128:(dc + 1) * 128],
                                       w[:, dc * INTER1 + cb + 512:dc * INTER1 + cb + cn],
                                       start=(dc == 0), stop=(dc == NDC - 1))]
                             if cn > 512 else []))[-1],
                            wait=(([("sa", t_silu_u.get(uu - 1)), ("sv", t_hnt[i])]
                                   if j == 0 else [("sv", t_spu.get(uu - 1))]) +
                                  ([(t_wg if j == 0 else t_wu)[pi]] if i == 0 else [])
                                  if dc == 0 else []),
                            inc=("st", 1) if dc == NDC - 1 else None)
                mmtok = self.cnt["st"]
                t_silu_u[uu] = self.op(
                    "scalar", lambda e, cb=cb, cn=cn: e.activation(
                        h2[:, cb:cb + cn], slots[0][:, 0:cn], AF.Silu),
                    wait=[("st", mmtok)] +
                         ([("st", t_h2tt.get(i - 1))] if pi == 0 else []),
                    inc=("sa", 1))
                t_spu[uu] = self.op(
                    "vector", lambda e, cb=cb, cn=cn: e.tensor_tensor(
                        h2[:, cb:cb + cn], h2[:, cb:cb + cn], slots[1][:, 0:cn],
                        ALU.mult),
                    wait=[("sa", t_silu_u[uu])], inc=("sv", 1))
                uu += 1
            t_guT[i] = self.cnt["st"]
            t_h2full = t_spu[uu - 1]
            if i + 1 < NTIL:
                front(i + 1)
            t_tr = None
            for k in range(22):
                ncols = 128 if k < 21 else 64
                t_tr = transpose(k, h2[:, k * 128:k * 128 + ncols],
                                 h2t[0:ncols, k * 128:(k + 1) * 128],
                                 xtra_tr=[("sv", t_h2full)] if k == 0 else ())
            t_h2tt[i] = self.cnt["st"]
            mm = None
            for k in range(22):
                kp = 128 if k < 21 else 64
                for q, pb in ((0, P1), (1, PS)):
                    mm = self.op("tensor", lambda e, k=k, q=q, pb=pb, kp=kp: e.matmul(
                        pb[:, 0:512], h2t[0:kp, k * 128:(k + 1) * 128],
                        wd1[0:kp, k * DM + q * 512:k * DM + (q + 1) * 512],
                        start=(k == 0), stop=(k == 21)),
                        wait=([("sv", t_tr), ("sv", t_v1b.get(i - 1))] +
                              ([t_wd, t_wdb] if i == 0 else [])
                              if k == 0 and q == 0 else []),
                        inc=("st", 1) if k == 21 and q == 1 else None)
            t_dn[i] = mm
            v1a = self.op("vector", lambda e, p=p: e.tensor_tensor(
                h1t[p][:, 0:512], h1t[p][:, 0:512], P1[:, 0:512], ALU.add),
                wait=[("st", mm), (dhp, t_ldh[i])], inc=("sv", 1))
            t_v1b[i] = self.op("vector", lambda e, p=p: e.tensor_tensor(
                h1t[p][:, 512:1024], h1t[p][:, 512:1024], PS[:, 0:512], ALU.add),
                wait=[("sv", v1a)], inc=("sv", 1))
            t_sp[i] = self.op("gpsimd", lambda e, i=i, p=p: e.dma_start(
                self.h1_dram[ts(i), :], h1t[p][:, :]),
                wait=[("sv", t_v1b[i])], inc=("dn%d" % p, 16))
        self.barrier()

    def phase_F(self):
        cv = self.carve
        c = self.c
        P0, P1, PT, PS = self.P0, self.P1, self.PT, self.PS
        SCL = 1.0 / 127.0
        wg2 = cv("wg2", OFF["R_C"], [128, NDC * INTER2], BF16)
        wu2 = cv("wu2", OFF["R_D"], [128, NDC * INTER2], BF16)
        wd2 = cv("wd2", OFF["R_D"] + 49152, [128, 21 * DM], BF16)
        wd2b = cv("wd2b", OFF["dt2"], [128, 3 * DM], BF16)
        xft = [cv(f"F_xf{p}", OFF["c_ssmw"] + p * 2048, [128, NDC * 128], BF16)
               for p in range(2)]
        h1t = [cv(f"F_h1{p}", OFF["R_B"] + p * 4096, [128, DM], F32)
               for p in range(2)]
        xt = [cv(f"F_x{p}", OFF["R_B"] + 8192 + p * 4096, [128, DM], F32)
              for p in range(2)]
        h3 = cv("F_h3", OFF["R_B"] + 16384, [128, INTER2], BF16)
        h3t = cv("F_h3t", OFF["R_B"] + 22528, [128, 24 * 128], BF16)
        wsem = ["dw", "dl0", "dl1", "dp", "dw", "dl0", "dl1", "dp"]
        t_wg, t_wu = {}, {}
        si = 0
        first = [("st", self.cnt["st"])]
        lastw = {}
        for pi in range(3):
            cb = pi * 1024
            for wdst, wsrc, td in ((wg2, self.w_g2, t_wg), (wu2, self.w_u2, t_wu)):
                s = wsem[si]
                w = first + ([(s, lastw[s])] if s in lastw else [])
                tok = self.op("gpsimd",
                    lambda e, wdst=wdst, wsrc=wsrc, cb=cb: e.dma_start(
                        bass.AP(wdst, cb, [[NDC * INTER2, 128], [INTER2, NDC], [1, 1024]]),
                        bass.AP(wsrc.tensor, cb,
                                [[INTER2, 128], [128 * INTER2, NDC], [1, 1024]])),
                    inc=(s, 16), wait=w)
                td[pi] = (s, tok)
                lastw[s] = tok
                first = []
                si = (si + 1) % len(wsem)
        t_wd = ("dl1", self.op("gpsimd", lambda e: e.dma_start(
            bass.AP(wd2, 0, [[21 * DM, 128], [DM, 21], [1, DM]]),
            bass.AP(self.w_d2.tensor, 0, [[DM, 128], [128 * DM, 21], [1, DM]])),
            inc=("dl1", 16), wait=[("dl1", lastw["dl1"])]))
        t_wdb = ("dp", self.op("gpsimd", lambda e: e.dma_start(
            bass.AP(wd2b, 0, [[3 * DM, 128], [DM, 3], [1, DM]]),
            bass.AP(self.w_d2.tensor, 21 * 128 * DM, [[DM, 128], [128 * DM, 3], [1, DM]])),
            inc=("dp", 16), wait=[("dp", lastw["dp"])]))
        slotA_b = P1[:, 640:704].bitcast(BF16)
        slotB_b = PT[:, 256:384]
        t_slot = {0: None, 1: None}

        def transpose(idx, src_ap, dst_ap, xtra_tr=(), xtra_ev=()):
            q = idx % 2
            pp = slotA_b if q == 0 else slotB_b
            np_ = src_ap.shape[0]
            nf = src_ap.shape[1]
            tt = self.op("tensor", lambda e, pp=pp, s=src_ap, np_=np_:
                         e.transpose(pp[:, 0:np_], s, c["ident"][0:np_, 0:np_]),
                         wait=[("sv", t_slot[q])] + list(xtra_tr), inc=("st", 1))
            t_slot[q] = self.op("vector",
                                lambda e, pp=pp, d=dst_ap, nf=nf, np_=np_:
                                e.tensor_copy(d, pp[0:nf, 0:np_]),
                                wait=[("st", tt)] + list(xtra_ev), inc=("sv", 1))
            return t_slot[q]

        parts = [(0, 1024), (1024, 1024), (2048, 1024)]
        slots = [P0[:, 0:1024], P0[:, 1024:2048]]
        t_sil = {}
        t_silu_u = {}
        t_spu = {}
        t_ldx = {}
        t_guT = {}
        t_trlast = {}
        t_dnlast = {}
        t_v1b = {}
        t_out = {}
        t_ost = {}
        uu = 0
        t_ldx[0] = self.op("sync", lambda e: e.dma_start(
            xft[0][:, :], bass.AP(self.xf_dram.tensor, 0,
                                  [[NDC * T, 128], [T, NDC], [1, 128]])),
            wait=[("st", self.cnt["st"])], inc=("de0", 16))
        for i in range(NTIL):
            p = i % 2
            dhp = "dh%d" % p
            if i + 1 < NTIL:
                t_ldx[i + 1] = self.op("sync", lambda e, i=i: e.dma_start(
                    xft[(i + 1) % 2][:, :],
                    bass.AP(self.xf_dram.tensor, (i + 1) * 128,
                            [[NDC * T, 128], [T, NDC], [1, 128]])),
                    wait=[("st", t_guT.get(i - 1))], inc=("de%d" % ((i + 1) % 2), 16))
            ldh = self.op("sync", lambda e, i=i, p=p: e.dma_start(
                h1t[p][:, :], self.h1_dram[ts(i), :]),
                wait=[("so" if p == 0 else "ad", t_ost.get(i - 2))], inc=(dhp, 16))
            ldxo = self.op("sync", lambda e, i=i, p=p: e.dma_start(
                xt[p][:, :], self.x_own[ts(i), :]),
                wait=[("sv", t_out.get(i - 2))], inc=(dhp, 16))
            for pi, (cb, cn) in enumerate(parts):
                for j, w in ((0, wg2), (1, wu2)):
                    mm = None
                    for dc in range(NDC):
                        mm = self.op("tensor", lambda e, i=i, dc=dc, cb=cb, j=j,
                                     w=w, p=p: [
                            e.matmul(slots[j][:, 0:512],
                                     xft[p][:, dc * 128:(dc + 1) * 128],
                                     w[:, dc * INTER2 + cb:dc * INTER2 + cb + 512],
                                     start=(dc == 0), stop=(dc == NDC - 1)),
                            e.matmul(slots[j][:, 512:1024],
                                     xft[p][:, dc * 128:(dc + 1) * 128],
                                     w[:, dc * INTER2 + cb + 512:dc * INTER2 + cb + 1024],
                                     start=(dc == 0), stop=(dc == NDC - 1))][-1],
                            wait=(([("sa", t_silu_u.get(uu - 1)),
                                    ("de%d" % p, t_ldx[i])] if j == 0 else
                                   [("sv", t_spu.get(uu - 1))]) +
                                  ([(t_wg if j == 0 else t_wu)[pi]] if i == 0 else [])
                                  if dc == 0 else []),
                            inc=("st", 1) if dc == NDC - 1 else None)
                mmtok = self.cnt["st"]
                t_silu_u[uu] = t_sil[(i, pi)] = self.op(
                    "scalar", lambda e, cb=cb, cn=cn: e.activation(
                        h3[:, cb:cb + cn], slots[0][:, 0:cn], AF.Silu, scale=SCL),
                    wait=[("st", mmtok)] +
                         ([("st", t_trlast.get(i - 1))] if pi == 0 else []),
                    inc=("sa", 1))
                t_spu[uu] = self.op(
                    "vector", lambda e, cb=cb, cn=cn: e.tensor_tensor(
                        h3[:, cb:cb + cn], h3[:, cb:cb + cn], slots[1][:, 0:cn],
                        ALU.mult),
                    wait=[("sa", t_sil[(i, pi)])], inc=("sv", 1))
                uu += 1
            t_guT[i] = self.cnt["st"]
            t_h3full = t_spu[uu - 1]
            t_tr = None
            for k in range(24):
                t_tr = transpose(k, h3[:, k * 128:(k + 1) * 128],
                                 h3t[:, k * 128:(k + 1) * 128],
                                 xtra_tr=[("sv", t_h3full)] if k == 0 else ())
            t_trlast[i] = self.cnt["st"]
            mm = None
            for k in range(24):
                for q, pb in ((0, P1), (1, PS)):
                    wsrc = (lambda kk: (wd2, kk) if kk < 21 else (wd2b, kk - 21))(k)
                    mm = self.op("tensor", lambda e, k=k, q=q, pb=pb,
                                 wt=wsrc[0], kk=wsrc[1]: e.matmul(
                        pb[:, 0:512], h3t[:, k * 128:(k + 1) * 128],
                        wt[:, kk * DM + q * 512:kk * DM + (q + 1) * 512],
                        start=(k == 0), stop=(k == 23)),
                        wait=([("sv", t_tr), ("sv", t_v1b.get(i - 1))] +
                              ([t_wd, t_wdb] if i == 0 else [])
                              if k == 0 and q == 0 else []),
                        inc=("st", 1) if k == 23 and q == 1 else None)
            t_dnlast[i] = mm
            v1a = self.op("vector", lambda e, p=p: e.scalar_tensor_tensor(
                h1t[p][:, 0:512], P1[:, 0:512], SCL * SCL, h1t[p][:, 0:512],
                ALU.mult, ALU.add),
                wait=[("st", mm), (dhp, ldxo)], inc=("sv", 1))
            t_v1b[i] = self.op("vector", lambda e, p=p: e.scalar_tensor_tensor(
                h1t[p][:, 512:1024], PS[:, 0:512], SCL * SCL, h1t[p][:, 512:1024],
                ALU.mult, ALU.add),
                wait=[("sv", v1a)], inc=("sv", 1))
            t_out[i] = self.op("vector", lambda e, p=p: e.tensor_tensor(
                h1t[p][:, :], h1t[p][:, :], xt[p][:, :], ALU.add),
                wait=[("sv", t_v1b[i]), (dhp, ldxo)], inc=("sv", 1))
            t_ost[i] = self.op("sync", lambda e, i=i, p=p: e.dma_start(
                self.out[ts(i), :], h1t[p][:, :]),
                wait=[("sv", t_out[i])], inc=("so" if p == 0 else "ad", 16))
        self.wait("sync", "so", self.cnt["so"])
        self.wait("sync", "ad", self.cnt["ad"])
        self.wait("gpsimd", "so", self.cnt["so"])
        self.wait("gpsimd", "ad", self.cnt["ad"])

    def dev_dump_B(self, tag):
        d1 = self.dout(f"dev_xbc_{tag}", [128, NCC * CW], BF16)
        d2 = self.dout(f"dev_dtf_{tag}", [32, T])
        d3 = self.dout(f"dev_dtA_{tag}", [32, T])
        self.barrier()
        self.op("scalar", lambda e: e.dma_start(d1, self.xbc[:, :]), inc=("so", 16))
        self.op("scalar", lambda e: e.dma_start(d2, self.dtf[:, :]), inc=("so", 16))
        self.op("scalar", lambda e: e.dma_start(d3, self.dtAf[:, :]), inc=("so", 16))
        self.wait("scalar", "so", self.cnt["so"])
        self.barrier()


def prep_inputs(full, core):
    b, h = core // 2, core % 2
    x = np.asarray(full["L0_attn_hidden_states"])
    own = np.ascontiguousarray(x[b, h * T:(h + 1) * T, :], dtype=np.float32)
    pre = (np.ascontiguousarray(x[b, 0:T, :], dtype=np.float32) if h == 1
           else np.zeros((T, DM), np.float32))
    m = {
        "x_own": own, "x_pre": pre,
        "w_in": np.ascontiguousarray(np.asarray(full["L0_attn_in_proj_w"]).T),
        "w_out": np.ascontiguousarray(np.asarray(full["L0_attn_out_proj_w"]).T),
        "w_g1": np.ascontiguousarray(np.asarray(full["L0_attn_gate_w"]).T),
        "w_u1": np.ascontiguousarray(np.asarray(full["L0_attn_up_w"]).T),
        "w_d1": np.ascontiguousarray(np.asarray(full["L0_attn_down_w"]).T),
        "w_g2": np.ascontiguousarray(np.asarray(full["L0_ffn_w_gate"]).T),
        "w_u2": np.ascontiguousarray(np.asarray(full["L0_ffn_w_up"]).T),
        "w_d2": np.ascontiguousarray(np.asarray(full["L0_ffn_w_down"]).T),
        "conv_w": np.ascontiguousarray(
            np.asarray(full["L0_attn_conv_w"]).reshape(NCC, 128, 4)
            .transpose(1, 0, 2).reshape(128, NCC * 4)),
        "conv_b": np.ascontiguousarray(
            np.asarray(full["L0_attn_conv_b"]).reshape(NCC, 128).T),
        "ssm_w_bc": np.broadcast_to(np.asarray(full["L0_attn_ssm_norm_w"])[None, :],
                                    (128, DIN)).copy(),
        "post_w_bc": np.broadcast_to(np.asarray(full["L0_attn_post_norm_w"])[None, :],
                                     (128, DM)).copy(),
        "d_exp_bc": np.broadcast_to(np.repeat(np.asarray(full["L0_attn_D"]), HPD)[None, :],
                                    (128, DIN)).copy(),
        "dtb_bc": np.ascontiguousarray(np.asarray(full["L0_attn_dt_bias"], np.float32)[:, None]),
        "alog_bc": np.ascontiguousarray(np.asarray(full["L0_attn_A_log"], np.float32)[:, None]),
        "triu_f": np.triu(np.ones((128, 128), np.float32)),
        "triu_b": np.triu(np.ones((128, 128), np.float32)).astype(ml_dtypes.bfloat16),
        "ident_b": np.eye(128, dtype=np.float32).astype(ml_dtypes.bfloat16),
        "ident_f": np.eye(128, dtype=np.float32),
        "fconst": np.broadcast_to(np.asarray(FCONST_VALS, np.float32)[None, :], (128, 8)).copy(),
        "smask": np.full((128, 1), 1.0 if h == 1 else 0.0, np.float32),
        "nwcol": np.ascontiguousarray(
            np.asarray(full["L0_attn_norm_w"]).reshape(NDC, 128).T.astype(np.float32)),
    }
    return m


def build(dev_phase=None):
    b = B(dev_phase)
    b.declare_io()
    b.load_consts()
    b.phase_A()
    if dev_phase == "A":
        pass
    elif dev_phase in ("Bmm", "Bconv"):
        b.phase_B(1, b_stop=dev_phase[1:])
    else:
        b.phase_B(1)
        if dev_phase == "Bp":
            b.dev_dump_B("p")
        elif dev_phase == "B":
            b.phase_B(0)
            b.dev_dump_B("m")
        else:
            b.phase_C(1)
            b.phase_B(0)
            b.phase_C(0)
            if dev_phase in (None, "full"):
                b.phase_E()
                b.phase_F()
            if dev_phase == "C":
                b.barrier()
                b.op("scalar", lambda e: e.dma_start(b.out, b.dram["h1_dram"]),
                     inc=("so", 16))
                b.wait("scalar", "so", b.cnt["so"])
    if dev_phase is not None and dev_phase != "full":
        b.barrier()
        b.op("scalar", lambda e: e.dma_start(b.out, b.dram["xn_dram"]), inc=("so", 16))
        b.wait("scalar", "so", b.cnt["so"])
    return b.finalize()


def kernel(**inputs):
    nc = build(None)
    in_maps = [prep_inputs(inputs, c) for c in range(8)]
    res = run_bass_kernel_spmd(nc, in_maps, core_ids=list(range(8)))
    y = np.empty((4, 4096, DM), np.float32)
    for c in range(8):
        b_, h_ = c // 2, c % 2
        y[b_, h_ * T:(h_ + 1) * T, :] = np.asarray(res.results[c]["out"])
    return y


# version 15
# speedup vs baseline: 1.0168x; 1.0168x over previous
import contextlib
import numpy as np
import ml_dtypes
import concourse.bass as bass
import concourse.mybir as mybir
from concourse.bass_utils import run_bass_kernel_spmd

F32 = mybir.dt.float32
BF16 = mybir.dt.bfloat16
I32 = mybir.dt.int32
AF = mybir.ActivationFunctionType
ALU = mybir.AluOpType
AX = mybir.AxisListType

T = 2048
NTIL = T // 128
Q = 128
NCH = T // Q
NH, HPD, DS = 32, 64, 128
DM, DIN, CONV_DIM, DIP = 1024, 2048, 2304, 4384
NDC = DM // 128
NCC = CONV_DIM // 128
INTER1, INTER2 = 2752, 3072
EPS_LN, EPS_RMS = 1e-5, 1e-6
CLAMP = 30.0
CW = 2052
FCONST_VALS = [1e-5, 1e-6, 1.0, 30.0, 0.0, 0.5, -1.0, 2.0]

ENGINES = ("sync", "tensor", "vector", "scalar", "gpsimd")

OFF = {}
def _mk_off():
    o = 0
    def put(name, nbytes):
        nonlocal o
        OFF[name] = o
        o += (nbytes + 31) // 32 * 32
    put("c_ssmw", 4096); put("c_postw", 2048); put("c_dexp", 4096)
    put("c_dtb", 128); put("c_A", 128); put("c_triu_f", 512)
    put("c_triu_b", 256); put("c_ident", 256); put("c_identf", 512)
    put("c_smask", 32); put("c_fconst", 32); put("c_convw", 320)
    put("c_convb", 96); put("r1col", 128); put("rstdln", 128)
    put("meanb", 128); put("nwcol", 32)
    put("dt2", 16384)
    put("pads", 128); put("lam_tok", 256); put("lamT", 1024)
    put("gam", 256); put("wrow", 256); put("dtw", 256); put("pt", 256)
    put("dtk", 512); put("lamend", 256); put("small2", 512); put("devscr", 32)
    for name, sz in [("R_C", 32768), ("R_A", 16384), ("R_B", 32768),
                     ("R_D", 18944), ("R_E", 73872)]:
        put(name, sz)
    OFF["END"] = o
    assert o <= 208600, o
_mk_off()


def ts(i, n=128):
    return slice(i * n, (i + 1) * n)


class B:
    def __init__(self, dev_phase=None):
        self.dev_phase = dev_phase
        self.nc = bass.Bass("TRN2", target_bir_lowering=False)
        self.es = contextlib.ExitStack()
        self.emits = {e: [] for e in ENGINES}
        self.sems = {}
        self.cnt = {}
        self.dram = {}
        for s in ("sd", "gd", "ad", "sv", "sa", "st", "sg", "so",
                  "dx0", "dx1", "dn0", "dn1", "dr", "dw", "db0", "db1",
                  "dz0", "dz1", "dh0", "dh1", "de0", "de1", "dl0", "dl1", "dp", "sq"):
            self.sem(s)
        self.arena = self.es.enter_context(
            self.nc.sbuf_tensor("ARENA", [128, OFF["END"]], mybir.dt.uint8))
        self.abase = self.nc.lookup_mloc(self.arena).addr
        self._carved = {}
        self.P0 = self.es.enter_context(self.nc.psum_tensor("P0", [128, 2048], F32))
        self.P1 = self.es.enter_context(self.nc.psum_tensor("P1", [128, 1024], F32))
        self.PT = self.es.enter_context(self.nc.psum_tensor("PT", [128, 512], BF16))
        self.PS = self.es.enter_context(self.nc.psum_tensor("PS", [128, 512], F32))

    def carve(self, name, off_bytes, shape, dtype):
        key = name
        if key not in self._carved:
            self._carved[key] = (self.nc.alloc_sbuf_tensor_at(
                name, shape, dtype, offset=self.abase + off_bytes), off_bytes)
        h, ob = self._carved[key]
        assert ob == off_bytes
        return h

    def sem(self, name):
        self.sems[name] = self.es.enter_context(self.nc.semaphore(name))
        self.cnt[name] = 0

    def op(self, eng, fn, inc=None, wait=None):
        tok = None
        if inc is not None:
            self.cnt[inc[0]] += inc[1]
            tok = self.cnt[inc[0]]
        waits = [(s, v) for (s, v) in (wait or []) if v]
        sems = self.sems

        def run(e, fn=fn, waits=waits, inc=inc):
            for sn, v in waits:
                e.wait_ge(sems[sn], v)
            inst = fn(e)
            if inc is not None and inst is not None:
                inst.then_inc(sems[inc[0]], inc[1])
        self.emits[eng].append(run)
        return tok

    def wait(self, eng, semname, val):
        if val:
            sems = self.sems
            self.emits[eng].append(lambda e, sn=semname, v=val: e.wait_ge(sems[sn], v))

    def barrier(self):
        snap = {s: self.cnt[s] for s in self.cnt if s != "so"}
        for e in ENGINES:
            for s, v in snap.items():
                self.wait(e, s, v)

    def din(self, name, shape, dtype=F32):
        self.dram[name] = self.nc.dram_tensor(name, shape, dtype, kind="ExternalInput").ap()
        return self.dram[name]

    def dout(self, name, shape, dtype=F32):
        self.dram[name] = self.nc.dram_tensor(name, shape, dtype, kind="ExternalOutput").ap()
        return self.dram[name]

    def dscratch(self, name, shape, dtype=F32):
        kind = "ExternalOutput" if self.dev_phase is not None else "Internal"
        self.dram[name] = self.nc.dram_tensor(name, shape, dtype, kind=kind).ap()
        return self.dram[name]

    def finalize(self):
        with self.nc.Block() as block:
            for e in ENGINES:
                fns = self.emits[e]
                getattr(block, e)(lambda eng, fns=fns: [f(eng) for f in fns])
        self.es.close()
        return self.nc

    def declare_io(self):
        d = self.din
        self.x_own = d("x_own", [T, DM])
        self.x_pre = d("x_pre", [T, DM])
        self.w_in = d("w_in", [DM, DIP])
        self.w_out = d("w_out", [DIN, DM])
        self.w_g1 = d("w_g1", [DM, INTER1])
        self.w_u1 = d("w_u1", [DM, INTER1])
        self.w_d1 = d("w_d1", [INTER1, DM])
        self.w_g2 = d("w_g2", [DM, INTER2], I32)
        self.w_u2 = d("w_u2", [DM, INTER2], I32)
        self.w_d2 = d("w_d2", [INTER2, DM], I32)
        self.conv_w = d("conv_w", [128, NCC * 4])
        self.conv_b = d("conv_b", [128, NCC])
        self.ssm_w_bc = d("ssm_w_bc", [128, DIN])
        self.post_w_bc = d("post_w_bc", [128, DM])
        self.d_exp_bc = d("d_exp_bc", [128, DIN])
        self.dtb_bc = d("dtb_bc", [32, 1])
        self.alog_bc = d("alog_bc", [32, 1])
        self.triu_f = d("triu_f", [128, 128])
        self.triu_b = d("triu_b", [128, 128], BF16)
        self.ident_b = d("ident_b", [128, 128], BF16)
        self.ident_f = d("ident_f", [128, 128])
        self.fconst = d("fconst", [128, 8])
        self.smask = d("smask", [128, 1])
        self.nwcol_h = d("nwcol", [128, NDC])
        self.out = self.dout("out", [T, DM])
        ds = self.dscratch
        self.xn_dram = ds("xn_dram", [T, DM])
        self.sz_dram = ds("sz_dram", [T, DIN], BF16)
        self.xf_dram = ds("xf_dram", [128, NDC * T], BF16)
        self.xfp_dram = ds("xfp_dram", [128, NDC * T], BF16)
        self.h1_dram = ds("h1_dram", [T, DM])
        self.h2t_dram = ds("h2t_dram", [128, 22 * T], BF16)
        self.h3t_dram = ds("h3t_dram", [128, 24 * T], BF16)
        self.hstate_dram = ds("hstate_dram", [128, DIN])
        self.row_dram = ds("row_dram", [1, 2 * T])
        self.lam_dram = ds("lam_dram", [1, NH * Q + 64])

    def load_consts(self):
        c = {}
        cv = self.carve
        c["ssmw"] = cv("c_ssmw", OFF["c_ssmw"], [128, DIN], BF16)
        c["postw"] = cv("c_postw", OFF["c_postw"], [128, DM], BF16)
        c["dexp"] = cv("c_dexp", OFF["c_dexp"], [128, DIN], BF16)
        c["dtb"] = cv("c_dtb", OFF["c_dtb"], [32, 1], F32)
        c["A"] = cv("c_A", OFF["c_A"], [32, 1], F32)
        c["triu_f"] = cv("c_triu_f", OFF["c_triu_f"], [128, 128], F32)
        c["triu_b"] = cv("c_triu_b", OFF["c_triu_b"], [128, 128], BF16)
        c["ident"] = cv("c_ident", OFF["c_ident"], [128, 128], BF16)
        c["identf"] = cv("c_identf", OFF["c_identf"], [128, 128], F32)
        c["smask"] = cv("c_smask", OFF["c_smask"], [128, 1], F32)
        c["fconst"] = cv("c_fconst", OFF["c_fconst"], [128, 8], F32)
        c["convw"] = cv("c_convw", OFF["c_convw"], [128, NCC * 4], F32)
        c["convb"] = cv("c_convb", OFF["c_convb"], [128, NCC], F32)
        self.c = c
        for j, v in enumerate(FCONST_VALS):
            self.nc.const_aps.register(F32, v, c["fconst"][:, j:j + 1])
        self.r1col = cv("r1col", OFF["r1col"], [128, 2 * NTIL], F32)
        self.rstdln = cv("rstdln", OFF["rstdln"], [128, 2 * NTIL], F32)
        self.meanb = cv("meanb", OFF["meanb"], [128, 2 * NTIL], F32)
        self.nwcol = cv("nwcol", OFF["nwcol"], [128, NDC], F32)
        self.dt2 = cv("dt2", OFF["dt2"], [32, 2 * T], F32)
        self.dtf = self.dt2[:, 0:T]
        self.dtAf = self.dt2[:, T:2 * T]
        self.dtscr_h = None
        self.pads_sb = cv("pads", OFF["pads"], [128, NCC * 3], BF16)
        self.lam_tok = [cv(f"lam_tok{p}", OFF["lam_tok"] + p * 128, [128, NH], F32)
                        for p in range(2)]
        self.lamT = [cv(f"lamT{p}", OFF["lamT"] + p * 512, [32, Q], F32)
                     for p in range(2)]
        self.gam = [cv(f"gam{p}", OFF["gam"] + p * 128, [128, NH], F32)
                    for p in range(2)]
        self.wrow = [cv(f"wrow{p}", OFF["wrow"] + p * 128, [128, NH], F32)
                     for p in range(2)]
        self.dtw = [cv(f"dtw{p}", OFF["dtw"] + p * 128, [128, NH], F32)
                    for p in range(2)]
        self.ptb = [cv(f"pt{p}", OFF["pt"] + p * 128, [128, NH], F32)
                    for p in range(2)]
        self.dtk = [cv(f"dtk{p}", OFF["dtk"] + p * 256, [128, 2 * NH], F32)
                    for p in range(2)]
        self.lamend = [cv(f"lamend{p}", OFF["lamend"] + p * 128, [128, NH], F32)
                       for p in range(2)]
        self.devscr = cv("devscr", OFF["devscr"], [128, 1], F32)
        pairs = [("ssmw", self.ssm_w_bc), ("postw", self.post_w_bc),
                 ("dexp", self.d_exp_bc), ("dtb", self.dtb_bc),
                 ("triu_f", self.triu_f), ("triu_b", self.triu_b),
                 ("ident", self.ident_b), ("identf", self.ident_f),
                 ("smask", self.smask), ("fconst", self.fconst),
                 ("A", self.alog_bc)]
        for key, src in pairs:
            kind = "gpsimd" if (c[key].dtype == BF16 and src.dtype == F32) else "sync"
            q = "gd" if kind == "gpsimd" else "sd"
            self.op(kind, lambda e, dd=c[key], s=src: e.dma_start(dd[:, :], s),
                    inc=(q, 16))
        self.op("sync", lambda e: e.dma_start(c["convw"][:, :], self.conv_w), inc=("sd", 16))
        self.op("sync", lambda e: e.dma_start(c["convb"][:, :], self.conv_b), inc=("sd", 16))
        self.op("sync", lambda e: e.dma_start(self.nwcol[:, :], self.nwcol_h), inc=("sd", 16))
        t_sd, t_gd = self.cnt["sd"], self.cnt["gd"]
        ta = self.op("scalar", lambda e: e.activation(c["A"][:, :], c["A"][:, :], AF.Exp),
                     wait=[("sd", t_sd), ("gd", t_gd)], inc=("sa", 1))
        self.op("vector", lambda e: e.tensor_scalar_mul(c["A"][:, :], c["A"][:, :], -1.0),
                wait=[("sa", ta)], inc=("sv", 1))
        self.barrier()

    def phase_A(self):
        cv = self.carve
        self.xf = cv("xf", OFF["R_B"], [128, NDC * T], BF16)
        NT2 = 2 * NTIL
        off = OFF["R_E"]
        xt = [cv(f"A_x{p}", off + p * 4096, [128, DM], F32) for p in range(2)]
        xn = [cv(f"A_n{p}", off + 8192 + p * 4096, [128, DM], F32) for p in range(2)]
        sq = [cv(f"A_q{p}", off + 16384 + p * 4096, [128, DM], F32) for p in range(2)]
        ssqb = cv("A_ssq", off + 24576, [128, NT2], F32)
        sumb = cv("A_sum", off + 24704, [128, NT2], F32)
        varb = cv("A_var", off + 24832, [128, NT2], F32)
        scrb = cv("A_scr", off + 24960, [128, NT2], F32)
        rrow = cv("A_rrow", off + 25088, [32, 128], F32)
        stg = [cv(f"A_st{p}", off + 25600 + p * 2048, [128, DM], BF16) for p in range(2)]

        t_sq = {}
        t_sum = {}
        for k in range(NT2):
            half, i = divmod(k, NTIL)
            src_h = self.x_own if half == 0 else self.x_pre
            p = k % 2
            dxp = "dx%d" % p
            ld = self.op("sync", lambda e, x=xt[p], s=src_h, i=i:
                         e.dma_start(x[:, :], s[ts(i), :]), inc=(dxp, 16),
                         wait=[("sa", t_sq.get(k - 2)), ("sv", t_sum.get(k - 2))])
            t_sq[k] = self.op("scalar", lambda e, x=xt[p], q=sq[p], k=k: e.activation(
                q[:, :], x[:, :], AF.Square, accum_out=ssqb[:, k:k + 1]),
                wait=[(dxp, ld)], inc=("sa", 1))
            t_sum[k] = self.op("vector", lambda e, x=xt[p], k=k: e.reduce_sum(
                sumb[:, k:k + 1], x[:, :], axis=AX.X), wait=[(dxp, ld)], inc=("sv", 1))
        h0 = self.op("vector", lambda e: e.tensor_scalar_mul(
            self.meanb[:, :], sumb[:, :], 1.0 / DM), inc=("sv", 1),
            wait=[("sv", t_sum[NT2 - 1]), ("sa", t_sq[NT2 - 1])])
        h1 = self.op("vector", lambda e: e.tensor_scalar_mul(
            varb[:, :], ssqb[:, :], 1.0 / DM), inc=("sv", 1))
        h2 = self.op("vector", lambda e: e.tensor_mul(
            scrb[:, :], self.meanb[:, :], self.meanb[:, :]),
            wait=[("sv", h0)], inc=("sv", 1))
        h3 = self.op("vector", lambda e: e.tensor_sub(
            varb[:, :], varb[:, :], scrb[:, :]), wait=[("sv", h2)], inc=("sv", 1))
        a0 = self.op("scalar", lambda e: e.activation(
            scrb[:, :], varb[:, :], AF.Ln, bias=EPS_LN), wait=[("sv", h3)], inc=("sa", 1))
        a1 = self.op("scalar", lambda e: e.activation(
            self.rstdln[:, :], scrb[:, :], AF.Exp, scale=-0.5),
            wait=[("sa", a0)], inc=("sa", 1))
        h4 = self.op("vector", lambda e: e.tensor_mul(
            scrb[:, :], self.rstdln[:, :], self.rstdln[:, :]),
            wait=[("sa", a1)], inc=("sv", 1))
        h5 = self.op("vector", lambda e: e.tensor_mul(
            scrb[:, :], scrb[:, :], varb[:, :]), wait=[("sv", h4)], inc=("sv", 1))
        a2 = self.op("scalar", lambda e: e.activation(
            scrb[:, :], scrb[:, :], AF.Ln, bias=EPS_RMS), wait=[("sv", h5)], inc=("sa", 1))
        a3 = self.op("scalar", lambda e: e.activation(
            self.r1col[:, :], scrb[:, :], AF.Exp, scale=-0.5),
            wait=[("sa", a2)], inc=("sa", 1))
        self.wait("vector", "sa", a3)

        t_v3 = {}
        t_tr = {}
        t_sp = {}
        ev_tok = {}
        for k in range(NT2):
            half, i = divmod(k, NTIL)
            src_h = self.x_own if half == 0 else self.x_pre
            p = k % 2
            dxp, dnp = "dx%d" % p, "dn%d" % p
            ld = self.op("sync", lambda e, x=xt[p], s=src_h, i=i:
                         e.dma_start(x[:, :], s[ts(i), :]), inc=(dxp, 16),
                         wait=[("sv", t_v3.get(k - 2))] +
                              ([("sa", a3), ("sv", h5)] if k < 2 else []))
            t_v3[k] = self.op("vector", lambda e, x=xt[p], n=xn[p], k=k:
                              e.tensor_scalar(n[:, :], x[:, :], self.meanb[:, k:k + 1],
                                              self.rstdln[:, k:k + 1],
                                              ALU.subtract, ALU.mult),
                              wait=[(dxp, ld), ("st", t_tr.get(k - 2)),
                                    (dnp, t_sp.get(k - 2))],
                              inc=("sv", 1))
            if half == 0:
                t_sp[k] = self.op("gpsimd", lambda e, n=xn[p], i=i: e.dma_start(
                    self.xn_dram[ts(i), :], n[:, :]),
                    wait=[("sv", t_v3[k])], inc=(dnp, 16))
            for dck in range(NDC):
                pp = self.P1[:, 0:128] if dck % 2 == 0 else self.P1[:, 512:640]
                tt = self.op("tensor", lambda e, n=xn[p], d=dck, pp=pp: e.transpose(
                    pp, n[:, ts(d)], self.c["identf"][:, :]),
                    wait=[("sv", t_v3[k]), ("sv", ev_tok.get(dck % 2))],
                    inc=("st", 1))
                if half == 0:
                    dst = self.xf[:, dck * T + i * 128: dck * T + (i + 1) * 128]
                else:
                    dst = stg[p][:, ts(dck)]
                ev_tok[dck % 2] = self.op(
                    "vector", lambda e, pp=pp, dst=dst: e.tensor_copy(dst, pp),
                    wait=[("st", tt)] + ([("dn%d" % p, t_sp.get(k - 2))]
                                         if half == 1 else []),
                    inc=("sv", 1))
            t_tr[k] = self.cnt["st"]
            if half == 1:
                ev_all = self.cnt["sv"]
                t_sp[k] = self.op("gpsimd", lambda e, s=stg[p], i=i: e.dma_start(
                    bass.AP(self.xfp_dram.tensor, i * 128,
                            [[NDC * T, 128], [T, NDC], [1, 128]]),
                    s[:, :]), wait=[("sv", ev_all)], inc=(dnp, 16))

        tt = self.op("tensor", lambda e: e.transpose(
            self.PS[0:32, 128:256], self.r1col[:, :], self.c["identf"][:, :]),
            wait=[("sa", a3)] + [("sv", ev_tok.get(q)) for q in range(2)],
            inc=("st", 1))
        ev = self.op("vector", lambda e: e.tensor_copy(rrow[:, :], self.PS[0:32, 128:256]),
                     wait=[("st", tt)], inc=("sv", 1))
        self.t_rowdram = self.op("sync", lambda e: e.dma_start(
            bass.AP(self.row_dram.tensor, 0, [[128, 32], [1, 128]]), rrow[:, :]),
            wait=[("sv", ev)], inc=("dr", 16))
        self.op("gpsimd", lambda e: e.dma_start(self.xf_dram, self.xf[:, :]),
                wait=[("sv", self.cnt["sv"])], inc=("gd", 16))
        self.barrier()

    def phase_B(self, stream, b_stop=None):
        pre = stream == 1
        cv = self.carve
        self.xbc = cv("xbc", OFF["R_E"], [128, NCC * CW], BF16)
        xbc = self.xbc
        self.r1bc = cv("r1bc", OFF["R_A"], [128, T], F32)
        self.dtscr = cv("dtscr", OFF["R_A"] + 8192, [32, T], F32)
        t_r1 = self.op("sync", lambda e, o=(T if pre else 0): e.dma_start(
            self.r1bc[:, :], bass.AP(self.row_dram.tensor, o, [[0, 128], [1, T]])),
            wait=[("dr", self.t_rowdram), ("sv", self.cnt["sv"]),
                  ("sa", self.cnt["sa"])], inc=("dr", 16))
        self.wbuf = cv("wbuf", OFF["R_D"], [128, NDC * 1184], BF16)
        xfh = cv("xfh", OFF["R_C"], [128, NDC * 1024], BF16)

        slots = [self.P0[:, 0:1024], self.P0[:, 1024:2048], self.P1[:, 0:1024]]
        t_ev = {}
        t_mm = {}
        ev_ec = {}
        uid = 0
        t_wv = getattr(self, "t_wdone", None)
        acc = [cv(f"B_acc{q}", OFF["R_C"] + 20480 + q * 4096, [128, T], BF16)
               for q in range(3)]
        t_conv = {}
        t_silu = {}
        self.op("vector", lambda e: e.memset(
            bass.AP(xbc, 2051, [[NCC * CW, 128], [CW, NCC], [1, 1]]), 0.0),
            inc=("sv", 1))

        def emit_pads(lo, hi):
            n = hi - lo
            if pre:
                pz = self.op("vector", lambda e, lo=lo, n=n: e.memset(
                    bass.AP(xbc, lo * CW, [[NCC * CW, 128], [CW, n], [1, 3]]), 0.0),
                    wait=[("sv", ev_ec[hi - 1])], inc=("sv", 1))
                tp = self.op("vector", lambda e, lo=lo, n=n: e.tensor_copy(
                    bass.AP(self.pads_sb, lo * 3, [[NCC * 3, 128], [3, n], [1, 3]]),
                    bass.AP(xbc, lo * CW + 2048, [[NCC * CW, 128], [CW, n], [1, 3]])),
                    wait=[("sv", ev_ec[hi - 1])], inc=("sv", 1))
                self.t_pads = tp
                return pz, tp
            pz = self.op("vector", lambda e, lo=lo, n=n: e.tensor_copy(
                bass.AP(xbc, lo * CW, [[NCC * CW, 128], [CW, n], [1, 3]]),
                bass.AP(self.pads_sb, lo * 3, [[NCC * 3, 128], [3, n], [1, 3]])),
                wait=[("sv", self.t_pads)], inc=("sv", 1))
            return pz, None

        def emit_conv(lo, hi):
            pz, tp = emit_pads(lo, hi)
            for g in range(lo, hi, 3):
                grp = list(range(g, min(g + 3, hi)))
                for step in range(4):
                    for j, ec in enumerate(grp):
                        a = acc[j]
                        if step == 0:
                            t_conv[ec] = self.op("vector", lambda e, ec=ec, a=a:
                                e.tensor_scalar(
                                    a[:, :], xbc[:, ec * CW:ec * CW + T],
                                    self.c["convw"][:, ec * 4:ec * 4 + 1], None,
                                    ALU.mult),
                                wait=[("sv", pz), ("sv", ev_ec[ec]),
                                      ("sa", t_silu.get(ec - 3))],
                                inc=("sv", 1))
                        else:
                            t_conv[ec] = self.op("vector", lambda e, ec=ec, a=a, k=step:
                                e.scalar_tensor_tensor(
                                    a[:, :], xbc[:, ec * CW + k:ec * CW + k + T],
                                    self.c["convw"][:, ec * 4 + k:ec * 4 + k + 1],
                                    a[:, :], ALU.mult, ALU.add),
                                wait=[("sv", t_conv[ec])], inc=("sv", 1))
                for j, ec in enumerate(grp):
                    t_silu[ec] = self.op("scalar", lambda e, ec=ec, a=acc[j]:
                        e.activation(
                            xbc[:, ec * CW + 3:ec * CW + 3 + T], a[:, :], AF.Silu,
                            bias=self.c["convb"][:, ec:ec + 1]),
                        wait=[("sv", t_conv[ec])] +
                             ([("sv", tp)] if pre else []),
                        inc=("sa", 1))

        for halfw, (cb, ncol, ecs, has_dt) in enumerate([
                (DIN, 1152, list(range(0, 9)), False),
                (DIN + 1152, 1184, list(range(9, 18)), True)]):
            ldw = self.op("gpsimd", lambda e, cb=cb, ncl=ncol: e.dma_start(
                bass.AP(self.wbuf, 0, [[NDC * 1184, 128], [1184, NDC], [1, ncl]]),
                bass.AP(self.w_in.tensor, cb, [[DIP, 128], [128 * DIP, NDC], [1, ncl]])),
                inc=("dw", 16), wait=[("st", self.cnt["st"])])
            wtoks = []
            for dc in range(NDC):
                wtoks.append(self.op("vector", lambda e, dc=dc, ncl=ncol:
                    e.tensor_scalar_mul(self.wbuf[:, dc * 1184:dc * 1184 + ncl],
                                        self.wbuf[:, dc * 1184:dc * 1184 + ncl],
                                        self.nwcol[:, dc:dc + 1]),
                    wait=[("dw", ldw)], inc=("sv", 1)))
            t_w = wtoks[-1]
            for th in range(2):
                if pre:
                    t_ldx = self.op("sync", lambda e, th=th: e.dma_start(
                        bass.AP(xfh, 0, [[NDC * 1024, 128], [1024, NDC], [1, 1024]]),
                        bass.AP(self.xfp_dram.tensor, th * 1024,
                                [[NDC * T, 128], [T, NDC], [1, 1024]])),
                        inc=("dx0", 16), wait=[("st", self.cnt["st"])])
                    rhs = lambda dc: xfh[:, dc * 1024:(dc + 1) * 1024]
                    wv_rhs = [("dx0", t_ldx)]
                else:
                    rhs = lambda dc: self.xf[:, dc * T + th * 1024:dc * T + th * 1024 + 1024]
                    wv_rhs = []
                for ec in ecs:
                    colrel = ec * 128 - (0 if halfw == 0 else 1152)
                    sl = slots[uid % 3]
                    for dc in range(NDC):
                        w0 = (wv_rhs + [("sv", t_w), ("sv", t_ev.get(uid - 3))]
                              if dc == 0 else [])
                        mm = self.op("tensor", lambda e, cr=colrel, dc=dc, sl=sl, r=rhs(dc):
                                     [e.matmul(sl[:, 0:512],
                                               self.wbuf[:, dc * 1184 + cr:
                                                          dc * 1184 + cr + 128],
                                               r[:, 0:512],
                                               start=(dc == 0), stop=(dc == NDC - 1)),
                                      e.matmul(sl[:, 512:1024],
                                               self.wbuf[:, dc * 1184 + cr:
                                                          dc * 1184 + cr + 128],
                                               r[:, 512:1024],
                                               start=(dc == 0), stop=(dc == NDC - 1))][-1],
                                     wait=w0,
                                     inc=("st", 1) if dc == NDC - 1 else None)
                    t_mm[uid] = mm
                    t_ev[uid] = self.op("vector", lambda e, ec=ec, th=th, sl=sl:
                        e.tensor_tensor(
                            xbc[:, ec * CW + 3 + th * 1024:ec * CW + 3 + th * 1024 + 1024],
                            sl, self.r1bc[:, th * 1024:th * 1024 + 1024], ALU.mult),
                        wait=[("st", mm), ("dr", t_r1)], inc=("sv", 1))
                    if th == 1:
                        ev_ec[ec] = t_ev[uid]
                    uid += 1
                if has_dt:
                    sl = slots[uid % 3]
                    for dc in range(NDC):
                        w0 = (wv_rhs + [("sv", t_w), ("sv", t_ev.get(uid - 3))]
                              if dc == 0 else [])
                        mm = self.op("tensor", lambda e, dc=dc, sl=sl, r=rhs(dc):
                            [e.matmul(sl[0:32, 0:512],
                                      self.wbuf[:, dc * 1184 + 1152:dc * 1184 + 1184],
                                      r[:, 0:512],
                                      start=(dc == 0), stop=(dc == NDC - 1)),
                             e.matmul(sl[0:32, 512:1024],
                                      self.wbuf[:, dc * 1184 + 1152:dc * 1184 + 1184],
                                      r[:, 512:1024],
                                      start=(dc == 0), stop=(dc == NDC - 1))][-1],
                            wait=w0, inc=("st", 1) if dc == NDC - 1 else None)
                    self.t_dtev = t_ev[uid] = self.op(
                        "vector", lambda e, th=th, sl=sl: e.tensor_copy(
                            self.dtf[:, th * 1024:th * 1024 + 1024], sl[0:32, :]),
                        wait=[("st", mm)], inc=("sv", 1))
                    uid += 1
            if halfw == 0 and b_stop != "mm":
                emit_conv(0, 9)
        self.t_wdone = self.cnt["st"]

        if not pre and b_stop != "mm":
            zst = [self.carve(f"zst{q}", OFF["R_C"] + 16384 + q * 2048, [128, 1024], BF16)
                   for q in range(2)]
            t_zev = {}
            t_zsp = {}
            zu = 0
            for zh in range(2):
                ldw = self.op("gpsimd", lambda e, zh=zh: e.dma_start(
                    bass.AP(self.wbuf, 0, [[NDC * 1184, 128], [1184, NDC], [1, 1024]]),
                    bass.AP(self.w_in.tensor, zh * 1024,
                            [[DIP, 128], [128 * DIP, NDC], [1, 1024]])),
                    inc=("dw", 16), wait=[("st", self.cnt["st"])])
                wtoks = []
                for dc in range(NDC):
                    wtoks.append(self.op("vector", lambda e, dc=dc:
                        e.tensor_scalar_mul(self.wbuf[:, dc * 1184:dc * 1184 + 1024],
                                            self.wbuf[:, dc * 1184:dc * 1184 + 1024],
                                            self.nwcol[:, dc:dc + 1]),
                        wait=[("dw", ldw)], inc=("sv", 1)))
                t_w = wtoks[-1]
                for i in range(NTIL):
                    sl = slots[zu % 3]
                    for dc in range(NDC):
                        w0 = (([("sv", t_w), ("sa", t_zev.get(zu - 3))] +
                               ([("sv", t_ev[uid - 1])] if zu < 3 else []))
                              if dc == 0 else [])
                        mm = self.op("tensor", lambda e, i=i, dc=dc, sl=sl:
                            [e.matmul(sl[:, 0:512],
                                      self.xf[:, dc * T + i * 128:dc * T + (i + 1) * 128],
                                      self.wbuf[:, dc * 1184:dc * 1184 + 512],
                                      start=(dc == 0), stop=(dc == NDC - 1)),
                             e.matmul(sl[:, 512:1024],
                                      self.xf[:, dc * T + i * 128:dc * T + (i + 1) * 128],
                                      self.wbuf[:, dc * 1184 + 512:dc * 1184 + 1024],
                                      start=(dc == 0), stop=(dc == NDC - 1))][-1],
                            wait=w0, inc=("st", 1) if dc == NDC - 1 else None)
                    t_zev[zu] = self.op("scalar", lambda e, i=i, sl=sl, zq=zu % 2:
                        e.activation(zst[zq][:, :], sl, AF.Silu,
                                     scale=self.r1col[:, i:i + 1]),
                        wait=[("st", mm), ("dn%d" % (zu % 2), t_zsp.get(zu - 2))],
                        inc=("sa", 1))
                    t_zsp[zu] = self.op("gpsimd", lambda e, i=i, zh=zh, zq=zu % 2:
                        e.dma_start(self.sz_dram[ts(i), zh * 1024:(zh + 1) * 1024],
                                    zst[zq][:, :]),
                        wait=[("sa", t_zev[zu])], inc=("dn%d" % (zu % 2), 16))
                    zu += 1
            self.t_zdone = self.cnt["sa"]

        if b_stop == "mm":
            return
        emit_conv(9, NCC)
        self.t_xbc = self.cnt["sa"]
        if b_stop == "conv":
            return

        h0 = self.op("vector", lambda e: e.tensor_mul(
            self.dtf[:, :], self.dtf[:, :], self.r1bc[0:32, :]),
            wait=[("sv", self.t_dtev), ("dr", t_r1)], inc=("sv", 1))
        h1 = self.op("vector", lambda e: e.tensor_tensor(
            self.dtf[:, :], self.dtf[:, :],
            bass.AP(self.c["dtb"], 0, [[1, 32], [0, T]]), ALU.add),
            wait=[("sv", h0)], inc=("sv", 1))
        a0 = self.op("scalar", lambda e: e.activation(self.dtscr[:, :], self.dtf[:, :],
                     AF.Abs), wait=[("sv", h1)], inc=("sa", 1))
        a1 = self.op("scalar", lambda e: e.activation(self.dtscr[:, :], self.dtscr[:, :],
                     AF.Exp, scale=-1.0), wait=[("sa", a0)], inc=("sa", 1))
        a2 = self.op("scalar", lambda e: e.activation(self.dtscr[:, :], self.dtscr[:, :],
                     AF.Ln, bias=1.0), wait=[("sa", a1)], inc=("sa", 1))
        h2 = self.op("vector", lambda e: e.tensor_scalar_max(
            self.dtf[:, :], self.dtf[:, :], 0.0), wait=[("sa", a0)], inc=("sv", 1))
        h3 = self.op("vector", lambda e: e.tensor_tensor(
            self.dtf[:, :], self.dtf[:, :], self.dtscr[:, :], ALU.add),
            wait=[("sa", a2), ("sv", h2)], inc=("sv", 1))
        h4 = self.op("vector", lambda e: e.tensor_tensor(
            self.dtAf[:, :], self.dtf[:, :],
            bass.AP(self.c["A"], 0, [[1, 32], [0, T]]), ALU.mult),
            wait=[("sv", h3)], inc=("sv", 1))
        self.t_dt = h4
        self.barrier()

    def phase_C(self, stream):
        import os as _os
        pre = stream == 1
        cv = self.carve
        xbc = self.xbc
        c = self.c
        H = cv("Hst", OFF["R_D"] + 8192, [128, DIN], F32)
        bc = [cv(f"bcast{q}", OFF["R_D"] + q * 4096, [128, 1024], F32) for q in range(2)]
        est = [cv(f"Est{q}", OFF["R_D"] + 16384 + q * 512, [128, 128], F32)
               for q in range(3)]
        emt = [cv(f"Emt{q}", OFF["R_D"] + 17920 + q * 256, [128, 128], BF16)
               for q in range(2)]
        y_acc = cv("y_acc", OFF["R_C"], [128, DIN], F32)
        ynT = cv("ynT", OFF["R_C"] + 8192, [128, DIN], BF16)
        xn_t = cv("xn_t", OFF["R_C"] + 12288, [128, DM], F32)
        h1_t = cv("h1_t", OFF["R_C"] + 16384, [128, DM], F32)
        xw = cv("xw", OFF["R_C"] + 20480, [128, DIN], BF16)
        cscr = cv("cscr", OFF["R_C"] + 24576, [128, DIN], F32)
        sz_t = cv("sz_t", OFF["R_A"], [128, DIN], BF16)
        x_tok = cv("x_tok", OFF["R_A"] + 4096, [128, DIN], BF16)
        H_bf = cv("H_bf", OFF["R_A"] + 8192, [128, DIN], BF16)
        yn = cv("yn", OFF["R_A"] + 12288, [128, DIN], BF16)
        B_tok = cv("B_tok", OFF["small2"], [128, 128], BF16)
        Gm = cv("Gm", OFF["small2"] + 256, [128, 128], BF16)
        ssq = cv("ssq", OFF["devscr"], [128, 1], F32)
        wout = cv("wout", OFF["R_B"], [128, 16 * DM], BF16)
        P0, P1, PT, PS = self.P0, self.P1, self.PT, self.PS

        if pre:
            t_h0 = self.op("vector", lambda e: e.memset(H[:, :], 0.0), inc=("sv", 1))
            t_carry = t_h0
        else:
            t_h = self.op("sync", lambda e: e.dma_start(H[:, :], self.dram["hstate_dram"]),
                          wait=[("dp", self.t_hspill)], inc=("dr", 16))
            self.wait("vector", "dr", t_h)
            t_carry = None
            self.t_wo = self.op("gpsimd", lambda e: e.dma_start(
                bass.AP(wout, 0, [[16 * DM, 128], [DM, 16], [1, DM]]),
                bass.AP(self.w_out.tensor, 0, [[DM, 128], [128 * DM, 16], [1, DM]])),
                inc=("dw", 16), wait=[("st", self.cnt["st"])])
            t_hbf = self.op("scalar", lambda e: e.activation(
                H_bf[:, :], H[:, :], AF.Copy), wait=[("dr", t_h)], inc=("sa", 1))

        slotA_f = PS[:, 0:128]
        slotA_b = PS[:, 128:192].bitcast(BF16)
        slotB_f = PT[:, 0:256].bitcast(F32)
        slotB_b = PT[:, 256:384]
        t_slot = {0: None, 1: None}

        def transpose(idx, src_ap, dst_ap, dt_, xtra_tr=(), xtra_ev=()):
            q = idx % 2
            pp = (slotA_f if dt_ == F32 else slotA_b) if q == 0 else \
                 (slotB_f if dt_ == F32 else slotB_b)
            ident = c["identf"] if dt_ == F32 else c["ident"]
            np_, nf = src_ap.shape[0], src_ap.shape[1]
            tt = self.op("tensor", lambda e, pp=pp, s=src_ap, i=ident, np_=np_, nf=nf:
                         e.transpose(pp[0:nf, 0:np_], s, i[0:np_, 0:np_]),
                         wait=[("sv", t_slot[q])] + list(xtra_tr), inc=("st", 1))
            t_slot[q] = self.op("vector",
                                lambda e, pp=pp, d=dst_ap, nf=nf, np_=np_:
                                e.tensor_copy(d, pp[0:nf, 0:np_]),
                                wait=[("st", tt)] + list(xtra_ev), inc=("sv", 1))
            return t_slot[q]

        exp64 = lambda buf, o=0: bass.AP(buf, o, [[buf.shape[1], 128], [1, NH], [0, HPD]])

        stA = {}
        t_endB = {-2: (0, 0, 0), -1: (0, 0, 0)}
        t_grp = {}
        t_gate = None
        t_zmm = None
        t_smm = None
        t_p0v = None
        t_xn_pv = None
        t_h1_sp = {}
        t_dchain = {-2: 0, -1: 0}
        t_lch = {-2: 0, -1: 0}
        t_tD = {}
        pend_tail = {}

        def emit_tail(cc):
            nonlocal t_xn_pv
            ynt_c, ldxn_c = pend_tail.pop(cc)
            omm_c = None
            for f in range(16):
                for q in range(2):
                    omm_c = self.op("tensor", lambda e, f=f, q=q: e.matmul(
                        P1[:, q * 512:(q + 1) * 512], ynT[:, f * 128:(f + 1) * 128],
                        wout[:, f * DM + q * 512:f * DM + (q + 1) * 512],
                        start=(f == 0), stop=(f == 15)),
                        wait=([("sv", ynt_c), ("dw", self.t_wo),
                               ("sv", t_xn_pv)]
                              if f == 0 and q == 0 else []),
                        inc=("st", 1) if f == 15 and q == 1 else None)
            ev_o = self.op("vector", lambda e: e.tensor_tensor(
                h1_t[:, :], P1[:, 0:1024], xn_t[:, :], ALU.add),
                wait=[("st", omm_c), ("dh%d" % (cc % 2), ldxn_c),
                      ("dn%d" % ((cc - 1) % 2), t_h1_sp.get(cc - 1))],
                inc=("sv", 1))
            t_xn_pv = ev_o
            t_h1_sp[cc] = self.op("gpsimd", lambda e, cc=cc: e.dma_start(
                self.h1_dram[ts(cc), :], h1_t[:, :]),
                wait=[("sv", ev_o)], inc=("dn%d" % (cc % 2), 16))

        def stageA(ch):
            par = ch % 2
            dbp = "db%d" % par
            svp, sap, stp = t_endB[ch - 2]
            W0 = ch * Q
            d = {}
            lt, lT = self.lam_tok[par], self.lamT[par]
            dk = self.dtk[par]
            gm_, wr_, dw_, pt_, le_ = (self.gam[par], self.wrow[par], self.dtw[par],
                                       self.ptb[par], self.lamend[par])
            tdt = transpose(0, self.dtf[:, W0:W0 + Q], dk[:, 0:NH], F32,
                            xtra_ev=[("sv", svp), ("sa", sap)])
            tda = transpose(1, self.dtAf[:, W0:W0 + Q], dk[:, NH:2 * NH], F32,
                            xtra_ev=[("sv", svp), ("sa", sap)])
            m1 = self.op("tensor", lambda e, dk=dk: e.matmul(
                slotA_f[:, 0:32], c["triu_f"][:, :], dk[:, NH:2 * NH],
                start=True, stop=True),
                wait=[("sv", tda), ("sv", svp), ("sv", t_slot[0])], inc=("st", 1))
            m2 = self.op("tensor", lambda e, dk=dk: e.matmul(
                slotB_f[0:32, 0:128], dk[:, NH:2 * NH], c["triu_f"][:, :],
                start=True, stop=True),
                wait=[("sv", t_slot[1])], inc=("st", 1))
            e1 = t_slot[0] = self.op("vector", lambda e, lt=lt: e.tensor_copy(
                lt[:, :], slotA_f[:, 0:32]),
                wait=[("st", m2), ("sa", sap)], inc=("sv", 1))
            e2 = t_slot[1] = self.op("vector", lambda e, lT=lT: e.tensor_copy(
                lT[:, 0:Q], slotB_f[0:32, 0:128]), inc=("sv", 1))
            m3 = self.op("tensor", lambda e, lT=lT: e.matmul(
                slotA_f[:, 0:32],
                bass.AP(lT, 127, [[128, 32], [0, 128]]),
                c["identf"][0:32, 0:32], start=True, stop=True),
                wait=[("sv", e2), ("sv", t_slot[0])], inc=("st", 1))
            e3 = t_slot[0] = self.op("vector", lambda e, le=le_: e.tensor_copy(
                le[:, :], slotA_f[:, 0:32]),
                wait=[("st", m3)], inc=("sv", 1))
            dtok = e3
            t_lch[ch] = e3
            vw = self.op("vector", lambda e, wr=wr_, le=le_, lt=lt: e.tensor_sub(
                wr[:, :], le[:, :], lt[:, :]),
                wait=[("sv", e3)], inc=("sv", 1))
            aw = self.op("scalar", lambda e, wr=wr_: e.activation(
                wr[:, :], wr[:, :], AF.Exp), wait=[("sv", vw)], inc=("sa", 1))
            ag = self.op("scalar", lambda e, gm=gm_, le=le_: e.activation(
                gm[:, :], le[:, :], AF.Exp),
                wait=[("sv", e3), ("sa", sap)], inc=("sa", 1))
            if not pre:
                ap_ = self.op("scalar", lambda e, pt=pt_, lt=lt: e.activation(
                    pt[:, 0:NH], lt[:, :], AF.Exp),
                    wait=[("sv", e1), ("sa", sap)], inc=("sa", 1))
                d["ap"] = ap_
            vdw = self.op("vector", lambda e, dw=dw_, dk=dk, wr=wr_: e.tensor_mul(
                dw[:, 0:NH], dk[:, 0:NH], wr[:, :]),
                wait=[("sa", aw), ("sv", tdt)], inc=("sv", 1))
            d.update(e1=e1, e2=e2, vw=vw, aw=aw, ag=ag, vdw=vdw, tdt=tdt, tda=tda,
                     dtok=dtok, par=par)
            return d

        def bcast_chain(chx):
            A = stA[chx]
            par = A["par"]
            dbp = "db%d" % par
            lT = self.lamT[par]
            rdhs = []
            dtok = t_dchain[chx - 2]
            for g in range(4):
                gq = chx * 4 + g
                wrh = self.op("sync", lambda e, lT=lT, g=g, par=par: e.dma_start(
                    bass.AP(self.lam_dram.tensor, par * 2048 + (g % 2) * 1024,
                            [[128, 8], [1, 128]]),
                    lT[g * 8:(g + 1) * 8, 0:Q]),
                    wait=[("sv", A["e2"]), (dbp, dtok)], inc=(dbp, 16))
                rdh = self.op("sync", lambda e, g=g, par=par: e.dma_start(
                    bc[g % 2][:, :],
                    bass.AP(self.lam_dram.tensor, par * 2048 + (g % 2) * 1024,
                            [[0, 128], [1, 1024]])),
                    wait=[(dbp, wrh), ("sq", max(0, gq - 1))],
                    inc=(dbp, 16))
                dtok = rdh
                rdhs.append(rdh)
            A["rdhs"] = rdhs
            t_dchain[chx] = dtok

        stA[0] = stageA(0)
        if not pre:
            bcast_chain(0)
        for ch in range(NCH):
            if ch + 1 < NCH:
                stA[ch + 1] = stageA(ch + 1)
                if not pre:
                    bcast_chain(ch + 1)
            A = stA[ch]
            par = ch % 2
            dbp = "db%d" % par
            dzp = "dz%d" % par
            dhp = "dh%d" % par
            dnp = "dn%d" % par
            W0 = ch * Q
            lt, dk = self.lam_tok[par], self.dtk[par]
            gm_, dw_, pt_ = self.gam[par], self.dtw[par], self.ptb[par]
            t_xtok = None
            for f in range(16):
                t_xtok = transpose(f, xbc[:, f * CW + 3 + W0:f * CW + 3 + W0 + Q],
                                   x_tok[:, f * 128:(f + 1) * 128], BF16,
                                   xtra_tr=[("sa", self.t_xbc)] if f == 0 else (),
                                   xtra_ev=[("st", t_smm),
                                            ("sg", t_tD.get(ch - 1))]
                                   if f == 0 else ())
            t_btok = transpose(1, xbc[:, 16 * CW + 3 + W0:16 * CW + 3 + W0 + Q],
                               B_tok[:, :], BF16)
            vxw = self.op("vector", lambda e, dw=dw_: e.tensor_tensor(
                xw[:, :], x_tok[:, :], exp64(dw), ALU.mult),
                wait=[("sv", A["vdw"]), ("sv", t_btok)], inc=("sv", 1))
            if not pre and (ch - 1) in pend_tail:
                emit_tail(ch - 1)
            if not pre:
                mg = self.op("tensor", lambda e, W0=W0: e.matmul(
                    slotB_f[:, 0:128], xbc[:, 16 * CW + 3 + W0:16 * CW + 3 + W0 + Q],
                    xbc[:, 17 * CW + 3 + W0:17 * CW + 3 + W0 + Q],
                    start=True, stop=True),
                    wait=[("sv", t_slot[1]), ("sa", self.t_xbc)],
                    inc=("st", 1))
                t_gm = t_slot[1] = self.op("vector", lambda e: e.tensor_tensor(
                    Gm[:, :], slotB_f[:, 0:128], c["triu_b"][:, :], ALU.mult),
                    wait=[("st", mg)], inc=("sv", 1))
                zmm = None
                for q in range(4):
                    zmm = self.op("tensor", lambda e, q=q, W0=W0: e.matmul(
                        P0[:, q * 512:(q + 1) * 512],
                        xbc[:, 17 * CW + 3 + W0:17 * CW + 3 + W0 + Q],
                        H_bf[:, q * 512:(q + 1) * 512], start=True, stop=True),
                        wait=([("sa", t_hbf), ("sv", t_p0v)] if q == 0 else []),
                        inc=("st", 1) if q == 3 else None)
                t_zmm = zmm
                ez = self.op("vector", lambda e, pt=pt_: e.tensor_tensor(
                    y_acc[:, :], P0[:, :], exp64(pt), ALU.mult),
                    wait=[("st", zmm), ("sa", A["ap"]), ("sv", t_gate)], inc=("sv", 1))
                t_ee = {}
                t_mth = {}
                t_vm = {}
                for hh in range(NH + 2):
                    if hh < NH:
                        h = hh
                        g = h // 8
                        gq = ch * 4 + g
                        rdh = A["rdhs"][g]
                        q3 = h % 3
                        lastg = h % 8 == 7
                        vd = self.op("vector", lambda e, h=h, g=g, q3=q3, lt=lt:
                                     e.tensor_scalar(
                                         est[q3][:, :],
                                         bc[g % 2][:, (h % 8) * 128:(h % 8 + 1) * 128],
                                         lt[:, h:h + 1], CLAMP, ALU.subtract, ALU.min),
                                     wait=[(dbp, rdh), ("sv", t_vm.get(h - 3))],
                                     inc=("sq", 1) if lastg else ("sv", 1))
                        t_ee[h] = self.op("scalar", lambda e, q3=q3: e.activation(
                            est[q3][:, :], est[q3][:, :], AF.Exp),
                            wait=[("sq", gq + 1) if lastg else ("sv", vd)],
                            inc=("sa", 1))
                    if hh >= 2:
                        h = hh - 2
                        q = h % 2
                        q3 = h % 3
                        t_vm[h] = vm = self.op("vector", lambda e, h=h, q=q, q3=q3, dk=dk:
                                     e.scalar_tensor_tensor(
                                         emt[q][:, :], est[q3][:, :], dk[:, h:h + 1],
                                         Gm[:, :], ALU.mult, ALU.mult),
                                     wait=[("sa", t_ee[h]), ("st", t_mth.get(h - 2))] +
                                          ([("sv", t_gm)] if h == 0 else []),
                                     inc=("sv", 1))
                        t_mth[h] = self.op("tensor", lambda e, h=h, q=q: e.matmul(
                            P0[:, h * 64:(h + 1) * 64], emt[q][:, :],
                            x_tok[:, h * 64:(h + 1) * 64], start=True, stop=True),
                            wait=[("sv", vm)] + ([("sv", ez), ("sv", t_xtok)]
                                                 if h == 0 else []),
                            inc=("st", 1))
                t_mt_last = t_mth[NH - 1]
                ev_y = self.op("vector", lambda e: e.tensor_tensor(
                    y_acc[:, :], y_acc[:, :], P0[:, :], ALU.add),
                    wait=[("st", t_mt_last), ("sv", ez)], inc=("sv", 1))
                tD = self.op("gpsimd", lambda e: e.tensor_tensor(
                    cscr[:, :], x_tok[:, :], c["dexp"][:, :], ALU.mult),
                    wait=[("sv", t_xtok), ("sa", t_endB[ch - 1][1])], inc=("sg", 1))
                t_tD[ch] = tD
                ev_y2 = self.op("vector", lambda e: e.tensor_tensor(
                    y_acc[:, :], y_acc[:, :], cscr[:, :], ALU.add),
                    wait=[("sv", ev_y), ("sg", tD)], inc=("sv", 1))
                t_ya = ev_y2
                if self.dev_phase is not None:
                    if "dev_y" not in self.dram:
                        self.dout("dev_y", [T, DIN])
                    tdmp = self.op("scalar", lambda e, ch=ch: e.dma_start(
                        self.dram["dev_y"][ts(ch), :], y_acc[:, :]),
                        wait=[("sv", ev_y2), ("so", self.cnt["so"])], inc=("so", 16))
                    t_ya = ("so", tdmp)
            else:
                t_ya = None
            smm = None
            for q in range(4):
                smm = self.op("tensor", lambda e, q=q: e.matmul(
                    P0[:, q * 512:(q + 1) * 512], B_tok[:, :],
                    xw[:, q * 512:(q + 1) * 512], start=True, stop=True),
                    wait=([("sv", vxw)] +
                          ([("sv", ev_y)] if not pre else [("sv", t_p0v)])
                          ) if q == 0 else [],
                    inc=("st", 1) if q == 3 else None)
            t_smm = smm
            c1 = self.op("vector", lambda e, gm=gm_: e.tensor_tensor(
                cscr[:, :], H[:, :], exp64(gm), ALU.mult),
                wait=[("sa", A["ag"]), ("sv", t_carry)] +
                     ([(t_ya if isinstance(t_ya, tuple) else ("sv", t_ya))]
                      if not pre else []), inc=("sv", 1))
            t_carry = self.op("vector", lambda e: e.tensor_tensor(
                H[:, :], cscr[:, :], P0[:, :], ALU.add),
                wait=[("st", smm), ("sv", c1)], inc=("sv", 1))
            t_p0v = t_carry
            if not pre:
                ldsz = self.op("gpsimd", lambda e, ch=ch: e.dma_start(
                    sz_t[:, :], self.sz_dram[ts(ch), :]),
                    wait=[("sv", t_gate)], inc=(dzp, 16))
                ldxn = self.op("scalar", lambda e, ch=ch: e.dma_start(
                    xn_t[:, :], self.xn_dram[ts(ch), :]),
                    wait=[("sv", t_xn_pv)], inc=(dhp, 16))
                g1 = self.op("vector", lambda e: e.tensor_tensor(
                    y_acc[:, :], y_acc[:, :], sz_t[:, :], ALU.mult),
                    wait=[(t_ya if isinstance(t_ya, tuple) else ("sv", t_ya)),
                          (dzp, ldsz)], inc=("sv", 1))
                t_gate = g1
                a1 = self.op("scalar", lambda e: e.activation(
                    cscr[:, :], y_acc[:, :], AF.Square, accum_out=ssq[:, :]),
                    wait=[("sv", g1), ("sv", t_carry)], inc=("sa", 1))
                a2 = self.op("scalar", lambda e: e.activation(
                    ssq[:, :], ssq[:, :], AF.Ln, bias=EPS_RMS, scale=1.0 / DIN),
                    wait=[("sa", a1)], inc=("sa", 1))
                a3 = self.op("scalar", lambda e: e.activation(
                    ssq[:, :], ssq[:, :], AF.Exp, scale=-0.5),
                    wait=[("sa", a2)], inc=("sa", 1))
                vyn = self.op("vector", lambda e: e.scalar_tensor_tensor(
                    yn[:, :], y_acc[:, :], ssq[:, :1], c["ssmw"][:, :],
                    ALU.mult, ALU.mult), wait=[("sa", a3)], inc=("sv", 1))
                t_ynt = None
                for f in range(16):
                    t_ynt = transpose(f, yn[:, f * 128:(f + 1) * 128],
                                      ynT[:, f * 128:(f + 1) * 128], BF16,
                                      xtra_tr=[("sv", vyn)] if f == 0 else ())
                pend_tail[ch] = (t_ynt, ldxn)
                if ch + 1 < NCH:
                    t_hbf = self.op("scalar", lambda e: e.activation(
                        H_bf[:, :], H[:, :], AF.Copy),
                        wait=[("sv", t_carry), ("st", t_zmm)], inc=("sa", 1))
            t_endB[ch] = (self.cnt["sv"], self.cnt["sa"], self.cnt["st"])
        if not pre and (NCH - 1) in pend_tail:
            emit_tail(NCH - 1)
        if pre:
            tm = self.op("vector", lambda e: e.tensor_scalar_mul(
                H[:, :], H[:, :], c["smask"][:, 0:1]),
                wait=[("sv", t_carry)], inc=("sv", 1))
            self.t_hspill = self.op("gpsimd", lambda e: e.dma_start(
                self.dram["hstate_dram"], H[:, :]), wait=[("sv", tm)], inc=("dp", 16))
        self.barrier()

    def phase_E(self):
        cv = self.carve
        c = self.c
        P0, P1, PT, PS = self.P0, self.P1, self.PT, self.PS
        wg1 = cv("wg1", OFF["R_C"], [128, NDC * INTER1], BF16)
        wu1 = cv("wu1", OFF["R_D"], [128, NDC * INTER1], BF16)
        wd1 = cv("wd1", OFF["R_D"] + 44032, [128, 22 * DM], BF16)
        h1t = [cv(f"E_h1{p}", OFF["R_B"] + p * 4096, [128, DM], F32)
               for p in range(2)]
        hn = [cv(f"E_hn{p}", OFF["R_B"] + 8192 + p * 2048, [128, DM], BF16)
              for p in range(2)]
        hnTt = [cv(f"E_hnT{p}", OFF["R_B"] + 12288 + p * 2048, [128, NDC * 128],
                   BF16) for p in range(2)]
        h2 = cv("E_h2", OFF["R_B"] + 16384, [128, INTER1], BF16)
        h2t = cv("E_h2t", OFF["R_B"] + 21888, [128, 22 * 128], BF16)
        sqscr = cv("E_sqs", OFF["dt2"], [128, DM], F32)
        ssqE = cv("E_ssq", OFF["devscr"], [128, 2], F32)
        EPARTS = [(0, 1024), (1024, 1024), (2048, 704)]
        wsem = ["dw", "dl0", "dl1", "dp", "dw", "dl0", "dl1", "dp"]
        t_wg, t_wu = {}, {}
        si = 0
        first = [("st", self.cnt["st"])]
        lastw = {}
        for pi, (cb, cn) in enumerate(EPARTS):
            for wdst, wsrc, td in ((wg1, self.w_g1, t_wg), (wu1, self.w_u1, t_wu)):
                s = wsem[si]
                w = first + ([(s, lastw[s])] if s in lastw else [])
                tok = self.op("gpsimd",
                    lambda e, wdst=wdst, wsrc=wsrc, cb=cb, cn=cn: e.dma_start(
                        bass.AP(wdst, cb, [[NDC * INTER1, 128], [INTER1, NDC], [1, cn]]),
                        bass.AP(wsrc.tensor, cb,
                                [[INTER1, 128], [128 * INTER1, NDC], [1, cn]])),
                    inc=(s, 16), wait=w)
                td[pi] = (s, tok)
                lastw[s] = tok
                first = []
                si = (si + 1) % len(wsem)
        t_wd = ("dl1", self.op("gpsimd", lambda e: e.dma_start(
            bass.AP(wd1, 0, [[22 * DM, 128], [DM, 21], [1, DM]]),
            bass.AP(self.w_d1.tensor, 0, [[DM, 128], [128 * DM, 21], [1, DM]])),
            inc=("dl1", 16), wait=[("dl1", lastw["dl1"])]))
        t_wdb = ("dp", self.op("gpsimd", lambda e: e.dma_start(
            bass.AP(wd1, 21 * DM, [[22 * DM, 64], [1, DM]]),
            bass.AP(self.w_d1.tensor, 21 * 128 * DM, [[DM, 64], [1, DM]])),
            inc=("dp", 16), wait=[("dp", lastw["dp"])]))
        slotA_b = P1[:, 640:704].bitcast(BF16)
        slotB_b = PT[:, 256:384]
        t_slot = {0: None, 1: None}

        def transpose(idx, src_ap, dst_ap, xtra_tr=(), xtra_ev=()):
            q = idx % 2
            pp = slotA_b if q == 0 else slotB_b
            np_ = src_ap.shape[0]
            nf = src_ap.shape[1]
            tt = self.op("tensor", lambda e, pp=pp, s=src_ap, np_=np_, nf=nf:
                         e.transpose(pp[0:nf, 0:np_], s, c["ident"][0:np_, 0:np_]),
                         wait=[("sv", t_slot[q])] + list(xtra_tr), inc=("st", 1))
            t_slot[q] = self.op("vector",
                                lambda e, pp=pp, d=dst_ap, nf=nf, np_=np_:
                                e.tensor_copy(d, pp[0:nf, 0:np_]),
                                wait=[("st", tt)] + list(xtra_ev), inc=("sv", 1))
            return t_slot[q]

        t_ldh = {}
        t_rms = {}
        t_vh = {}
        t_hnt = {}
        t_ttl = {}
        t_guT = {}
        t_h2tt = {}
        t_v1b = {}
        t_sp = {}

        def front(i):
            p = i % 2
            dhp = "dh%d" % p
            t_ldh[i] = self.op("sync", lambda e, i=i, p=p: e.dma_start(
                h1t[p][:, :], self.h1_dram[ts(i), :]),
                wait=[("dn%d" % p, t_sp.get(i - 2))], inc=(dhp, 16))
            a1 = self.op("scalar", lambda e, p=p: e.activation(
                sqscr[:, :], h1t[p][:, :], AF.Square, accum_out=ssqE[:, p:p + 1]),
                wait=[(dhp, t_ldh[i]), ("sa", t_rms.get(i - 1))], inc=("sa", 1))
            a2 = self.op("scalar", lambda e, p=p: e.activation(
                ssqE[:, p:p + 1], ssqE[:, p:p + 1], AF.Ln, bias=EPS_RMS,
                scale=1.0 / DM), wait=[("sa", a1)], inc=("sa", 1))
            t_rms[i] = a3 = self.op("scalar", lambda e, p=p: e.activation(
                ssqE[:, p:p + 1], ssqE[:, p:p + 1], AF.Exp, scale=-0.5),
                wait=[("sa", a2)], inc=("sa", 1))
            t_vh[i] = self.op("vector", lambda e, p=p: e.scalar_tensor_tensor(
                hn[p][:, :], h1t[p][:, :], ssqE[:, p:p + 1], c["postw"][:, :],
                ALU.mult, ALU.mult),
                wait=[("sa", a3), ("st", t_ttl.get(i - 2))], inc=("sv", 1))
            for dc in range(NDC):
                t_hnt[i] = transpose(
                    dc, hn[p][:, ts(dc)], hnTt[p][:, ts(dc)],
                    xtra_tr=[("sv", t_vh[i])] if dc == 0 else (),
                    xtra_ev=[("st", t_guT.get(i - 2))] if dc == 0 else ())
            t_ttl[i] = self.cnt["st"]

        parts = [(0, 1024), (1024, 1024), (2048, 704)]
        slots = [P0[:, 0:1024], P0[:, 1024:2048]]
        t_silu_u = {}
        t_spu = {}
        t_dn = {}
        uu = 0
        front(0)
        for i in range(NTIL):
            p = i % 2
            dhp = "dh%d" % p
            for pi, (cb, cn) in enumerate(parts):
                for j, w in ((0, wg1), (1, wu1)):
                    mm = None
                    for dc in range(NDC):
                        mm = self.op("tensor", lambda e, dc=dc, cb=cb, cn=cn,
                                     j=j, w=w, p=p: ([
                            e.matmul(slots[j][:, 0:512],
                                     hnTt[p][:, dc * 128:(dc + 1) * 128],
                                     w[:, dc * INTER1 + cb:dc * INTER1 + cb + min(cn, 512)],
                                     start=(dc == 0), stop=(dc == NDC - 1))] +
                            ([e.matmul(slots[j][:, 512:512 + cn - 512],
                                       hnTt[p][:, dc * 128:(dc + 1) * 128],
                                       w[:, dc * INTER1 + cb + 512:dc * INTER1 + cb + cn],
                                       start=(dc == 0), stop=(dc == NDC - 1))]
                             if cn > 512 else []))[-1],
                            wait=(([("sa", t_silu_u.get(uu - 1)), ("sv", t_hnt[i])]
                                   if j == 0 else [("sv", t_spu.get(uu - 1))]) +
                                  ([(t_wg if j == 0 else t_wu)[pi]] if i == 0 else [])
                                  if dc == 0 else []),
                            inc=("st", 1) if dc == NDC - 1 else None)
                mmtok = self.cnt["st"]
                t_silu_u[uu] = self.op(
                    "scalar", lambda e, cb=cb, cn=cn: e.activation(
                        h2[:, cb:cb + cn], slots[0][:, 0:cn], AF.Silu),
                    wait=[("st", mmtok)] +
                         ([("st", t_h2tt.get(i - 1))] if pi == 0 else []),
                    inc=("sa", 1))
                t_spu[uu] = self.op(
                    "vector", lambda e, cb=cb, cn=cn: e.tensor_tensor(
                        h2[:, cb:cb + cn], h2[:, cb:cb + cn], slots[1][:, 0:cn],
                        ALU.mult),
                    wait=[("sa", t_silu_u[uu])], inc=("sv", 1))
                uu += 1
            t_guT[i] = self.cnt["st"]
            t_h2full = t_spu[uu - 1]
            if i + 1 < NTIL:
                front(i + 1)
            t_tr = None
            for k in range(22):
                ncols = 128 if k < 21 else 64
                t_tr = transpose(k, h2[:, k * 128:k * 128 + ncols],
                                 h2t[0:ncols, k * 128:(k + 1) * 128],
                                 xtra_tr=[("sv", t_h2full)] if k == 0 else ())
            t_h2tt[i] = self.cnt["st"]
            mm = None
            for k in range(22):
                kp = 128 if k < 21 else 64
                for q, pb in ((0, P1), (1, PS)):
                    mm = self.op("tensor", lambda e, k=k, q=q, pb=pb, kp=kp: e.matmul(
                        pb[:, 0:512], h2t[0:kp, k * 128:(k + 1) * 128],
                        wd1[0:kp, k * DM + q * 512:k * DM + (q + 1) * 512],
                        start=(k == 0), stop=(k == 21)),
                        wait=([("sv", t_tr), ("sv", t_v1b.get(i - 1))] +
                              ([t_wd, t_wdb] if i == 0 else [])
                              if k == 0 and q == 0 else []),
                        inc=("st", 1) if k == 21 and q == 1 else None)
            t_dn[i] = mm
            v1a = self.op("vector", lambda e, p=p: e.tensor_tensor(
                h1t[p][:, 0:512], h1t[p][:, 0:512], P1[:, 0:512], ALU.add),
                wait=[("st", mm), (dhp, t_ldh[i])], inc=("sv", 1))
            t_v1b[i] = self.op("vector", lambda e, p=p: e.tensor_tensor(
                h1t[p][:, 512:1024], h1t[p][:, 512:1024], PS[:, 0:512], ALU.add),
                wait=[("sv", v1a)], inc=("sv", 1))
            t_sp[i] = self.op("gpsimd", lambda e, i=i, p=p: e.dma_start(
                self.h1_dram[ts(i), :], h1t[p][:, :]),
                wait=[("sv", t_v1b[i])], inc=("dn%d" % p, 16))
        self.barrier()

    def phase_F(self):
        cv = self.carve
        c = self.c
        P0, P1, PT, PS = self.P0, self.P1, self.PT, self.PS
        SCL = 1.0 / 127.0
        wg2 = cv("wg2", OFF["R_C"], [128, NDC * INTER2], BF16)
        wu2 = cv("wu2", OFF["R_D"], [128, NDC * INTER2], BF16)
        wd2 = cv("wd2", OFF["R_D"] + 49152, [128, 21 * DM], BF16)
        wd2b = cv("wd2b", OFF["dt2"], [128, 3 * DM], BF16)
        xft = [cv(f"F_xf{p}", OFF["c_ssmw"] + p * 2048, [128, NDC * 128], BF16)
               for p in range(2)]
        h1t = [cv(f"F_h1{p}", OFF["R_B"] + p * 4096, [128, DM], F32)
               for p in range(2)]
        xt = [cv(f"F_x{p}", OFF["R_B"] + 8192 + p * 4096, [128, DM], F32)
              for p in range(2)]
        h3 = cv("F_h3", OFF["R_B"] + 16384, [128, INTER2], BF16)
        h3t = cv("F_h3t", OFF["R_B"] + 22528, [128, 24 * 128], BF16)
        wsem = ["dw", "dl0", "dl1", "dp", "dw", "dl0", "dl1", "dp"]
        t_wg, t_wu = {}, {}
        si = 0
        first = [("st", self.cnt["st"])]
        lastw = {}
        for pi in range(3):
            cb = pi * 1024
            for wdst, wsrc, td in ((wg2, self.w_g2, t_wg), (wu2, self.w_u2, t_wu)):
                s = wsem[si]
                w = first + ([(s, lastw[s])] if s in lastw else [])
                tok = self.op("gpsimd",
                    lambda e, wdst=wdst, wsrc=wsrc, cb=cb: e.dma_start(
                        bass.AP(wdst, cb, [[NDC * INTER2, 128], [INTER2, NDC], [1, 1024]]),
                        bass.AP(wsrc.tensor, cb,
                                [[INTER2, 128], [128 * INTER2, NDC], [1, 1024]])),
                    inc=(s, 16), wait=w)
                td[pi] = (s, tok)
                lastw[s] = tok
                first = []
                si = (si + 1) % len(wsem)
        t_wd = ("dl1", self.op("gpsimd", lambda e: e.dma_start(
            bass.AP(wd2, 0, [[21 * DM, 128], [DM, 21], [1, DM]]),
            bass.AP(self.w_d2.tensor, 0, [[DM, 128], [128 * DM, 21], [1, DM]])),
            inc=("dl1", 16), wait=[("dl1", lastw["dl1"])]))
        t_wdb = ("dp", self.op("gpsimd", lambda e: e.dma_start(
            bass.AP(wd2b, 0, [[3 * DM, 128], [DM, 3], [1, DM]]),
            bass.AP(self.w_d2.tensor, 21 * 128 * DM, [[DM, 128], [128 * DM, 3], [1, DM]])),
            inc=("dp", 16), wait=[("dp", lastw["dp"])]))
        slotA_b = P1[:, 640:704].bitcast(BF16)
        slotB_b = PT[:, 256:384]
        t_slot = {0: None, 1: None}

        def transpose(idx, src_ap, dst_ap, xtra_tr=(), xtra_ev=()):
            q = idx % 2
            pp = slotA_b if q == 0 else slotB_b
            np_ = src_ap.shape[0]
            nf = src_ap.shape[1]
            tt = self.op("tensor", lambda e, pp=pp, s=src_ap, np_=np_:
                         e.transpose(pp[:, 0:np_], s, c["ident"][0:np_, 0:np_]),
                         wait=[("sv", t_slot[q])] + list(xtra_tr), inc=("st", 1))
            t_slot[q] = self.op("vector",
                                lambda e, pp=pp, d=dst_ap, nf=nf, np_=np_:
                                e.tensor_copy(d, pp[0:nf, 0:np_]),
                                wait=[("st", tt)] + list(xtra_ev), inc=("sv", 1))
            return t_slot[q]

        parts = [(0, 1024), (1024, 1024), (2048, 1024)]
        slots = [P0[:, 0:1024], P0[:, 1024:2048]]
        t_sil = {}
        t_silu_u = {}
        t_spu = {}
        t_ldx = {}
        t_guT = {}
        t_trlast = {}
        t_dnlast = {}
        t_v1b = {}
        t_out = {}
        t_ost = {}
        uu = 0
        t_ldx[0] = self.op("sync", lambda e: e.dma_start(
            xft[0][:, :], bass.AP(self.xf_dram.tensor, 0,
                                  [[NDC * T, 128], [T, NDC], [1, 128]])),
            wait=[("st", self.cnt["st"])], inc=("de0", 16))
        for i in range(NTIL):
            p = i % 2
            dhp = "dh%d" % p
            if i + 1 < NTIL:
                t_ldx[i + 1] = self.op("sync", lambda e, i=i: e.dma_start(
                    xft[(i + 1) % 2][:, :],
                    bass.AP(self.xf_dram.tensor, (i + 1) * 128,
                            [[NDC * T, 128], [T, NDC], [1, 128]])),
                    wait=[("st", t_guT.get(i - 1))], inc=("de%d" % ((i + 1) % 2), 16))
            ldh = self.op("sync", lambda e, i=i, p=p: e.dma_start(
                h1t[p][:, :], self.h1_dram[ts(i), :]),
                wait=[("so" if p == 0 else "ad", t_ost.get(i - 2))], inc=(dhp, 16))
            ldxo = self.op("sync", lambda e, i=i, p=p: e.dma_start(
                xt[p][:, :], self.x_own[ts(i), :]),
                wait=[("sv", t_out.get(i - 2))], inc=(dhp, 16))
            for pi, (cb, cn) in enumerate(parts):
                for j, w in ((0, wg2), (1, wu2)):
                    mm = None
                    for dc in range(NDC):
                        mm = self.op("tensor", lambda e, i=i, dc=dc, cb=cb, j=j,
                                     w=w, p=p: [
                            e.matmul(slots[j][:, 0:512],
                                     xft[p][:, dc * 128:(dc + 1) * 128],
                                     w[:, dc * INTER2 + cb:dc * INTER2 + cb + 512],
                                     start=(dc == 0), stop=(dc == NDC - 1)),
                            e.matmul(slots[j][:, 512:1024],
                                     xft[p][:, dc * 128:(dc + 1) * 128],
                                     w[:, dc * INTER2 + cb + 512:dc * INTER2 + cb + 1024],
                                     start=(dc == 0), stop=(dc == NDC - 1))][-1],
                            wait=(([("sa", t_silu_u.get(uu - 1)),
                                    ("de%d" % p, t_ldx[i])] if j == 0 else
                                   [("sv", t_spu.get(uu - 1))]) +
                                  ([(t_wg if j == 0 else t_wu)[pi]] if i == 0 else [])
                                  if dc == 0 else []),
                            inc=("st", 1) if dc == NDC - 1 else None)
                mmtok = self.cnt["st"]
                t_silu_u[uu] = t_sil[(i, pi)] = self.op(
                    "scalar", lambda e, cb=cb, cn=cn: e.activation(
                        h3[:, cb:cb + cn], slots[0][:, 0:cn], AF.Silu, scale=SCL),
                    wait=[("st", mmtok)] +
                         ([("st", t_trlast.get(i - 1))] if pi == 0 else []),
                    inc=("sa", 1))
                t_spu[uu] = self.op(
                    "vector", lambda e, cb=cb, cn=cn: e.tensor_tensor(
                        h3[:, cb:cb + cn], h3[:, cb:cb + cn], slots[1][:, 0:cn],
                        ALU.mult),
                    wait=[("sa", t_sil[(i, pi)])], inc=("sv", 1))
                uu += 1
            t_guT[i] = self.cnt["st"]
            t_h3full = t_spu[uu - 1]
            t_tr = None
            for k in range(24):
                t_tr = transpose(k, h3[:, k * 128:(k + 1) * 128],
                                 h3t[:, k * 128:(k + 1) * 128],
                                 xtra_tr=[("sv", t_h3full)] if k == 0 else ())
            t_trlast[i] = self.cnt["st"]
            mm = None
            for k in range(24):
                for q, pb in ((0, P1), (1, PS)):
                    wsrc = (lambda kk: (wd2, kk) if kk < 21 else (wd2b, kk - 21))(k)
                    mm = self.op("tensor", lambda e, k=k, q=q, pb=pb,
                                 wt=wsrc[0], kk=wsrc[1]: e.matmul(
                        pb[:, 0:512], h3t[:, k * 128:(k + 1) * 128],
                        wt[:, kk * DM + q * 512:kk * DM + (q + 1) * 512],
                        start=(k == 0), stop=(k == 23)),
                        wait=([("sv", t_tr), ("sv", t_v1b.get(i - 1))] +
                              ([t_wd, t_wdb] if i == 0 else [])
                              if k == 0 and q == 0 else []),
                        inc=("st", 1) if k == 23 and q == 1 else None)
            t_dnlast[i] = mm
            v1a = self.op("vector", lambda e, p=p: e.scalar_tensor_tensor(
                h1t[p][:, 0:512], P1[:, 0:512], SCL * SCL, h1t[p][:, 0:512],
                ALU.mult, ALU.add),
                wait=[("st", mm), (dhp, ldxo)], inc=("sv", 1))
            t_v1b[i] = self.op("vector", lambda e, p=p: e.scalar_tensor_tensor(
                h1t[p][:, 512:1024], PS[:, 0:512], SCL * SCL, h1t[p][:, 512:1024],
                ALU.mult, ALU.add),
                wait=[("sv", v1a)], inc=("sv", 1))
            t_out[i] = self.op("vector", lambda e, p=p: e.tensor_tensor(
                h1t[p][:, :], h1t[p][:, :], xt[p][:, :], ALU.add),
                wait=[("sv", t_v1b[i]), (dhp, ldxo)], inc=("sv", 1))
            t_ost[i] = self.op("sync", lambda e, i=i, p=p: e.dma_start(
                self.out[ts(i), :], h1t[p][:, :]),
                wait=[("sv", t_out[i])], inc=("so" if p == 0 else "ad", 16))
        self.wait("sync", "so", self.cnt["so"])
        self.wait("sync", "ad", self.cnt["ad"])
        self.wait("gpsimd", "so", self.cnt["so"])
        self.wait("gpsimd", "ad", self.cnt["ad"])

    def dev_dump_B(self, tag):
        d1 = self.dout(f"dev_xbc_{tag}", [128, NCC * CW], BF16)
        d2 = self.dout(f"dev_dtf_{tag}", [32, T])
        d3 = self.dout(f"dev_dtA_{tag}", [32, T])
        self.barrier()
        self.op("scalar", lambda e: e.dma_start(d1, self.xbc[:, :]), inc=("so", 16))
        self.op("scalar", lambda e: e.dma_start(d2, self.dtf[:, :]), inc=("so", 16))
        self.op("scalar", lambda e: e.dma_start(d3, self.dtAf[:, :]), inc=("so", 16))
        self.wait("scalar", "so", self.cnt["so"])
        self.barrier()


def prep_inputs(full, core):
    b, h = core // 2, core % 2
    x = np.asarray(full["L0_attn_hidden_states"])
    own = np.ascontiguousarray(x[b, h * T:(h + 1) * T, :], dtype=np.float32)
    pre = (np.ascontiguousarray(x[b, 0:T, :], dtype=np.float32) if h == 1
           else np.zeros((T, DM), np.float32))
    m = {
        "x_own": own, "x_pre": pre,
        "w_in": np.ascontiguousarray(np.asarray(full["L0_attn_in_proj_w"]).T),
        "w_out": np.ascontiguousarray(np.asarray(full["L0_attn_out_proj_w"]).T),
        "w_g1": np.ascontiguousarray(np.asarray(full["L0_attn_gate_w"]).T),
        "w_u1": np.ascontiguousarray(np.asarray(full["L0_attn_up_w"]).T),
        "w_d1": np.ascontiguousarray(np.asarray(full["L0_attn_down_w"]).T),
        "w_g2": np.ascontiguousarray(np.asarray(full["L0_ffn_w_gate"]).T),
        "w_u2": np.ascontiguousarray(np.asarray(full["L0_ffn_w_up"]).T),
        "w_d2": np.ascontiguousarray(np.asarray(full["L0_ffn_w_down"]).T),
        "conv_w": np.ascontiguousarray(
            np.asarray(full["L0_attn_conv_w"]).reshape(NCC, 128, 4)
            .transpose(1, 0, 2).reshape(128, NCC * 4)),
        "conv_b": np.ascontiguousarray(
            np.asarray(full["L0_attn_conv_b"]).reshape(NCC, 128).T),
        "ssm_w_bc": np.broadcast_to(np.asarray(full["L0_attn_ssm_norm_w"])[None, :],
                                    (128, DIN)).copy(),
        "post_w_bc": np.broadcast_to(np.asarray(full["L0_attn_post_norm_w"])[None, :],
                                     (128, DM)).copy(),
        "d_exp_bc": np.broadcast_to(np.repeat(np.asarray(full["L0_attn_D"]), HPD)[None, :],
                                    (128, DIN)).copy(),
        "dtb_bc": np.ascontiguousarray(np.asarray(full["L0_attn_dt_bias"], np.float32)[:, None]),
        "alog_bc": np.ascontiguousarray(np.asarray(full["L0_attn_A_log"], np.float32)[:, None]),
        "triu_f": np.triu(np.ones((128, 128), np.float32)),
        "triu_b": np.triu(np.ones((128, 128), np.float32)).astype(ml_dtypes.bfloat16),
        "ident_b": np.eye(128, dtype=np.float32).astype(ml_dtypes.bfloat16),
        "ident_f": np.eye(128, dtype=np.float32),
        "fconst": np.broadcast_to(np.asarray(FCONST_VALS, np.float32)[None, :], (128, 8)).copy(),
        "smask": np.full((128, 1), 1.0 if h == 1 else 0.0, np.float32),
        "nwcol": np.ascontiguousarray(
            np.asarray(full["L0_attn_norm_w"]).reshape(NDC, 128).T.astype(np.float32)),
    }
    return m


def build(dev_phase=None):
    b = B(dev_phase)
    b.declare_io()
    b.load_consts()
    b.phase_A()
    if dev_phase == "A":
        pass
    elif dev_phase in ("Bmm", "Bconv"):
        b.phase_B(1, b_stop=dev_phase[1:])
    else:
        b.phase_B(1)
        if dev_phase == "Bp":
            b.dev_dump_B("p")
        elif dev_phase == "B":
            b.phase_B(0)
            b.dev_dump_B("m")
        else:
            b.phase_C(1)
            b.phase_B(0)
            b.phase_C(0)
            if dev_phase in (None, "full"):
                b.phase_E()
                b.phase_F()
            if dev_phase == "C":
                b.barrier()
                b.op("scalar", lambda e: e.dma_start(b.out, b.dram["h1_dram"]),
                     inc=("so", 16))
                b.wait("scalar", "so", b.cnt["so"])
    if dev_phase is not None and dev_phase != "full":
        b.barrier()
        b.op("scalar", lambda e: e.dma_start(b.out, b.dram["xn_dram"]), inc=("so", 16))
        b.wait("scalar", "so", b.cnt["so"])
    return b.finalize()


def kernel(**inputs):
    nc = build(None)
    in_maps = [prep_inputs(inputs, c) for c in range(8)]
    res = run_bass_kernel_spmd(nc, in_maps, core_ids=list(range(8)))
    y = np.empty((4, 4096, DM), np.float32)
    for c in range(8):
        b_, h_ = c // 2, c % 2
        y[b_, h_ * T:(h_ + 1) * T, :] = np.asarray(res.results[c]["out"])
    return y


# version 16
# speedup vs baseline: 1.0254x; 1.0085x over previous
import contextlib
import numpy as np
import ml_dtypes
import concourse.bass as bass
import concourse.mybir as mybir
from concourse.bass_utils import run_bass_kernel_spmd

F32 = mybir.dt.float32
BF16 = mybir.dt.bfloat16
I32 = mybir.dt.int32
AF = mybir.ActivationFunctionType
ALU = mybir.AluOpType
AX = mybir.AxisListType

T = 2048
NTIL = T // 128
Q = 128
NCH = T // Q
NH, HPD, DS = 32, 64, 128
DM, DIN, CONV_DIM, DIP = 1024, 2048, 2304, 4384
NDC = DM // 128
NCC = CONV_DIM // 128
INTER1, INTER2 = 2752, 3072
EPS_LN, EPS_RMS = 1e-5, 1e-6
CLAMP = 30.0
CW = 2052
FCONST_VALS = [1e-5, 1e-6, 1.0, 30.0, 0.0, 0.5, -1.0, 2.0]

ENGINES = ("sync", "tensor", "vector", "scalar", "gpsimd")

OFF = {}
def _mk_off():
    o = 0
    def put(name, nbytes):
        nonlocal o
        OFF[name] = o
        o += (nbytes + 31) // 32 * 32
    put("c_ssmw", 4096); put("c_postw", 2048); put("c_dexp", 4096)
    put("c_dtb", 128); put("c_A", 128); put("c_triu_f", 512)
    put("c_triu_b", 256); put("c_ident", 256); put("c_identf", 512)
    put("c_smask", 32); put("c_fconst", 32); put("c_convw", 320)
    put("c_convb", 96); put("r1col", 128); put("rstdln", 128)
    put("meanb", 128); put("nwcol", 32)
    put("dt2", 16384)
    put("pads", 128); put("lam_tok", 256); put("lamT", 1024)
    put("gam", 256); put("wrow", 256); put("dtw", 256); put("pt", 256)
    put("dtk", 512); put("lamend", 256); put("small2", 512); put("devscr", 32)
    for name, sz in [("R_C", 32768), ("R_A", 16384), ("R_B", 32768),
                     ("R_D", 18944), ("R_E", 73872)]:
        put(name, sz)
    OFF["END"] = o
    assert o <= 208600, o
_mk_off()


def ts(i, n=128):
    return slice(i * n, (i + 1) * n)


class B:
    def __init__(self, dev_phase=None):
        self.dev_phase = dev_phase
        self.nc = bass.Bass("TRN2", target_bir_lowering=False)
        self.es = contextlib.ExitStack()
        self.emits = {e: [] for e in ENGINES}
        self.sems = {}
        self.cnt = {}
        self.dram = {}
        for s in ("sd", "gd", "ad", "sv", "sa", "st", "sg", "so",
                  "dx0", "dx1", "dn0", "dn1", "dr", "dw", "db0", "db1",
                  "dz0", "dz1", "dh0", "dh1", "de0", "de1", "dl0", "dl1", "dp", "sq"):
            self.sem(s)
        self.arena = self.es.enter_context(
            self.nc.sbuf_tensor("ARENA", [128, OFF["END"]], mybir.dt.uint8))
        self.abase = self.nc.lookup_mloc(self.arena).addr
        self._carved = {}
        self.P0 = self.es.enter_context(self.nc.psum_tensor("P0", [128, 2048], F32))
        self.P1 = self.es.enter_context(self.nc.psum_tensor("P1", [128, 1024], F32))
        self.PT = self.es.enter_context(self.nc.psum_tensor("PT", [128, 512], BF16))
        self.PS = self.es.enter_context(self.nc.psum_tensor("PS", [128, 512], F32))

    def carve(self, name, off_bytes, shape, dtype):
        key = name
        if key not in self._carved:
            self._carved[key] = (self.nc.alloc_sbuf_tensor_at(
                name, shape, dtype, offset=self.abase + off_bytes), off_bytes)
        h, ob = self._carved[key]
        assert ob == off_bytes
        return h

    def sem(self, name):
        self.sems[name] = self.es.enter_context(self.nc.semaphore(name))
        self.cnt[name] = 0

    def op(self, eng, fn, inc=None, wait=None):
        tok = None
        if inc is not None:
            self.cnt[inc[0]] += inc[1]
            tok = self.cnt[inc[0]]
        waits = [(s, v) for (s, v) in (wait or []) if v]
        sems = self.sems

        def run(e, fn=fn, waits=waits, inc=inc):
            for sn, v in waits:
                e.wait_ge(sems[sn], v)
            inst = fn(e)
            if inc is not None and inst is not None:
                inst.then_inc(sems[inc[0]], inc[1])
        self.emits[eng].append(run)
        return tok

    def wait(self, eng, semname, val):
        if val:
            sems = self.sems
            self.emits[eng].append(lambda e, sn=semname, v=val: e.wait_ge(sems[sn], v))

    def barrier(self):
        snap = {s: self.cnt[s] for s in self.cnt if s != "so"}
        for e in ENGINES:
            for s, v in snap.items():
                self.wait(e, s, v)

    def din(self, name, shape, dtype=F32):
        self.dram[name] = self.nc.dram_tensor(name, shape, dtype, kind="ExternalInput").ap()
        return self.dram[name]

    def dout(self, name, shape, dtype=F32):
        self.dram[name] = self.nc.dram_tensor(name, shape, dtype, kind="ExternalOutput").ap()
        return self.dram[name]

    def dscratch(self, name, shape, dtype=F32):
        kind = "ExternalOutput" if self.dev_phase is not None else "Internal"
        self.dram[name] = self.nc.dram_tensor(name, shape, dtype, kind=kind).ap()
        return self.dram[name]

    def finalize(self):
        with self.nc.Block() as block:
            for e in ENGINES:
                fns = self.emits[e]
                getattr(block, e)(lambda eng, fns=fns: [f(eng) for f in fns])
        self.es.close()
        return self.nc

    def declare_io(self):
        d = self.din
        self.x_own = d("x_own", [T, DM])
        self.x_pre = d("x_pre", [T, DM])
        self.w_in = d("w_in", [DM, DIP])
        self.w_out = d("w_out", [DIN, DM])
        self.w_g1 = d("w_g1", [DM, INTER1])
        self.w_u1 = d("w_u1", [DM, INTER1])
        self.w_d1 = d("w_d1", [INTER1, DM])
        self.w_g2 = d("w_g2", [DM, INTER2], I32)
        self.w_u2 = d("w_u2", [DM, INTER2], I32)
        self.w_d2 = d("w_d2", [INTER2, DM], I32)
        self.conv_w = d("conv_w", [128, NCC * 4])
        self.conv_b = d("conv_b", [128, NCC])
        self.ssm_w_bc = d("ssm_w_bc", [128, DIN])
        self.post_w_bc = d("post_w_bc", [128, DM])
        self.d_exp_bc = d("d_exp_bc", [128, DIN])
        self.dtb_bc = d("dtb_bc", [32, 1])
        self.alog_bc = d("alog_bc", [32, 1])
        self.triu_f = d("triu_f", [128, 128])
        self.triu_b = d("triu_b", [128, 128], BF16)
        self.ident_b = d("ident_b", [128, 128], BF16)
        self.ident_f = d("ident_f", [128, 128])
        self.fconst = d("fconst", [128, 8])
        self.smask = d("smask", [128, 1])
        self.nwcol_h = d("nwcol", [128, NDC])
        self.out = self.dout("out", [T, DM])
        ds = self.dscratch
        self.xn_dram = ds("xn_dram", [T, DM])
        self.sz_dram = ds("sz_dram", [T, DIN], BF16)
        self.xf_dram = ds("xf_dram", [128, NDC * T], BF16)
        self.xfp_dram = ds("xfp_dram", [128, NDC * T], BF16)
        self.h1_dram = ds("h1_dram", [T, DM])
        self.h2t_dram = ds("h2t_dram", [128, 22 * T], BF16)
        self.h3t_dram = ds("h3t_dram", [128, 24 * T], BF16)
        self.hstate_dram = ds("hstate_dram", [128, DIN])
        self.row_dram = ds("row_dram", [1, 2 * T])
        self.lam_dram = ds("lam_dram", [1, NH * Q + 64])

    def load_consts(self):
        c = {}
        cv = self.carve
        c["ssmw"] = cv("c_ssmw", OFF["c_ssmw"], [128, DIN], BF16)
        c["postw"] = cv("c_postw", OFF["c_postw"], [128, DM], BF16)
        c["dexp"] = cv("c_dexp", OFF["c_dexp"], [128, DIN], BF16)
        c["dtb"] = cv("c_dtb", OFF["c_dtb"], [32, 1], F32)
        c["A"] = cv("c_A", OFF["c_A"], [32, 1], F32)
        c["triu_f"] = cv("c_triu_f", OFF["c_triu_f"], [128, 128], F32)
        c["triu_b"] = cv("c_triu_b", OFF["c_triu_b"], [128, 128], BF16)
        c["ident"] = cv("c_ident", OFF["c_ident"], [128, 128], BF16)
        c["identf"] = cv("c_identf", OFF["c_identf"], [128, 128], F32)
        c["smask"] = cv("c_smask", OFF["c_smask"], [128, 1], F32)
        c["fconst"] = cv("c_fconst", OFF["c_fconst"], [128, 8], F32)
        c["convw"] = cv("c_convw", OFF["c_convw"], [128, NCC * 4], F32)
        c["convb"] = cv("c_convb", OFF["c_convb"], [128, NCC], F32)
        self.c = c
        for j, v in enumerate(FCONST_VALS):
            self.nc.const_aps.register(F32, v, c["fconst"][:, j:j + 1])
        self.r1col = cv("r1col", OFF["r1col"], [128, 2 * NTIL], F32)
        self.rstdln = cv("rstdln", OFF["rstdln"], [128, 2 * NTIL], F32)
        self.meanb = cv("meanb", OFF["meanb"], [128, 2 * NTIL], F32)
        self.nwcol = cv("nwcol", OFF["nwcol"], [128, NDC], F32)
        self.dt2 = cv("dt2", OFF["dt2"], [32, 2 * T], F32)
        self.dtf = self.dt2[:, 0:T]
        self.dtAf = self.dt2[:, T:2 * T]
        self.dtscr_h = None
        self.pads_sb = cv("pads", OFF["pads"], [128, NCC * 3], BF16)
        self.lam_tok = [cv(f"lam_tok{p}", OFF["lam_tok"] + p * 128, [128, NH], F32)
                        for p in range(2)]
        self.lamT = [cv(f"lamT{p}", OFF["lamT"] + p * 512, [32, Q], F32)
                     for p in range(2)]
        self.gam = [cv(f"gam{p}", OFF["gam"] + p * 128, [128, NH], F32)
                    for p in range(2)]
        self.wrow = [cv(f"wrow{p}", OFF["wrow"] + p * 128, [128, NH], F32)
                     for p in range(2)]
        self.dtw = [cv(f"dtw{p}", OFF["dtw"] + p * 128, [128, NH], F32)
                    for p in range(2)]
        self.ptb = [cv(f"pt{p}", OFF["pt"] + p * 128, [128, NH], F32)
                    for p in range(2)]
        self.dtk = [cv(f"dtk{p}", OFF["dtk"] + p * 256, [128, 2 * NH], F32)
                    for p in range(2)]
        self.lamend = [cv(f"lamend{p}", OFF["lamend"] + p * 128, [128, NH], F32)
                       for p in range(2)]
        self.devscr = cv("devscr", OFF["devscr"], [128, 1], F32)
        pairs = [("ssmw", self.ssm_w_bc), ("postw", self.post_w_bc),
                 ("dexp", self.d_exp_bc), ("dtb", self.dtb_bc),
                 ("triu_f", self.triu_f), ("triu_b", self.triu_b),
                 ("ident", self.ident_b), ("identf", self.ident_f),
                 ("smask", self.smask), ("fconst", self.fconst),
                 ("A", self.alog_bc)]
        for key, src in pairs:
            kind = "gpsimd" if (c[key].dtype == BF16 and src.dtype == F32) else "sync"
            q = "gd" if kind == "gpsimd" else "sd"
            self.op(kind, lambda e, dd=c[key], s=src: e.dma_start(dd[:, :], s),
                    inc=(q, 16))
        self.op("sync", lambda e: e.dma_start(c["convw"][:, :], self.conv_w), inc=("sd", 16))
        self.op("sync", lambda e: e.dma_start(c["convb"][:, :], self.conv_b), inc=("sd", 16))
        self.op("sync", lambda e: e.dma_start(self.nwcol[:, :], self.nwcol_h), inc=("sd", 16))
        t_sd, t_gd = self.cnt["sd"], self.cnt["gd"]
        ta = self.op("scalar", lambda e: e.activation(c["A"][:, :], c["A"][:, :], AF.Exp),
                     wait=[("sd", t_sd), ("gd", t_gd)], inc=("sa", 1))
        self.op("vector", lambda e: e.tensor_scalar_mul(c["A"][:, :], c["A"][:, :], -1.0),
                wait=[("sa", ta)], inc=("sv", 1))
        self.barrier()

    def phase_A(self):
        cv = self.carve
        self.xf = cv("xf", OFF["R_B"], [128, NDC * T], BF16)
        NT2 = 2 * NTIL
        off = OFF["R_E"]
        xt = [cv(f"A_x{p}", off + p * 4096, [128, DM], F32) for p in range(2)]
        xn = [cv(f"A_n{p}", off + 8192 + p * 4096, [128, DM], F32) for p in range(2)]
        sq = [cv(f"A_q{p}", off + 16384 + p * 4096, [128, DM], F32) for p in range(2)]
        ssqb = cv("A_ssq", off + 24576, [128, NT2], F32)
        sumb = cv("A_sum", off + 24704, [128, NT2], F32)
        varb = cv("A_var", off + 24832, [128, NT2], F32)
        scrb = cv("A_scr", off + 24960, [128, NT2], F32)
        rrow = cv("A_rrow", off + 25088, [32, 128], F32)
        stg = [cv(f"A_st{p}", off + 25600 + p * 2048, [128, DM], BF16) for p in range(2)]

        t_sq = {}
        t_sum = {}
        for k in range(NT2):
            half, i = divmod(k, NTIL)
            src_h = self.x_own if half == 0 else self.x_pre
            p = k % 2
            dxp = "dx%d" % p
            ld = self.op("sync", lambda e, x=xt[p], s=src_h, i=i:
                         e.dma_start(x[:, :], s[ts(i), :]), inc=(dxp, 16),
                         wait=[("sa", t_sq.get(k - 2)), ("sv", t_sum.get(k - 2))])
            t_sq[k] = self.op("scalar", lambda e, x=xt[p], q=sq[p], k=k: e.activation(
                q[:, :], x[:, :], AF.Square, accum_out=ssqb[:, k:k + 1]),
                wait=[(dxp, ld)], inc=("sa", 1))
            t_sum[k] = self.op("vector", lambda e, x=xt[p], k=k: e.reduce_sum(
                sumb[:, k:k + 1], x[:, :], axis=AX.X), wait=[(dxp, ld)], inc=("sv", 1))
        h0 = self.op("vector", lambda e: e.tensor_scalar_mul(
            self.meanb[:, :], sumb[:, :], 1.0 / DM), inc=("sv", 1),
            wait=[("sv", t_sum[NT2 - 1]), ("sa", t_sq[NT2 - 1])])
        h1 = self.op("vector", lambda e: e.tensor_scalar_mul(
            varb[:, :], ssqb[:, :], 1.0 / DM), inc=("sv", 1))
        h2 = self.op("vector", lambda e: e.tensor_mul(
            scrb[:, :], self.meanb[:, :], self.meanb[:, :]),
            wait=[("sv", h0)], inc=("sv", 1))
        h3 = self.op("vector", lambda e: e.tensor_sub(
            varb[:, :], varb[:, :], scrb[:, :]), wait=[("sv", h2)], inc=("sv", 1))
        a0 = self.op("scalar", lambda e: e.activation(
            scrb[:, :], varb[:, :], AF.Ln, bias=EPS_LN), wait=[("sv", h3)], inc=("sa", 1))
        a1 = self.op("scalar", lambda e: e.activation(
            self.rstdln[:, :], scrb[:, :], AF.Exp, scale=-0.5),
            wait=[("sa", a0)], inc=("sa", 1))
        h4 = self.op("vector", lambda e: e.tensor_mul(
            scrb[:, :], self.rstdln[:, :], self.rstdln[:, :]),
            wait=[("sa", a1)], inc=("sv", 1))
        h5 = self.op("vector", lambda e: e.tensor_mul(
            scrb[:, :], scrb[:, :], varb[:, :]), wait=[("sv", h4)], inc=("sv", 1))
        a2 = self.op("scalar", lambda e: e.activation(
            scrb[:, :], scrb[:, :], AF.Ln, bias=EPS_RMS), wait=[("sv", h5)], inc=("sa", 1))
        a3 = self.op("scalar", lambda e: e.activation(
            self.r1col[:, :], scrb[:, :], AF.Exp, scale=-0.5),
            wait=[("sa", a2)], inc=("sa", 1))
        self.wait("vector", "sa", a3)

        t_v3 = {}
        t_tr = {}
        t_sp = {}
        ev_tok = {}
        for k in range(NT2):
            half, i = divmod(k, NTIL)
            src_h = self.x_own if half == 0 else self.x_pre
            p = k % 2
            dxp, dnp = "dx%d" % p, "dn%d" % p
            ld = self.op("sync", lambda e, x=xt[p], s=src_h, i=i:
                         e.dma_start(x[:, :], s[ts(i), :]), inc=(dxp, 16),
                         wait=[("sv", t_v3.get(k - 2))] +
                              ([("sa", a3), ("sv", h5)] if k < 2 else []))
            t_v3[k] = self.op("vector", lambda e, x=xt[p], n=xn[p], k=k:
                              e.tensor_scalar(n[:, :], x[:, :], self.meanb[:, k:k + 1],
                                              self.rstdln[:, k:k + 1],
                                              ALU.subtract, ALU.mult),
                              wait=[(dxp, ld), ("st", t_tr.get(k - 2)),
                                    (dnp, t_sp.get(k - 2))],
                              inc=("sv", 1))
            if half == 0:
                t_sp[k] = self.op("gpsimd", lambda e, n=xn[p], i=i: e.dma_start(
                    self.xn_dram[ts(i), :], n[:, :]),
                    wait=[("sv", t_v3[k])], inc=(dnp, 16))
            for dck in range(NDC):
                pp = self.P1[:, 0:128] if dck % 2 == 0 else self.P1[:, 512:640]
                tt = self.op("tensor", lambda e, n=xn[p], d=dck, pp=pp: e.transpose(
                    pp, n[:, ts(d)], self.c["identf"][:, :]),
                    wait=[("sv", t_v3[k]), ("sv", ev_tok.get(dck % 2))],
                    inc=("st", 1))
                if half == 0:
                    dst = self.xf[:, dck * T + i * 128: dck * T + (i + 1) * 128]
                else:
                    dst = stg[p][:, ts(dck)]
                ev_tok[dck % 2] = self.op(
                    "vector", lambda e, pp=pp, dst=dst: e.tensor_copy(dst, pp),
                    wait=[("st", tt)] + ([("dn%d" % p, t_sp.get(k - 2))]
                                         if half == 1 else []),
                    inc=("sv", 1))
            t_tr[k] = self.cnt["st"]
            if half == 1:
                ev_all = self.cnt["sv"]
                t_sp[k] = self.op("gpsimd", lambda e, s=stg[p], i=i: e.dma_start(
                    bass.AP(self.xfp_dram.tensor, i * 128,
                            [[NDC * T, 128], [T, NDC], [1, 128]]),
                    s[:, :]), wait=[("sv", ev_all)], inc=(dnp, 16))

        tt = self.op("tensor", lambda e: e.transpose(
            self.PS[0:32, 128:256], self.r1col[:, :], self.c["identf"][:, :]),
            wait=[("sa", a3)] + [("sv", ev_tok.get(q)) for q in range(2)],
            inc=("st", 1))
        ev = self.op("vector", lambda e: e.tensor_copy(rrow[:, :], self.PS[0:32, 128:256]),
                     wait=[("st", tt)], inc=("sv", 1))
        self.t_rowdram = self.op("sync", lambda e: e.dma_start(
            bass.AP(self.row_dram.tensor, 0, [[128, 32], [1, 128]]), rrow[:, :]),
            wait=[("sv", ev)], inc=("dr", 16))
        self.op("gpsimd", lambda e: e.dma_start(self.xf_dram, self.xf[:, :]),
                wait=[("sv", self.cnt["sv"])], inc=("gd", 16))
        self.barrier()

    def phase_B(self, stream, b_stop=None):
        pre = stream == 1
        cv = self.carve
        self.xbc = cv("xbc", OFF["R_E"], [128, NCC * CW], BF16)
        xbc = self.xbc
        self.r1bc = cv("r1bc", OFF["R_A"], [128, T], F32)
        self.dtscr = cv("dtscr", OFF["R_A"] + 8192, [32, T], F32)
        t_r1 = self.op("sync", lambda e, o=(T if pre else 0): e.dma_start(
            self.r1bc[:, :], bass.AP(self.row_dram.tensor, o, [[0, 128], [1, T]])),
            wait=[("dr", self.t_rowdram), ("sv", self.cnt["sv"]),
                  ("sa", self.cnt["sa"])], inc=("dr", 16))
        self.wbuf = cv("wbuf", OFF["R_D"], [128, NDC * 1184], BF16)
        xfh = cv("xfh", OFF["R_C"], [128, NDC * 1024], BF16)

        slots = [self.P0[:, 0:1024], self.P0[:, 1024:2048], self.P1[:, 0:1024]]
        t_ev = {}
        t_mm = {}
        ev_ec = {}
        uid = 0
        t_wv = getattr(self, "t_wdone", None)
        acc = [cv(f"B_acc{q}", OFF["R_C"] + 20480 + q * 4096, [128, T], BF16)
               for q in range(3)]
        t_conv = {}
        t_silu = {}
        self.op("vector", lambda e: e.memset(
            bass.AP(xbc, 2051, [[NCC * CW, 128], [CW, NCC], [1, 1]]), 0.0),
            inc=("sv", 1))

        def emit_pads(lo, hi):
            n = hi - lo
            if pre:
                pz = self.op("vector", lambda e, lo=lo, n=n: e.memset(
                    bass.AP(xbc, lo * CW, [[NCC * CW, 128], [CW, n], [1, 3]]), 0.0),
                    wait=[("sv", ev_ec[hi - 1])], inc=("sv", 1))
                tp = self.op("vector", lambda e, lo=lo, n=n: e.tensor_copy(
                    bass.AP(self.pads_sb, lo * 3, [[NCC * 3, 128], [3, n], [1, 3]]),
                    bass.AP(xbc, lo * CW + 2048, [[NCC * CW, 128], [CW, n], [1, 3]])),
                    wait=[("sv", ev_ec[hi - 1])], inc=("sv", 1))
                self.t_pads = tp
                return pz, tp
            pz = self.op("vector", lambda e, lo=lo, n=n: e.tensor_copy(
                bass.AP(xbc, lo * CW, [[NCC * CW, 128], [CW, n], [1, 3]]),
                bass.AP(self.pads_sb, lo * 3, [[NCC * 3, 128], [3, n], [1, 3]])),
                wait=[("sv", self.t_pads)], inc=("sv", 1))
            return pz, None

        def emit_conv(lo, hi):
            pz, tp = emit_pads(lo, hi)
            for g in range(lo, hi, 3):
                grp = list(range(g, min(g + 3, hi)))
                for step in range(4):
                    for j, ec in enumerate(grp):
                        a = acc[j]
                        if step == 0:
                            t_conv[ec] = self.op("vector", lambda e, ec=ec, a=a:
                                e.tensor_scalar(
                                    a[:, :], xbc[:, ec * CW:ec * CW + T],
                                    self.c["convw"][:, ec * 4:ec * 4 + 1], None,
                                    ALU.mult),
                                wait=[("sv", pz), ("sv", ev_ec[ec]),
                                      ("sa", t_silu.get(ec - 3))],
                                inc=("sv", 1))
                        else:
                            t_conv[ec] = self.op("vector", lambda e, ec=ec, a=a, k=step:
                                e.scalar_tensor_tensor(
                                    a[:, :], xbc[:, ec * CW + k:ec * CW + k + T],
                                    self.c["convw"][:, ec * 4 + k:ec * 4 + k + 1],
                                    a[:, :], ALU.mult, ALU.add),
                                wait=[("sv", t_conv[ec])], inc=("sv", 1))
                for j, ec in enumerate(grp):
                    t_silu[ec] = self.op("scalar", lambda e, ec=ec, a=acc[j]:
                        e.activation(
                            xbc[:, ec * CW + 3:ec * CW + 3 + T], a[:, :], AF.Silu,
                            bias=self.c["convb"][:, ec:ec + 1]),
                        wait=[("sv", t_conv[ec])] +
                             ([("sv", tp)] if pre else []),
                        inc=("sa", 1))

        for halfw, (cb, ncol, ecs, has_dt) in enumerate([
                (DIN, 1152, list(range(0, 9)), False),
                (DIN + 1152, 1184, list(range(9, 18)), True)]):
            ldw = self.op("gpsimd", lambda e, cb=cb, ncl=ncol: e.dma_start(
                bass.AP(self.wbuf, 0, [[NDC * 1184, 128], [1184, NDC], [1, ncl]]),
                bass.AP(self.w_in.tensor, cb, [[DIP, 128], [128 * DIP, NDC], [1, ncl]])),
                inc=("dw", 16), wait=[("st", self.cnt["st"])])
            wtoks = []
            for dc in range(NDC):
                wtoks.append(self.op("vector", lambda e, dc=dc, ncl=ncol:
                    e.tensor_scalar_mul(self.wbuf[:, dc * 1184:dc * 1184 + ncl],
                                        self.wbuf[:, dc * 1184:dc * 1184 + ncl],
                                        self.nwcol[:, dc:dc + 1]),
                    wait=[("dw", ldw)], inc=("sv", 1)))
            t_w = wtoks[-1]
            for th in range(2):
                if pre:
                    t_ldx = self.op("sync", lambda e, th=th: e.dma_start(
                        bass.AP(xfh, 0, [[NDC * 1024, 128], [1024, NDC], [1, 1024]]),
                        bass.AP(self.xfp_dram.tensor, th * 1024,
                                [[NDC * T, 128], [T, NDC], [1, 1024]])),
                        inc=("dx0", 16), wait=[("st", self.cnt["st"])])
                    rhs = lambda dc: xfh[:, dc * 1024:(dc + 1) * 1024]
                    wv_rhs = [("dx0", t_ldx)]
                else:
                    rhs = lambda dc: self.xf[:, dc * T + th * 1024:dc * T + th * 1024 + 1024]
                    wv_rhs = []
                for ec in ecs:
                    colrel = ec * 128 - (0 if halfw == 0 else 1152)
                    sl = slots[uid % 3]
                    for dc in range(NDC):
                        w0 = (wv_rhs + [("sv", t_w), ("sv", t_ev.get(uid - 3))]
                              if dc == 0 else [])
                        mm = self.op("tensor", lambda e, cr=colrel, dc=dc, sl=sl, r=rhs(dc):
                                     [e.matmul(sl[:, 0:512],
                                               self.wbuf[:, dc * 1184 + cr:
                                                          dc * 1184 + cr + 128],
                                               r[:, 0:512],
                                               start=(dc == 0), stop=(dc == NDC - 1)),
                                      e.matmul(sl[:, 512:1024],
                                               self.wbuf[:, dc * 1184 + cr:
                                                          dc * 1184 + cr + 128],
                                               r[:, 512:1024],
                                               start=(dc == 0), stop=(dc == NDC - 1))][-1],
                                     wait=w0,
                                     inc=("st", 1) if dc == NDC - 1 else None)
                    t_mm[uid] = mm
                    t_ev[uid] = self.op("vector", lambda e, ec=ec, th=th, sl=sl:
                        e.tensor_tensor(
                            xbc[:, ec * CW + 3 + th * 1024:ec * CW + 3 + th * 1024 + 1024],
                            sl, self.r1bc[:, th * 1024:th * 1024 + 1024], ALU.mult),
                        wait=[("st", mm), ("dr", t_r1)], inc=("sv", 1))
                    if th == 1:
                        ev_ec[ec] = t_ev[uid]
                    uid += 1
                if has_dt:
                    sl = slots[uid % 3]
                    for dc in range(NDC):
                        w0 = (wv_rhs + [("sv", t_w), ("sv", t_ev.get(uid - 3))]
                              if dc == 0 else [])
                        mm = self.op("tensor", lambda e, dc=dc, sl=sl, r=rhs(dc):
                            [e.matmul(sl[0:32, 0:512],
                                      self.wbuf[:, dc * 1184 + 1152:dc * 1184 + 1184],
                                      r[:, 0:512],
                                      start=(dc == 0), stop=(dc == NDC - 1)),
                             e.matmul(sl[0:32, 512:1024],
                                      self.wbuf[:, dc * 1184 + 1152:dc * 1184 + 1184],
                                      r[:, 512:1024],
                                      start=(dc == 0), stop=(dc == NDC - 1))][-1],
                            wait=w0, inc=("st", 1) if dc == NDC - 1 else None)
                    self.t_dtev = t_ev[uid] = self.op(
                        "vector", lambda e, th=th, sl=sl: e.tensor_copy(
                            self.dtf[:, th * 1024:th * 1024 + 1024], sl[0:32, :]),
                        wait=[("st", mm)], inc=("sv", 1))
                    uid += 1
            if halfw == 0 and b_stop != "mm":
                emit_conv(0, 9)
        self.t_wdone = self.cnt["st"]

        if not pre and b_stop != "mm":
            zst = [self.carve(f"zst{q}", OFF["R_C"] + 16384 + q * 2048, [128, 1024], BF16)
                   for q in range(2)]
            t_zev = {}
            t_zsp = {}
            zu = 0
            for zh in range(2):
                ldw = self.op("gpsimd", lambda e, zh=zh: e.dma_start(
                    bass.AP(self.wbuf, 0, [[NDC * 1184, 128], [1184, NDC], [1, 1024]]),
                    bass.AP(self.w_in.tensor, zh * 1024,
                            [[DIP, 128], [128 * DIP, NDC], [1, 1024]])),
                    inc=("dw", 16), wait=[("st", self.cnt["st"])])
                wtoks = []
                for dc in range(NDC):
                    wtoks.append(self.op("vector", lambda e, dc=dc:
                        e.tensor_scalar_mul(self.wbuf[:, dc * 1184:dc * 1184 + 1024],
                                            self.wbuf[:, dc * 1184:dc * 1184 + 1024],
                                            self.nwcol[:, dc:dc + 1]),
                        wait=[("dw", ldw)], inc=("sv", 1)))
                t_w = wtoks[-1]
                for i in range(NTIL):
                    sl = slots[zu % 3]
                    for dc in range(NDC):
                        w0 = (([("sv", t_w), ("sa", t_zev.get(zu - 3))] +
                               ([("sv", t_ev[uid - 1])] if zu < 3 else []))
                              if dc == 0 else [])
                        mm = self.op("tensor", lambda e, i=i, dc=dc, sl=sl:
                            [e.matmul(sl[:, 0:512],
                                      self.xf[:, dc * T + i * 128:dc * T + (i + 1) * 128],
                                      self.wbuf[:, dc * 1184:dc * 1184 + 512],
                                      start=(dc == 0), stop=(dc == NDC - 1)),
                             e.matmul(sl[:, 512:1024],
                                      self.xf[:, dc * T + i * 128:dc * T + (i + 1) * 128],
                                      self.wbuf[:, dc * 1184 + 512:dc * 1184 + 1024],
                                      start=(dc == 0), stop=(dc == NDC - 1))][-1],
                            wait=w0, inc=("st", 1) if dc == NDC - 1 else None)
                    t_zev[zu] = self.op("scalar", lambda e, i=i, sl=sl, zq=zu % 2:
                        e.activation(zst[zq][:, :], sl, AF.Silu,
                                     scale=self.r1col[:, i:i + 1]),
                        wait=[("st", mm), ("dn%d" % (zu % 2), t_zsp.get(zu - 2))],
                        inc=("sa", 1))
                    t_zsp[zu] = self.op("gpsimd", lambda e, i=i, zh=zh, zq=zu % 2:
                        e.dma_start(self.sz_dram[ts(i), zh * 1024:(zh + 1) * 1024],
                                    zst[zq][:, :]),
                        wait=[("sa", t_zev[zu])], inc=("dn%d" % (zu % 2), 16))
                    zu += 1
            self.t_zdone = self.cnt["sa"]

        if b_stop == "mm":
            return
        emit_conv(9, NCC)
        self.t_xbc = self.cnt["sa"]
        if b_stop == "conv":
            return

        h0 = self.op("vector", lambda e: e.tensor_mul(
            self.dtf[:, :], self.dtf[:, :], self.r1bc[0:32, :]),
            wait=[("sv", self.t_dtev), ("dr", t_r1)], inc=("sv", 1))
        h1 = self.op("vector", lambda e: e.tensor_tensor(
            self.dtf[:, :], self.dtf[:, :],
            bass.AP(self.c["dtb"], 0, [[1, 32], [0, T]]), ALU.add),
            wait=[("sv", h0)], inc=("sv", 1))
        a0 = self.op("scalar", lambda e: e.activation(self.dtscr[:, :], self.dtf[:, :],
                     AF.Abs), wait=[("sv", h1)], inc=("sa", 1))
        a1 = self.op("scalar", lambda e: e.activation(self.dtscr[:, :], self.dtscr[:, :],
                     AF.Exp, scale=-1.0), wait=[("sa", a0)], inc=("sa", 1))
        a2 = self.op("scalar", lambda e: e.activation(self.dtscr[:, :], self.dtscr[:, :],
                     AF.Ln, bias=1.0), wait=[("sa", a1)], inc=("sa", 1))
        h2 = self.op("vector", lambda e: e.tensor_scalar_max(
            self.dtf[:, :], self.dtf[:, :], 0.0), wait=[("sa", a0)], inc=("sv", 1))
        h3 = self.op("vector", lambda e: e.tensor_tensor(
            self.dtf[:, :], self.dtf[:, :], self.dtscr[:, :], ALU.add),
            wait=[("sa", a2), ("sv", h2)], inc=("sv", 1))
        h4 = self.op("vector", lambda e: e.tensor_tensor(
            self.dtAf[:, :], self.dtf[:, :],
            bass.AP(self.c["A"], 0, [[1, 32], [0, T]]), ALU.mult),
            wait=[("sv", h3)], inc=("sv", 1))
        self.t_dt = h4
        self.barrier()

    def phase_C(self, stream):
        import os as _os
        pre = stream == 1
        cv = self.carve
        xbc = self.xbc
        c = self.c
        H = cv("Hst", OFF["R_D"] + 8192, [128, DIN], F32)
        bc = [cv(f"bcast{q}", OFF["R_D"] + q * 4096, [128, 1024], F32) for q in range(2)]
        est = [cv(f"Est{q}", OFF["R_D"] + 16384 + q * 512, [128, 128], F32)
               for q in range(3)]
        emt = [cv(f"Emt{q}", OFF["R_D"] + 17920 + q * 256, [128, 128], BF16)
               for q in range(2)]
        y_acc = cv("y_acc", OFF["R_C"], [128, DIN], F32)
        ynT = cv("ynT", OFF["R_C"] + 8192, [128, DIN], BF16)
        xn_t = cv("xn_t", OFF["R_C"] + 12288, [128, DM], F32)
        h1_t = cv("h1_t", OFF["R_C"] + 16384, [128, DM], F32)
        xw = cv("xw", OFF["R_C"] + 20480, [128, DIN], BF16)
        cscr = cv("cscr", OFF["R_C"] + 24576, [128, DIN], F32)
        sz_t = cv("sz_t", OFF["R_A"], [128, DIN], BF16)
        x_tok = cv("x_tok", OFF["R_A"] + 4096, [128, DIN], BF16)
        H_bf = cv("H_bf", OFF["R_A"] + 8192, [128, DIN], BF16)
        yn = cv("yn", OFF["R_A"] + 12288, [128, DIN], BF16)
        B_tok = cv("B_tok", OFF["small2"], [128, 128], BF16)
        Gm = cv("Gm", OFF["small2"] + 256, [128, 128], BF16)
        ssq = cv("ssq", OFF["devscr"], [128, 1], F32)
        wout = cv("wout", OFF["R_B"], [128, 16 * DM], BF16)
        P0, P1, PT, PS = self.P0, self.P1, self.PT, self.PS

        if pre:
            t_h0 = self.op("vector", lambda e: e.memset(H[:, :], 0.0), inc=("sv", 1))
            t_carry = t_h0
        else:
            t_h = self.op("sync", lambda e: e.dma_start(H[:, :], self.dram["hstate_dram"]),
                          wait=[("dp", self.t_hspill)], inc=("dr", 16))
            self.wait("vector", "dr", t_h)
            t_carry = None
            self.t_wo = self.op("gpsimd", lambda e: e.dma_start(
                bass.AP(wout, 0, [[16 * DM, 128], [DM, 16], [1, DM]]),
                bass.AP(self.w_out.tensor, 0, [[DM, 128], [128 * DM, 16], [1, DM]])),
                inc=("dw", 16), wait=[("st", self.cnt["st"])])
            t_hbf = self.op("scalar", lambda e: e.activation(
                H_bf[:, :], H[:, :], AF.Copy), wait=[("dr", t_h)], inc=("sa", 1))

        slotA_f = PS[:, 0:128]
        slotA_b = PS[:, 128:192].bitcast(BF16)
        slotB_f = PT[:, 0:256].bitcast(F32)
        slotB_b = PT[:, 256:384]
        t_slot = {0: None, 1: None}

        def transpose(idx, src_ap, dst_ap, dt_, xtra_tr=(), xtra_ev=()):
            q = idx % 2
            pp = (slotA_f if dt_ == F32 else slotA_b) if q == 0 else \
                 (slotB_f if dt_ == F32 else slotB_b)
            ident = c["identf"] if dt_ == F32 else c["ident"]
            np_, nf = src_ap.shape[0], src_ap.shape[1]
            tt = self.op("tensor", lambda e, pp=pp, s=src_ap, i=ident, np_=np_, nf=nf:
                         e.transpose(pp[0:nf, 0:np_], s, i[0:np_, 0:np_]),
                         wait=[("sv", t_slot[q])] + list(xtra_tr), inc=("st", 1))
            t_slot[q] = self.op("vector",
                                lambda e, pp=pp, d=dst_ap, nf=nf, np_=np_:
                                e.tensor_copy(d, pp[0:nf, 0:np_]),
                                wait=[("st", tt)] + list(xtra_ev), inc=("sv", 1))
            return t_slot[q]

        exp64 = lambda buf, o=0: bass.AP(buf, o, [[buf.shape[1], 128], [1, NH], [0, HPD]])

        stA = {}
        t_endB = {-2: (0, 0, 0), -1: (0, 0, 0)}
        t_grp = {}
        t_gate = None
        t_zmm = None
        t_smm = None
        t_p0v = None
        t_xn_pv = None
        t_h1_sp = {}
        t_dchain = {-2: 0, -1: 0}
        t_lch = {-2: 0, -1: 0}
        t_tD = {}
        pend_tail = {}

        def emit_tail(cc):
            nonlocal t_xn_pv
            ynt_c, ldxn_c = pend_tail.pop(cc)
            o_tok = {}
            for q in range(2):
                omm_c = None
                for f in range(16):
                    omm_c = self.op("tensor", lambda e, f=f, q=q: e.matmul(
                        P1[:, q * 512:(q + 1) * 512], ynT[:, f * 128:(f + 1) * 128],
                        wout[:, f * DM + q * 512:f * DM + (q + 1) * 512],
                        start=(f == 0), stop=(f == 15)),
                        wait=([("sv", ynt_c), ("dw", self.t_wo),
                               ("sv", t_xn_pv)]
                              if f == 0 and q == 0 else []),
                        inc=("st", 1) if f == 15 else None)
                o_tok[q] = omm_c
            ev_a = self.op("vector", lambda e: e.tensor_tensor(
                h1_t[:, 0:512], P1[:, 0:512], xn_t[:, 0:512], ALU.add),
                wait=[("st", o_tok[0]), ("dh%d" % (cc % 2), ldxn_c),
                      ("dn%d" % ((cc - 1) % 2), t_h1_sp.get(cc - 1))],
                inc=("sv", 1))
            ev_o = self.op("vector", lambda e: e.tensor_tensor(
                h1_t[:, 512:1024], P1[:, 512:1024], xn_t[:, 512:1024], ALU.add),
                wait=[("st", o_tok[1]), ("dh%d" % (cc % 2), ldxn_c),
                      ("dn%d" % ((cc - 1) % 2), t_h1_sp.get(cc - 1))],
                inc=("sv", 1))
            t_xn_pv = ev_o
            t_h1_sp[cc] = self.op("gpsimd", lambda e, cc=cc: e.dma_start(
                self.h1_dram[ts(cc), :], h1_t[:, :]),
                wait=[("sv", ev_o)], inc=("dn%d" % (cc % 2), 16))

        def stageA(ch):
            par = ch % 2
            dbp = "db%d" % par
            svp, sap, stp = t_endB[ch - 2]
            W0 = ch * Q
            d = {}
            lt, lT = self.lam_tok[par], self.lamT[par]
            dk = self.dtk[par]
            gm_, wr_, dw_, pt_, le_ = (self.gam[par], self.wrow[par], self.dtw[par],
                                       self.ptb[par], self.lamend[par])
            tdt = transpose(0, self.dtf[:, W0:W0 + Q], dk[:, 0:NH], F32,
                            xtra_ev=[("sv", svp), ("sa", sap)])
            tda = transpose(1, self.dtAf[:, W0:W0 + Q], dk[:, NH:2 * NH], F32,
                            xtra_ev=[("sv", svp), ("sa", sap)])
            m1 = self.op("tensor", lambda e, dk=dk: e.matmul(
                slotA_f[:, 0:32], c["triu_f"][:, :], dk[:, NH:2 * NH],
                start=True, stop=True),
                wait=[("sv", tda), ("sv", svp), ("sv", t_slot[0])], inc=("st", 1))
            m2 = self.op("tensor", lambda e, dk=dk: e.matmul(
                slotB_f[0:32, 0:128], dk[:, NH:2 * NH], c["triu_f"][:, :],
                start=True, stop=True),
                wait=[("sv", t_slot[1])], inc=("st", 1))
            e1 = t_slot[0] = self.op("vector", lambda e, lt=lt: e.tensor_copy(
                lt[:, :], slotA_f[:, 0:32]),
                wait=[("st", m2), ("sa", sap)], inc=("sv", 1))
            e2 = t_slot[1] = self.op("vector", lambda e, lT=lT: e.tensor_copy(
                lT[:, 0:Q], slotB_f[0:32, 0:128]), inc=("sv", 1))
            m3 = self.op("tensor", lambda e, lT=lT: e.matmul(
                slotA_f[:, 0:32],
                bass.AP(lT, 127, [[128, 32], [0, 128]]),
                c["identf"][0:32, 0:32], start=True, stop=True),
                wait=[("sv", e2), ("sv", t_slot[0])], inc=("st", 1))
            e3 = t_slot[0] = self.op("vector", lambda e, le=le_: e.tensor_copy(
                le[:, :], slotA_f[:, 0:32]),
                wait=[("st", m3)], inc=("sv", 1))
            dtok = e3
            t_lch[ch] = e3
            vw = self.op("vector", lambda e, wr=wr_, le=le_, lt=lt: e.tensor_sub(
                wr[:, :], le[:, :], lt[:, :]),
                wait=[("sv", e3)], inc=("sv", 1))
            aw = self.op("scalar", lambda e, wr=wr_: e.activation(
                wr[:, :], wr[:, :], AF.Exp), wait=[("sv", vw)], inc=("sa", 1))
            ag = self.op("scalar", lambda e, gm=gm_, le=le_: e.activation(
                gm[:, :], le[:, :], AF.Exp),
                wait=[("sv", e3), ("sa", sap)], inc=("sa", 1))
            if not pre:
                ap_ = self.op("scalar", lambda e, pt=pt_, lt=lt: e.activation(
                    pt[:, 0:NH], lt[:, :], AF.Exp),
                    wait=[("sv", e1), ("sa", sap)], inc=("sa", 1))
                d["ap"] = ap_
            vdw = self.op("vector", lambda e, dw=dw_, dk=dk, wr=wr_: e.tensor_mul(
                dw[:, 0:NH], dk[:, 0:NH], wr[:, :]),
                wait=[("sa", aw), ("sv", tdt)], inc=("sv", 1))
            d.update(e1=e1, e2=e2, vw=vw, aw=aw, ag=ag, vdw=vdw, tdt=tdt, tda=tda,
                     dtok=dtok, par=par)
            return d

        def bcast_chain(chx):
            A = stA[chx]
            par = A["par"]
            dbp = "db%d" % par
            lT = self.lamT[par]
            rdhs = []
            dtok = t_dchain[chx - 2]
            for g in range(4):
                gq = chx * 4 + g
                wrh = self.op("sync", lambda e, lT=lT, g=g, par=par: e.dma_start(
                    bass.AP(self.lam_dram.tensor, par * 2048 + (g % 2) * 1024,
                            [[128, 8], [1, 128]]),
                    lT[g * 8:(g + 1) * 8, 0:Q]),
                    wait=[("sv", A["e2"]), (dbp, dtok)], inc=(dbp, 16))
                rdh = self.op("sync", lambda e, g=g, par=par: e.dma_start(
                    bc[g % 2][:, :],
                    bass.AP(self.lam_dram.tensor, par * 2048 + (g % 2) * 1024,
                            [[0, 128], [1, 1024]])),
                    wait=[(dbp, wrh), ("sq", max(0, gq - 1))],
                    inc=(dbp, 16))
                dtok = rdh
                rdhs.append(rdh)
            A["rdhs"] = rdhs
            t_dchain[chx] = dtok

        stA[0] = stageA(0)
        if not pre:
            bcast_chain(0)
        for ch in range(NCH):
            if ch + 1 < NCH:
                stA[ch + 1] = stageA(ch + 1)
                if not pre:
                    bcast_chain(ch + 1)
            A = stA[ch]
            par = ch % 2
            dbp = "db%d" % par
            dzp = "dz%d" % par
            dhp = "dh%d" % par
            dnp = "dn%d" % par
            W0 = ch * Q
            lt, dk = self.lam_tok[par], self.dtk[par]
            gm_, dw_, pt_ = self.gam[par], self.dtw[par], self.ptb[par]
            t_xtok = None
            for f in range(16):
                t_xtok = transpose(f, xbc[:, f * CW + 3 + W0:f * CW + 3 + W0 + Q],
                                   x_tok[:, f * 128:(f + 1) * 128], BF16,
                                   xtra_tr=[("sa", self.t_xbc)] if f == 0 else (),
                                   xtra_ev=[("st", t_smm),
                                            ("sg", t_tD.get(ch - 1))]
                                   if f == 0 else ())
            t_btok = transpose(1, xbc[:, 16 * CW + 3 + W0:16 * CW + 3 + W0 + Q],
                               B_tok[:, :], BF16)
            vxw = self.op("vector", lambda e, dw=dw_: e.tensor_tensor(
                xw[:, :], x_tok[:, :], exp64(dw), ALU.mult),
                wait=[("sv", A["vdw"]), ("sv", t_btok)], inc=("sv", 1))
            if not pre and (ch - 1) in pend_tail:
                emit_tail(ch - 1)
            if not pre:
                mg = self.op("tensor", lambda e, W0=W0: e.matmul(
                    slotB_f[:, 0:128], xbc[:, 16 * CW + 3 + W0:16 * CW + 3 + W0 + Q],
                    xbc[:, 17 * CW + 3 + W0:17 * CW + 3 + W0 + Q],
                    start=True, stop=True),
                    wait=[("sv", t_slot[1]), ("sa", self.t_xbc)],
                    inc=("st", 1))
                t_gm = t_slot[1] = self.op("vector", lambda e: e.tensor_tensor(
                    Gm[:, :], slotB_f[:, 0:128], c["triu_b"][:, :], ALU.mult),
                    wait=[("st", mg)], inc=("sv", 1))
                zmm = None
                for q in range(4):
                    zmm = self.op("tensor", lambda e, q=q, W0=W0: e.matmul(
                        P0[:, q * 512:(q + 1) * 512],
                        xbc[:, 17 * CW + 3 + W0:17 * CW + 3 + W0 + Q],
                        H_bf[:, q * 512:(q + 1) * 512], start=True, stop=True),
                        wait=([("sa", t_hbf), ("sv", t_p0v)] if q == 0 else []),
                        inc=("st", 1) if q == 3 else None)
                t_zmm = zmm
                ez = self.op("vector", lambda e, pt=pt_: e.tensor_tensor(
                    y_acc[:, :], P0[:, :], exp64(pt), ALU.mult),
                    wait=[("st", zmm), ("sa", A["ap"]), ("sv", t_gate)], inc=("sv", 1))
                t_ee = {}
                t_mth = {}
                t_vm = {}
                for hh in range(NH + 2):
                    if hh < NH:
                        h = hh
                        g = h // 8
                        gq = ch * 4 + g
                        rdh = A["rdhs"][g]
                        q3 = h % 3
                        lastg = h % 8 == 7
                        vd = self.op("vector", lambda e, h=h, g=g, q3=q3, lt=lt:
                                     e.tensor_scalar(
                                         est[q3][:, :],
                                         bc[g % 2][:, (h % 8) * 128:(h % 8 + 1) * 128],
                                         lt[:, h:h + 1], CLAMP, ALU.subtract, ALU.min),
                                     wait=[(dbp, rdh), ("sv", t_vm.get(h - 3))],
                                     inc=("sq", 1) if lastg else ("sv", 1))
                        t_ee[h] = self.op("scalar", lambda e, q3=q3: e.activation(
                            est[q3][:, :], est[q3][:, :], AF.Exp),
                            wait=[("sq", gq + 1) if lastg else ("sv", vd)],
                            inc=("sa", 1))
                    if hh >= 2:
                        h = hh - 2
                        q = h % 2
                        q3 = h % 3
                        t_vm[h] = vm = self.op("vector", lambda e, h=h, q=q, q3=q3, dk=dk:
                                     e.scalar_tensor_tensor(
                                         emt[q][:, :], est[q3][:, :], dk[:, h:h + 1],
                                         Gm[:, :], ALU.mult, ALU.mult),
                                     wait=[("sa", t_ee[h]), ("st", t_mth.get(h - 2))] +
                                          ([("sv", t_gm)] if h == 0 else []),
                                     inc=("sv", 1))
                        t_mth[h] = self.op("tensor", lambda e, h=h, q=q: e.matmul(
                            P0[:, h * 64:(h + 1) * 64], emt[q][:, :],
                            x_tok[:, h * 64:(h + 1) * 64], start=True, stop=True),
                            wait=[("sv", vm)] + ([("sv", ez), ("sv", t_xtok)]
                                                 if h == 0 else []),
                            inc=("st", 1))
                t_mt_last = t_mth[NH - 1]
                ev_y = self.op("vector", lambda e: e.tensor_tensor(
                    y_acc[:, :], y_acc[:, :], P0[:, :], ALU.add),
                    wait=[("st", t_mt_last), ("sv", ez)], inc=("sv", 1))
                tD = self.op("gpsimd", lambda e: e.tensor_tensor(
                    cscr[:, :], x_tok[:, :], c["dexp"][:, :], ALU.mult),
                    wait=[("sv", t_xtok), ("sa", t_endB[ch - 1][1])], inc=("sg", 1))
                t_tD[ch] = tD
                ev_y2 = self.op("vector", lambda e: e.tensor_tensor(
                    y_acc[:, :], y_acc[:, :], cscr[:, :], ALU.add),
                    wait=[("sv", ev_y), ("sg", tD)], inc=("sv", 1))
                t_ya = ev_y2
                if self.dev_phase is not None:
                    if "dev_y" not in self.dram:
                        self.dout("dev_y", [T, DIN])
                    tdmp = self.op("scalar", lambda e, ch=ch: e.dma_start(
                        self.dram["dev_y"][ts(ch), :], y_acc[:, :]),
                        wait=[("sv", ev_y2), ("so", self.cnt["so"])], inc=("so", 16))
                    t_ya = ("so", tdmp)
            else:
                t_ya = None
            smm = None
            for q in range(4):
                smm = self.op("tensor", lambda e, q=q: e.matmul(
                    P0[:, q * 512:(q + 1) * 512], B_tok[:, :],
                    xw[:, q * 512:(q + 1) * 512], start=True, stop=True),
                    wait=([("sv", vxw)] +
                          ([("sv", ev_y)] if not pre else [("sv", t_p0v)])
                          ) if q == 0 else [],
                    inc=("st", 1) if q == 3 else None)
            t_smm = smm
            c1 = self.op("vector", lambda e, gm=gm_: e.tensor_tensor(
                cscr[:, :], H[:, :], exp64(gm), ALU.mult),
                wait=[("sa", A["ag"]), ("sv", t_carry)] +
                     ([(t_ya if isinstance(t_ya, tuple) else ("sv", t_ya))]
                      if not pre else []), inc=("sv", 1))
            t_carry = self.op("vector", lambda e: e.tensor_tensor(
                H[:, :], cscr[:, :], P0[:, :], ALU.add),
                wait=[("st", smm), ("sv", c1)], inc=("sv", 1))
            t_p0v = t_carry
            if not pre:
                ldsz = self.op("gpsimd", lambda e, ch=ch: e.dma_start(
                    sz_t[:, :], self.sz_dram[ts(ch), :]),
                    wait=[("sv", t_gate)], inc=(dzp, 16))
                ldxn = self.op("scalar", lambda e, ch=ch: e.dma_start(
                    xn_t[:, :], self.xn_dram[ts(ch), :]),
                    wait=[("sv", t_xn_pv)], inc=(dhp, 16))
                g1 = self.op("vector", lambda e: e.tensor_tensor(
                    y_acc[:, :], y_acc[:, :], sz_t[:, :], ALU.mult),
                    wait=[(t_ya if isinstance(t_ya, tuple) else ("sv", t_ya)),
                          (dzp, ldsz)], inc=("sv", 1))
                t_gate = g1
                a1 = self.op("scalar", lambda e: e.activation(
                    cscr[:, :], y_acc[:, :], AF.Square, accum_out=ssq[:, :]),
                    wait=[("sv", g1), ("sv", t_carry)], inc=("sa", 1))
                a2 = self.op("scalar", lambda e: e.activation(
                    ssq[:, :], ssq[:, :], AF.Ln, bias=EPS_RMS, scale=1.0 / DIN),
                    wait=[("sa", a1)], inc=("sa", 1))
                a3 = self.op("scalar", lambda e: e.activation(
                    ssq[:, :], ssq[:, :], AF.Exp, scale=-0.5),
                    wait=[("sa", a2)], inc=("sa", 1))
                vyn = self.op("vector", lambda e: e.scalar_tensor_tensor(
                    yn[:, :], y_acc[:, :], ssq[:, :1], c["ssmw"][:, :],
                    ALU.mult, ALU.mult), wait=[("sa", a3)], inc=("sv", 1))
                t_ynt = None
                for f in range(16):
                    t_ynt = transpose(f, yn[:, f * 128:(f + 1) * 128],
                                      ynT[:, f * 128:(f + 1) * 128], BF16,
                                      xtra_tr=[("sv", vyn)] if f == 0 else ())
                pend_tail[ch] = (t_ynt, ldxn)
                if ch + 1 < NCH:
                    t_hbf = self.op("scalar", lambda e: e.activation(
                        H_bf[:, :], H[:, :], AF.Copy),
                        wait=[("sv", t_carry), ("st", t_zmm)], inc=("sa", 1))
            t_endB[ch] = (self.cnt["sv"], self.cnt["sa"], self.cnt["st"])
        if not pre and (NCH - 1) in pend_tail:
            emit_tail(NCH - 1)
        if pre:
            tm = self.op("vector", lambda e: e.tensor_scalar_mul(
                H[:, :], H[:, :], c["smask"][:, 0:1]),
                wait=[("sv", t_carry)], inc=("sv", 1))
            self.t_hspill = self.op("gpsimd", lambda e: e.dma_start(
                self.dram["hstate_dram"], H[:, :]), wait=[("sv", tm)], inc=("dp", 16))
        self.barrier()

    def phase_E(self):
        cv = self.carve
        c = self.c
        P0, P1, PT, PS = self.P0, self.P1, self.PT, self.PS
        wg1 = cv("wg1", OFF["R_C"], [128, NDC * INTER1], BF16)
        wu1 = cv("wu1", OFF["R_D"], [128, NDC * INTER1], BF16)
        wd1 = cv("wd1", OFF["R_D"] + 44032, [128, 22 * DM], BF16)
        h1t = [cv(f"E_h1{p}", OFF["R_B"] + p * 4096, [128, DM], F32)
               for p in range(2)]
        hn = [cv(f"E_hn{p}", OFF["R_B"] + 8192 + p * 2048, [128, DM], BF16)
              for p in range(2)]
        hnTt = [cv(f"E_hnT{p}", OFF["R_B"] + 12288 + p * 2048, [128, NDC * 128],
                   BF16) for p in range(2)]
        h2 = cv("E_h2", OFF["R_B"] + 16384, [128, INTER1], BF16)
        h2t = cv("E_h2t", OFF["R_B"] + 21888, [128, 22 * 128], BF16)
        sqscr = cv("E_sqs", OFF["dt2"], [128, DM], F32)
        ssqE = cv("E_ssq", OFF["devscr"], [128, 2], F32)
        EPARTS = [(0, 1024), (1024, 1024), (2048, 704)]
        wsem = ["dw", "dl0", "dl1", "dp", "dw", "dl0", "dl1", "dp"]
        t_wg, t_wu = {}, {}
        si = 0
        first = [("st", self.cnt["st"])]
        lastw = {}
        for pi, (cb, cn) in enumerate(EPARTS):
            for wdst, wsrc, td in ((wg1, self.w_g1, t_wg), (wu1, self.w_u1, t_wu)):
                s = wsem[si]
                w = first + ([(s, lastw[s])] if s in lastw else [])
                tok = self.op("gpsimd",
                    lambda e, wdst=wdst, wsrc=wsrc, cb=cb, cn=cn: e.dma_start(
                        bass.AP(wdst, cb, [[NDC * INTER1, 128], [INTER1, NDC], [1, cn]]),
                        bass.AP(wsrc.tensor, cb,
                                [[INTER1, 128], [128 * INTER1, NDC], [1, cn]])),
                    inc=(s, 16), wait=w)
                td[pi] = (s, tok)
                lastw[s] = tok
                first = []
                si = (si + 1) % len(wsem)
        t_wd = ("dl1", self.op("gpsimd", lambda e: e.dma_start(
            bass.AP(wd1, 0, [[22 * DM, 128], [DM, 21], [1, DM]]),
            bass.AP(self.w_d1.tensor, 0, [[DM, 128], [128 * DM, 21], [1, DM]])),
            inc=("dl1", 16), wait=[("dl1", lastw["dl1"])]))
        t_wdb = ("dp", self.op("gpsimd", lambda e: e.dma_start(
            bass.AP(wd1, 21 * DM, [[22 * DM, 64], [1, DM]]),
            bass.AP(self.w_d1.tensor, 21 * 128 * DM, [[DM, 64], [1, DM]])),
            inc=("dp", 16), wait=[("dp", lastw["dp"])]))
        slotA_b = P1[:, 640:704].bitcast(BF16)
        slotB_b = PT[:, 256:384]
        t_slot = {0: None, 1: None}

        def transpose(idx, src_ap, dst_ap, xtra_tr=(), xtra_ev=()):
            q = idx % 2
            pp = slotA_b if q == 0 else slotB_b
            np_ = src_ap.shape[0]
            nf = src_ap.shape[1]
            tt = self.op("tensor", lambda e, pp=pp, s=src_ap, np_=np_, nf=nf:
                         e.transpose(pp[0:nf, 0:np_], s, c["ident"][0:np_, 0:np_]),
                         wait=[("sv", t_slot[q])] + list(xtra_tr), inc=("st", 1))
            t_slot[q] = self.op("vector",
                                lambda e, pp=pp, d=dst_ap, nf=nf, np_=np_:
                                e.tensor_copy(d, pp[0:nf, 0:np_]),
                                wait=[("st", tt)] + list(xtra_ev), inc=("sv", 1))
            return t_slot[q]

        t_ldh = {}
        t_rms = {}
        t_vh = {}
        t_hnt = {}
        t_ttl = {}
        t_guT = {}
        t_h2tt = {}
        t_v1b = {}
        t_sp = {}

        def front(i):
            p = i % 2
            dhp = "dh%d" % p
            t_ldh[i] = self.op("sync", lambda e, i=i, p=p: e.dma_start(
                h1t[p][:, :], self.h1_dram[ts(i), :]),
                wait=[("dn%d" % p, t_sp.get(i - 2))], inc=(dhp, 16))
            a1 = self.op("scalar", lambda e, p=p: e.activation(
                sqscr[:, :], h1t[p][:, :], AF.Square, accum_out=ssqE[:, p:p + 1]),
                wait=[(dhp, t_ldh[i]), ("sa", t_rms.get(i - 1))], inc=("sa", 1))
            a2 = self.op("scalar", lambda e, p=p: e.activation(
                ssqE[:, p:p + 1], ssqE[:, p:p + 1], AF.Ln, bias=EPS_RMS,
                scale=1.0 / DM), wait=[("sa", a1)], inc=("sa", 1))
            t_rms[i] = a3 = self.op("scalar", lambda e, p=p: e.activation(
                ssqE[:, p:p + 1], ssqE[:, p:p + 1], AF.Exp, scale=-0.5),
                wait=[("sa", a2)], inc=("sa", 1))
            t_vh[i] = self.op("vector", lambda e, p=p: e.scalar_tensor_tensor(
                hn[p][:, :], h1t[p][:, :], ssqE[:, p:p + 1], c["postw"][:, :],
                ALU.mult, ALU.mult),
                wait=[("sa", a3), ("st", t_ttl.get(i - 2))], inc=("sv", 1))
            for dc in range(NDC):
                t_hnt[i] = transpose(
                    dc, hn[p][:, ts(dc)], hnTt[p][:, ts(dc)],
                    xtra_tr=[("sv", t_vh[i])] if dc == 0 else (),
                    xtra_ev=[("st", t_guT.get(i - 2))] if dc == 0 else ())
            t_ttl[i] = self.cnt["st"]

        parts = [(0, 1024), (1024, 1024), (2048, 704)]
        slots = [P0[:, 0:1024], P0[:, 1024:2048]]
        t_silu_u = {}
        t_spu = {}
        t_dn = {}
        uu = 0
        front(0)
        for i in range(NTIL):
            p = i % 2
            dhp = "dh%d" % p
            for pi, (cb, cn) in enumerate(parts):
                for j, w in ((0, wg1), (1, wu1)):
                    mm = None
                    for dc in range(NDC):
                        mm = self.op("tensor", lambda e, dc=dc, cb=cb, cn=cn,
                                     j=j, w=w, p=p: ([
                            e.matmul(slots[j][:, 0:512],
                                     hnTt[p][:, dc * 128:(dc + 1) * 128],
                                     w[:, dc * INTER1 + cb:dc * INTER1 + cb + min(cn, 512)],
                                     start=(dc == 0), stop=(dc == NDC - 1))] +
                            ([e.matmul(slots[j][:, 512:512 + cn - 512],
                                       hnTt[p][:, dc * 128:(dc + 1) * 128],
                                       w[:, dc * INTER1 + cb + 512:dc * INTER1 + cb + cn],
                                       start=(dc == 0), stop=(dc == NDC - 1))]
                             if cn > 512 else []))[-1],
                            wait=(([("sa", t_silu_u.get(uu - 1)), ("sv", t_hnt[i])]
                                   if j == 0 else [("sv", t_spu.get(uu - 1))]) +
                                  ([(t_wg if j == 0 else t_wu)[pi]] if i == 0 else [])
                                  if dc == 0 else []),
                            inc=("st", 1) if dc == NDC - 1 else None)
                mmtok = self.cnt["st"]
                t_silu_u[uu] = self.op(
                    "scalar", lambda e, cb=cb, cn=cn: e.activation(
                        h2[:, cb:cb + cn], slots[0][:, 0:cn], AF.Silu),
                    wait=[("st", mmtok)] +
                         ([("st", t_h2tt.get(i - 1))] if pi == 0 else []),
                    inc=("sa", 1))
                t_spu[uu] = self.op(
                    "vector", lambda e, cb=cb, cn=cn: e.tensor_tensor(
                        h2[:, cb:cb + cn], h2[:, cb:cb + cn], slots[1][:, 0:cn],
                        ALU.mult),
                    wait=[("sa", t_silu_u[uu])], inc=("sv", 1))
                uu += 1
            t_guT[i] = self.cnt["st"]
            t_h2full = t_spu[uu - 1]
            if i + 1 < NTIL:
                front(i + 1)
            t_tr = None
            for k in range(22):
                ncols = 128 if k < 21 else 64
                t_tr = transpose(k, h2[:, k * 128:k * 128 + ncols],
                                 h2t[0:ncols, k * 128:(k + 1) * 128],
                                 xtra_tr=[("sv", t_h2full)] if k == 0 else ())
            t_h2tt[i] = self.cnt["st"]
            mm = None
            for k in range(22):
                kp = 128 if k < 21 else 64
                for q, pb in ((0, P1), (1, PS)):
                    mm = self.op("tensor", lambda e, k=k, q=q, pb=pb, kp=kp: e.matmul(
                        pb[:, 0:512], h2t[0:kp, k * 128:(k + 1) * 128],
                        wd1[0:kp, k * DM + q * 512:k * DM + (q + 1) * 512],
                        start=(k == 0), stop=(k == 21)),
                        wait=([("sv", t_tr), ("sv", t_v1b.get(i - 1))] +
                              ([t_wd, t_wdb] if i == 0 else [])
                              if k == 0 and q == 0 else []),
                        inc=("st", 1) if k == 21 and q == 1 else None)
            t_dn[i] = mm
            v1a = self.op("vector", lambda e, p=p: e.tensor_tensor(
                h1t[p][:, 0:512], h1t[p][:, 0:512], P1[:, 0:512], ALU.add),
                wait=[("st", mm), (dhp, t_ldh[i])], inc=("sv", 1))
            t_v1b[i] = self.op("vector", lambda e, p=p: e.tensor_tensor(
                h1t[p][:, 512:1024], h1t[p][:, 512:1024], PS[:, 0:512], ALU.add),
                wait=[("sv", v1a)], inc=("sv", 1))
            t_sp[i] = self.op("gpsimd", lambda e, i=i, p=p: e.dma_start(
                self.h1_dram[ts(i), :], h1t[p][:, :]),
                wait=[("sv", t_v1b[i])], inc=("dn%d" % p, 16))
        self.barrier()

    def phase_F(self):
        cv = self.carve
        c = self.c
        P0, P1, PT, PS = self.P0, self.P1, self.PT, self.PS
        SCL = 1.0 / 127.0
        wg2 = cv("wg2", OFF["R_C"], [128, NDC * INTER2], BF16)
        wu2 = cv("wu2", OFF["R_D"], [128, NDC * INTER2], BF16)
        wd2 = cv("wd2", OFF["R_D"] + 49152, [128, 21 * DM], BF16)
        wd2b = cv("wd2b", OFF["dt2"], [128, 3 * DM], BF16)
        xft = [cv(f"F_xf{p}", OFF["c_ssmw"] + p * 2048, [128, NDC * 128], BF16)
               for p in range(2)]
        h1t = [cv(f"F_h1{p}", OFF["R_B"] + p * 4096, [128, DM], F32)
               for p in range(2)]
        xt = [cv(f"F_x{p}", OFF["R_B"] + 8192 + p * 4096, [128, DM], F32)
              for p in range(2)]
        h3 = cv("F_h3", OFF["R_B"] + 16384, [128, INTER2], BF16)
        h3t = cv("F_h3t", OFF["R_B"] + 22528, [128, 24 * 128], BF16)
        wsem = ["dw", "dl0", "dl1", "dp", "dw", "dl0", "dl1", "dp"]
        t_wg, t_wu = {}, {}
        si = 0
        first = [("st", self.cnt["st"])]
        lastw = {}
        for pi in range(3):
            cb = pi * 1024
            for wdst, wsrc, td in ((wg2, self.w_g2, t_wg), (wu2, self.w_u2, t_wu)):
                s = wsem[si]
                w = first + ([(s, lastw[s])] if s in lastw else [])
                tok = self.op("gpsimd",
                    lambda e, wdst=wdst, wsrc=wsrc, cb=cb: e.dma_start(
                        bass.AP(wdst, cb, [[NDC * INTER2, 128], [INTER2, NDC], [1, 1024]]),
                        bass.AP(wsrc.tensor, cb,
                                [[INTER2, 128], [128 * INTER2, NDC], [1, 1024]])),
                    inc=(s, 16), wait=w)
                td[pi] = (s, tok)
                lastw[s] = tok
                first = []
                si = (si + 1) % len(wsem)
        t_wd = ("dl1", self.op("gpsimd", lambda e: e.dma_start(
            bass.AP(wd2, 0, [[21 * DM, 128], [DM, 21], [1, DM]]),
            bass.AP(self.w_d2.tensor, 0, [[DM, 128], [128 * DM, 21], [1, DM]])),
            inc=("dl1", 16), wait=[("dl1", lastw["dl1"])]))
        t_wdb = ("dp", self.op("gpsimd", lambda e: e.dma_start(
            bass.AP(wd2b, 0, [[3 * DM, 128], [DM, 3], [1, DM]]),
            bass.AP(self.w_d2.tensor, 21 * 128 * DM, [[DM, 128], [128 * DM, 3], [1, DM]])),
            inc=("dp", 16), wait=[("dp", lastw["dp"])]))
        slotA_b = P1[:, 640:704].bitcast(BF16)
        slotB_b = PT[:, 256:384]
        t_slot = {0: None, 1: None}

        def transpose(idx, src_ap, dst_ap, xtra_tr=(), xtra_ev=()):
            q = idx % 2
            pp = slotA_b if q == 0 else slotB_b
            np_ = src_ap.shape[0]
            nf = src_ap.shape[1]
            tt = self.op("tensor", lambda e, pp=pp, s=src_ap, np_=np_:
                         e.transpose(pp[:, 0:np_], s, c["ident"][0:np_, 0:np_]),
                         wait=[("sv", t_slot[q])] + list(xtra_tr), inc=("st", 1))
            t_slot[q] = self.op("vector",
                                lambda e, pp=pp, d=dst_ap, nf=nf, np_=np_:
                                e.tensor_copy(d, pp[0:nf, 0:np_]),
                                wait=[("st", tt)] + list(xtra_ev), inc=("sv", 1))
            return t_slot[q]

        parts = [(0, 1024), (1024, 1024), (2048, 1024)]
        slots = [P0[:, 0:1024], P0[:, 1024:2048]]
        t_sil = {}
        t_silu_u = {}
        t_spu = {}
        t_ldx = {}
        t_guT = {}
        t_trlast = {}
        t_dnlast = {}
        t_v1b = {}
        t_out = {}
        t_ost = {}
        uu = 0
        t_ldx[0] = self.op("sync", lambda e: e.dma_start(
            xft[0][:, :], bass.AP(self.xf_dram.tensor, 0,
                                  [[NDC * T, 128], [T, NDC], [1, 128]])),
            wait=[("st", self.cnt["st"])], inc=("de0", 16))
        for i in range(NTIL):
            p = i % 2
            dhp = "dh%d" % p
            if i + 1 < NTIL:
                t_ldx[i + 1] = self.op("sync", lambda e, i=i: e.dma_start(
                    xft[(i + 1) % 2][:, :],
                    bass.AP(self.xf_dram.tensor, (i + 1) * 128,
                            [[NDC * T, 128], [T, NDC], [1, 128]])),
                    wait=[("st", t_guT.get(i - 1))], inc=("de%d" % ((i + 1) % 2), 16))
            ldh = self.op("sync", lambda e, i=i, p=p: e.dma_start(
                h1t[p][:, :], self.h1_dram[ts(i), :]),
                wait=[("so" if p == 0 else "ad", t_ost.get(i - 2))], inc=(dhp, 16))
            ldxo = self.op("sync", lambda e, i=i, p=p: e.dma_start(
                xt[p][:, :], self.x_own[ts(i), :]),
                wait=[("sv", t_out.get(i - 2))], inc=(dhp, 16))
            for pi, (cb, cn) in enumerate(parts):
                for j, w in ((0, wg2), (1, wu2)):
                    mm = None
                    for dc in range(NDC):
                        mm = self.op("tensor", lambda e, i=i, dc=dc, cb=cb, j=j,
                                     w=w, p=p: [
                            e.matmul(slots[j][:, 0:512],
                                     xft[p][:, dc * 128:(dc + 1) * 128],
                                     w[:, dc * INTER2 + cb:dc * INTER2 + cb + 512],
                                     start=(dc == 0), stop=(dc == NDC - 1)),
                            e.matmul(slots[j][:, 512:1024],
                                     xft[p][:, dc * 128:(dc + 1) * 128],
                                     w[:, dc * INTER2 + cb + 512:dc * INTER2 + cb + 1024],
                                     start=(dc == 0), stop=(dc == NDC - 1))][-1],
                            wait=(([("sa", t_silu_u.get(uu - 1)),
                                    ("de%d" % p, t_ldx[i])] if j == 0 else
                                   [("sv", t_spu.get(uu - 1))]) +
                                  ([(t_wg if j == 0 else t_wu)[pi]] if i == 0 else [])
                                  if dc == 0 else []),
                            inc=("st", 1) if dc == NDC - 1 else None)
                mmtok = self.cnt["st"]
                t_silu_u[uu] = t_sil[(i, pi)] = self.op(
                    "scalar", lambda e, cb=cb, cn=cn: e.activation(
                        h3[:, cb:cb + cn], slots[0][:, 0:cn], AF.Silu, scale=SCL),
                    wait=[("st", mmtok)] +
                         ([("st", t_trlast.get(i - 1))] if pi == 0 else []),
                    inc=("sa", 1))
                t_spu[uu] = self.op(
                    "vector", lambda e, cb=cb, cn=cn: e.tensor_tensor(
                        h3[:, cb:cb + cn], h3[:, cb:cb + cn], slots[1][:, 0:cn],
                        ALU.mult),
                    wait=[("sa", t_sil[(i, pi)])], inc=("sv", 1))
                uu += 1
            t_guT[i] = self.cnt["st"]
            t_h3full = t_spu[uu - 1]
            t_tr = None
            for k in range(24):
                t_tr = transpose(k, h3[:, k * 128:(k + 1) * 128],
                                 h3t[:, k * 128:(k + 1) * 128],
                                 xtra_tr=[("sv", t_h3full)] if k == 0 else ())
            t_trlast[i] = self.cnt["st"]
            mm = None
            for k in range(24):
                for q, pb in ((0, P1), (1, PS)):
                    wsrc = (lambda kk: (wd2, kk) if kk < 21 else (wd2b, kk - 21))(k)
                    mm = self.op("tensor", lambda e, k=k, q=q, pb=pb,
                                 wt=wsrc[0], kk=wsrc[1]: e.matmul(
                        pb[:, 0:512], h3t[:, k * 128:(k + 1) * 128],
                        wt[:, kk * DM + q * 512:kk * DM + (q + 1) * 512],
                        start=(k == 0), stop=(k == 23)),
                        wait=([("sv", t_tr), ("sv", t_v1b.get(i - 1))] +
                              ([t_wd, t_wdb] if i == 0 else [])
                              if k == 0 and q == 0 else []),
                        inc=("st", 1) if k == 23 and q == 1 else None)
            t_dnlast[i] = mm
            v1a = self.op("vector", lambda e, p=p: e.scalar_tensor_tensor(
                h1t[p][:, 0:512], P1[:, 0:512], SCL * SCL, h1t[p][:, 0:512],
                ALU.mult, ALU.add),
                wait=[("st", mm), (dhp, ldxo)], inc=("sv", 1))
            t_v1b[i] = self.op("vector", lambda e, p=p: e.scalar_tensor_tensor(
                h1t[p][:, 512:1024], PS[:, 0:512], SCL * SCL, h1t[p][:, 512:1024],
                ALU.mult, ALU.add),
                wait=[("sv", v1a)], inc=("sv", 1))
            t_out[i] = self.op("vector", lambda e, p=p: e.tensor_tensor(
                h1t[p][:, :], h1t[p][:, :], xt[p][:, :], ALU.add),
                wait=[("sv", t_v1b[i]), (dhp, ldxo)], inc=("sv", 1))
            t_ost[i] = self.op("sync", lambda e, i=i, p=p: e.dma_start(
                self.out[ts(i), :], h1t[p][:, :]),
                wait=[("sv", t_out[i])], inc=("so" if p == 0 else "ad", 16))
        self.wait("sync", "so", self.cnt["so"])
        self.wait("sync", "ad", self.cnt["ad"])
        self.wait("gpsimd", "so", self.cnt["so"])
        self.wait("gpsimd", "ad", self.cnt["ad"])

    def dev_dump_B(self, tag):
        d1 = self.dout(f"dev_xbc_{tag}", [128, NCC * CW], BF16)
        d2 = self.dout(f"dev_dtf_{tag}", [32, T])
        d3 = self.dout(f"dev_dtA_{tag}", [32, T])
        self.barrier()
        self.op("scalar", lambda e: e.dma_start(d1, self.xbc[:, :]), inc=("so", 16))
        self.op("scalar", lambda e: e.dma_start(d2, self.dtf[:, :]), inc=("so", 16))
        self.op("scalar", lambda e: e.dma_start(d3, self.dtAf[:, :]), inc=("so", 16))
        self.wait("scalar", "so", self.cnt["so"])
        self.barrier()


def prep_inputs(full, core):
    b, h = core // 2, core % 2
    x = np.asarray(full["L0_attn_hidden_states"])
    own = np.ascontiguousarray(x[b, h * T:(h + 1) * T, :], dtype=np.float32)
    pre = (np.ascontiguousarray(x[b, 0:T, :], dtype=np.float32) if h == 1
           else np.zeros((T, DM), np.float32))
    m = {
        "x_own": own, "x_pre": pre,
        "w_in": np.ascontiguousarray(np.asarray(full["L0_attn_in_proj_w"]).T),
        "w_out": np.ascontiguousarray(np.asarray(full["L0_attn_out_proj_w"]).T),
        "w_g1": np.ascontiguousarray(np.asarray(full["L0_attn_gate_w"]).T),
        "w_u1": np.ascontiguousarray(np.asarray(full["L0_attn_up_w"]).T),
        "w_d1": np.ascontiguousarray(np.asarray(full["L0_attn_down_w"]).T),
        "w_g2": np.ascontiguousarray(np.asarray(full["L0_ffn_w_gate"]).T),
        "w_u2": np.ascontiguousarray(np.asarray(full["L0_ffn_w_up"]).T),
        "w_d2": np.ascontiguousarray(np.asarray(full["L0_ffn_w_down"]).T),
        "conv_w": np.ascontiguousarray(
            np.asarray(full["L0_attn_conv_w"]).reshape(NCC, 128, 4)
            .transpose(1, 0, 2).reshape(128, NCC * 4)),
        "conv_b": np.ascontiguousarray(
            np.asarray(full["L0_attn_conv_b"]).reshape(NCC, 128).T),
        "ssm_w_bc": np.broadcast_to(np.asarray(full["L0_attn_ssm_norm_w"])[None, :],
                                    (128, DIN)).copy(),
        "post_w_bc": np.broadcast_to(np.asarray(full["L0_attn_post_norm_w"])[None, :],
                                     (128, DM)).copy(),
        "d_exp_bc": np.broadcast_to(np.repeat(np.asarray(full["L0_attn_D"]), HPD)[None, :],
                                    (128, DIN)).copy(),
        "dtb_bc": np.ascontiguousarray(np.asarray(full["L0_attn_dt_bias"], np.float32)[:, None]),
        "alog_bc": np.ascontiguousarray(np.asarray(full["L0_attn_A_log"], np.float32)[:, None]),
        "triu_f": np.triu(np.ones((128, 128), np.float32)),
        "triu_b": np.triu(np.ones((128, 128), np.float32)).astype(ml_dtypes.bfloat16),
        "ident_b": np.eye(128, dtype=np.float32).astype(ml_dtypes.bfloat16),
        "ident_f": np.eye(128, dtype=np.float32),
        "fconst": np.broadcast_to(np.asarray(FCONST_VALS, np.float32)[None, :], (128, 8)).copy(),
        "smask": np.full((128, 1), 1.0 if h == 1 else 0.0, np.float32),
        "nwcol": np.ascontiguousarray(
            np.asarray(full["L0_attn_norm_w"]).reshape(NDC, 128).T.astype(np.float32)),
    }
    return m


def build(dev_phase=None):
    b = B(dev_phase)
    b.declare_io()
    b.load_consts()
    b.phase_A()
    if dev_phase == "A":
        pass
    elif dev_phase in ("Bmm", "Bconv"):
        b.phase_B(1, b_stop=dev_phase[1:])
    else:
        b.phase_B(1)
        if dev_phase == "Bp":
            b.dev_dump_B("p")
        elif dev_phase == "B":
            b.phase_B(0)
            b.dev_dump_B("m")
        else:
            b.phase_C(1)
            b.phase_B(0)
            b.phase_C(0)
            if dev_phase in (None, "full"):
                b.phase_E()
                b.phase_F()
            if dev_phase == "C":
                b.barrier()
                b.op("scalar", lambda e: e.dma_start(b.out, b.dram["h1_dram"]),
                     inc=("so", 16))
                b.wait("scalar", "so", b.cnt["so"])
    if dev_phase is not None and dev_phase != "full":
        b.barrier()
        b.op("scalar", lambda e: e.dma_start(b.out, b.dram["xn_dram"]), inc=("so", 16))
        b.wait("scalar", "so", b.cnt["so"])
    return b.finalize()


def kernel(**inputs):
    nc = build(None)
    in_maps = [prep_inputs(inputs, c) for c in range(8)]
    res = run_bass_kernel_spmd(nc, in_maps, core_ids=list(range(8)))
    y = np.empty((4, 4096, DM), np.float32)
    for c in range(8):
        b_, h_ = c // 2, c % 2
        y[b_, h_ * T:(h_ + 1) * T, :] = np.asarray(res.results[c]["out"])
    return y
